# version 24
# baseline (speedup 1.0000x reference)
import numpy as np
import ml_dtypes
import concourse.bass as bass
import concourse.mybir as mybir
from concourse.bass_utils import run_bass_kernel_spmd

F32 = mybir.dt.float32
BF16 = mybir.dt.bfloat16
I32 = mybir.dt.int32
AF = mybir.ActivationFunctionType
ALU = mybir.AluOpType
bf16 = ml_dtypes.bfloat16
AP = bass.AP

D = 4096
T = 4096
NCH = 32
NQ = 1024
H = 32
HD = 128
NSLOT = 8
NHG = 8
RSQ = 0.08838834764831845
KB = 1024
DEBUG = False


def build_program():
    nc = bass.Bass("TRN2", target_bir_lowering=False)

    def din(name, shape, dt):
        return nc.dram_tensor(name, shape, dt, kind="ExternalInput")

    d_xT = din("xT", [D, T], F32)
    d_xqT = din("xqT", [D, NQ], F32)
    d_masks = din("masks", [NSLOT * 4, 128, 512], BF16)
    d_wqT = din("wqT", [D, D], F32)
    d_wkT = din("wkT", [D, HD], F32)
    d_wvT = din("wvT", [D, HD], F32)
    d_woT = din("woT", [D, D], F32)
    d_WbT = din("WbT", [D, D], F32)
    d_wqb = din("wq_bP", [128, NCH], F32)
    d_wob = din("wo_bP", [128, NCH], F32)
    d_bbb = din("bb_P", [128, NCH], F32)
    d_wkb = din("wk_bP", [128, 1], F32)
    d_wvb = din("wv_bP", [128, 1], F32)
    d_AT = din("AT", [D, 128], F32)
    d_Bt = din("Bt", [128, D], F32)
    d_rT = din("rT", [D, 16], F32)
    d_rb = din("rb_P", [16, 1], F32)
    d_pr = din("prior_P", [16, 1], F32)
    d_act = din("act_i", [4], I32)
    d_iota = din("iota16", [16, 1], F32)
    d_iotab = din("iota16b", [16, 1], BF16)
    d_iotaR4 = din("iotaR4", [4, 16], F32)
    d_identB = din("identB", [128, 128], BF16)
    d_identF4 = din("identF4", [4, 4], F32)
    d_identF16 = din("identF16", [16, 16], F32)
    d_R2 = din("R2", [16, 128], BF16)
    d_out = nc.dram_tensor("outT", [D, NQ], F32, kind="ExternalOutput")
    if DEBUG:
        d_dbg = {
            "dbg_ft": nc.dram_tensor("dbg_ft", [1, 16], F32, kind="ExternalOutput"),
            "dbg_cb": nc.dram_tensor("dbg_cb", [128, 1], F32, kind="ExternalOutput"),
            "dbg_kT": nc.dram_tensor("dbg_kT", [128, T], BF16, kind="ExternalOutput"),
            "dbg_Vt": nc.dram_tensor("dbg_Vt", [128, T], BF16, kind="ExternalOutput"),
            "dbg_xn": nc.dram_tensor("dbg_xn", [128, NCH * NQ], BF16, kind="ExternalOutput"),
            "dbg_qT": nc.dram_tensor("dbg_qT", [128, NCH * NQ], BF16, kind="ExternalOutput"),
            "dbg_at": nc.dram_tensor("dbg_at", [128, NCH * NQ], BF16, kind="ExternalOutput"),
            "dbg_z": nc.dram_tensor("dbg_z", [128, NQ], BF16, kind="ExternalOutput"),
            "dbg_zp": nc.dram_tensor("dbg_zp", [128, NQ], BF16, kind="ExternalOutput"),
            "dbg_lg": nc.dram_tensor("dbg_lg", [16, NQ], BF16, kind="ExternalOutput"),
            "dbg_alT": nc.dram_tensor("dbg_alT", [4, NQ], BF16, kind="ExternalOutput"),
            "dbg_aE": nc.dram_tensor("dbg_aE", [16, NQ], BF16, kind="ExternalOutput"),
            "dbg_rows": nc.dram_tensor("dbg_rows", [1, 2048], F32, kind="ExternalOutput"),
            "dbg_brow": nc.dram_tensor("dbg_brow", [1, 1024], BF16, kind="ExternalOutput"),
            "dbg_rB": nc.dram_tensor("dbg_rB", [128, 512], BF16, kind="ExternalOutput"),
            "dbg_nB": nc.dram_tensor("dbg_nB", [128, 512], BF16, kind="ExternalOutput"),
            "dbg_wk": nc.dram_tensor("dbg_wk", [128, NCH * HD], BF16, kind="ExternalOutput"),
            "dbg_T1": nc.dram_tensor("dbg_T1", [128, 512], BF16, kind="ExternalOutput"),
            "dbg_prB": nc.dram_tensor("dbg_prB", [128, 512], BF16, kind="ExternalOutput"),
            "dbg_csk": nc.dram_tensor("dbg_csk", [1, 128], BF16, kind="ExternalOutput"),
            "dbg_pre": nc.dram_tensor("dbg_pre", [4, 512], F32, kind="ExternalOutput"),
            "dbg_lp4": nc.dram_tensor("dbg_lp4", [4, 1], F32, kind="ExternalOutput"),
            "dbg_Sf": nc.dram_tensor("dbg_Sf", [16, 4], F32, kind="ExternalOutput"),
            "dbg_e4": nc.dram_tensor("dbg_e4", [128, 4], F32, kind="ExternalOutput"),
            "dbg_lpv": nc.dram_tensor("dbg_lpv", [16, 1], F32, kind="ExternalOutput"),
            "dbg_lp16": nc.dram_tensor("dbg_lp16", [16, 1], BF16, kind="ExternalOutput"),
            "dbg_Se16": nc.dram_tensor("dbg_Se16", [16, 4], BF16, kind="ExternalOutput"),
            "dbg_a4": nc.dram_tensor("dbg_a4", [4, 1], F32, kind="ExternalOutput"),
            "dbg_Se16e": nc.dram_tensor("dbg_Se16e", [16, 4], BF16, kind="ExternalOutput"),
            "dbg_lp16e": nc.dram_tensor("dbg_lp16e", [16, 1], BF16, kind="ExternalOutput"),
            "dbg_can1": nc.dram_tensor("dbg_can1", [128, 512], F32, kind="ExternalOutput"),
            "dbg_can2": nc.dram_tensor("dbg_can2", [128, 512], F32, kind="ExternalOutput"),
        }

    SBB = 16512

    def sb(name, shape, dt, off):
        return nc.alloc_sbuf_tensor_at(name, shape, dt, offset=SBB + off)

    xnbq = sb("xnbq", [128, NCH * NQ], BF16, 0)
    qTat = sb("qTat", [128, NCH * NQ], BF16, 64 * KB)
    kT = sb("kT", [128, T], BF16, 128 * KB)
    Vt = sb("Vt", [128, T], BF16, 136 * KB)
    wkTb = sb("wkTb", [128, NCH * HD], BF16, 144 * KB)
    wvTb = sb("wvTb", [128, NCH * HD], BF16, 152 * KB)
    ATb = sb("ATb", [128, NCH * 128], BF16, 160 * KB)
    XB = [sb("XB0", [128, NCH * 512], BF16, 0),
          sb("XB1", [128, NCH * 512], BF16, 32 * KB)]
    X2R = sb("X2R", [128, 8 * 512], BF16, 64 * KB)
    T1R = sb("T1R", [128, 8 * 512], BF16, 72 * KB)
    p1rstdB = [sb("p1rstdB0", [128, 512], BF16, 190 * KB),
               sb("p1rstdB1", [128, 512], BF16, 193 * KB)]
    p1k2b = sb("p1k2b", [128, 512], BF16, 191 * KB)
    p1VTs = sb("p1VTs", [128, 512], BF16, 192 * KB)
    p1rows = [sb("p1rows0", [1, 2048], F32, 198 * KB),
              (sb("p1r1a", [1, 512], F32, 181 * KB),
               sb("p1r1b", [1, 512], F32, 183 * KB),
               sb("p1r1c", [1, 512], F32, 185 * KB),
               sb("p1r1d", [1, 512], F32, 179 * KB))]
    p1brow = [sb("p1brow0", [1, 1024], BF16, 177 * KB),
              sb("p1brow1", [1, 1024], BF16, 187 * KB)]
    XQB = sb("XQB", [128, NCH * 512], BF16, 64 * KB)
    X2F2 = sb("X2F2", [128, NCH * 512], BF16, 96 * KB)
    p2rstdB = sb("p2rstdB", [128, 512], BF16, 190 * KB)
    p2nmrB = sb("p2nmrB", [128, 512], BF16, 191 * KB)
    p2rows = sb("p2rows", [1, 2048], F32, 198 * KB)
    p2brow = sb("p2brow", [1, 1024], BF16, 179 * KB)
    WQP = [sb("WQP0", [128, NCH * 256], BF16, 144 * KB),
           sb("WQP1", [128, NCH * 256], BF16, 160 * KB)]
    WOP = [sb("WOP0", [128, NCH * 256], BF16, 128 * KB),
           sb("WOP1", [128, NCH * 256], BF16, 144 * KB)]
    WBP = sb("WBP", [128, NCH * 256 + 256], BF16, 160 * KB)
    rTb = sb("rTb", [128, NCH * 16], BF16, 176 * KB)
    z16 = sb("z16", [128, NQ], BF16, 177 * KB)
    zp16 = sb("zp16", [128, NQ], BF16, 179 * KB)
    lg16 = sb("lg16", [16, NQ], BF16, 181 * KB)
    alphaT16 = sb("alphaT16", [4, NQ], BF16, 183 * KB)
    aE16 = sb("aE16", [16, NQ], BF16, 185 * KB)
    aB16 = sb("aB16", [128, 512], BF16, 187 * KB)
    pre_f = sb("pre_f", [4, 512], F32, 188 * KB)
    MB = [sb("MB0", [128, 4 * 512], BF16, 181 * KB),
          sb("MB1", [128, 4 * 512], BF16, 185 * KB)]
    XRS = [sb("XRS0", [128, 512], F32, 181 * KB),
           sb("XRS1", [128, 512], F32, 183 * KB),
           sb("XRS2", [128, 512], F32, 198 * KB),
           sb("XRS3", [128, 512], F32, 200 * KB)]
    OUTS = [sb("OUTS0", [128, 512], F32, 185 * KB),
            sb("OUTS1", [128, 512], F32, 187 * KB)]
    Pbp = [sb("Pbp0", [128, 1024], BF16, 190 * KB),
           sb("Pbp1", [128, 1024], BF16, 192 * KB),
           sb("Pbp2", [128, 1024], BF16, 146 * KB),
           sb("Pbp3", [128, 1024], BF16, 148 * KB)]
    q2b = sb("q2b", [128, NQ], BF16, 190 * KB)
    SGT = [sb("SGT0", [128, 512], BF16, 190 * KB), sb("SGT1", [128, 512], BF16, 191 * KB),
           sb("SGT2", [128, 512], BF16, 192 * KB), sb("SGT3", [128, 512], BF16, 193 * KB)]
    SGT2 = [sb("SGT2a", [128, 512], BF16, 189 * KB),
            sb("SGT2b", [128, 512], BF16, 177 * KB)]
    b0 = 194 * KB
    cbias = sb("cbias", [128, 1], F32, b0 + 0)
    escale = sb("escale", [128, 1], F32, b0 + 32)
    wkb = sb("wkb", [128, 1], F32, b0 + 64)
    wvb = sb("wvb", [128, 1], F32, b0 + 96)
    ones_col = sb("ones_col", [128, 1], BF16, b0 + 128)
    mxc = sb("mxc", [128, 1], F32, b0 + 160)
    nmxc = sb("nmxc", [128, 1], F32, b0 + 192)
    s4c = sb("s4c", [128, 1], F32, b0 + 224)
    rsc = sb("rsc", [128, 1], F32, b0 + 256)
    e4t = sb("e4t", [128, 4], F32, b0 + 288)
    alpha16 = sb("alpha16", [128, 4], BF16, b0 + 320)
    wq_bP = sb("wq_bPs", [128, NCH], F32, b0 + 384)
    wo_bP = sb("wo_bPs", [128, NCH], F32, b0 + 512)
    bb_P = sb("bb_Ps", [128, NCH], F32, b0 + 640)
    identB = sb("identBs", [128, 128], BF16, b0 + 768)
    b2 = 196 * KB
    ones_row = sb("ones_row", [1, 128], BF16, b2 + 0)
    csk16 = sb("csk16", [1, 128], BF16, b2 + 256)
    csv16 = sb("csv16", [1, 128], BF16, b2 + 512)
    identF4 = sb("identF4s", [4, 4], F32, b2 + 768)
    identF16 = sb("identF16s", [16, 16], F32, 206 * KB)
    iotaf = sb("iotaf", [16, 1], F32, b2 + 832)
    acts16 = sb("acts16", [16, 4], I32, b2 + 864)
    S_f = sb("S_f", [16, 4], F32, b2 + 896)
    S_e16 = sb("S_e16", [16, 4], BF16, b2 + 928)
    S_fT = sb("S_fT", [4, 16], F32, b2 + 960)
    S_eT16 = sb("S_eT16", [4, 16], BF16, b2 + 1024)
    iotaR4 = sb("iotaR4s", [4, 16], F32, b2 + 1056)
    R2b = sb("R2b", [16, 128], BF16, b2 + 1152)
    lpv = sb("lpv", [16, 1], F32, b2 + 1408)
    lp16 = sb("lp16", [16, 1], BF16, b2 + 1440)
    lp4 = sb("lp4", [4, 1], F32, b2 + 1472)
    actf4 = sb("actf4", [4, 1], F32, b2 + 1504)
    rbv = sb("rbv", [16, 1], F32, b2 + 1536)
    prv = sb("prv", [16, 1], F32, b2 + 1568)
    ftiny = sb("ftiny", [1, 16], F32, b2 + 1600)
    krow = sb("krow", [1, 8], F32, b2 + 1920)
    qrow = sb("qrow", [1, 64], F32, 207 * KB)
    btiny = sb("btiny", [1, 16], BF16, b2 + 1664)
    iota16b = sb("iota16b", [16, 1], BF16, b2 + 1696)
    eps12c = sb("eps12c", [16, 1], F32, b2 + 1792)
    neg1c = sb("neg1c", [128, 1], F32, b2 + 1856)
    canary = sb("canary", [128, 512], F32, 196 * KB)
    rz16 = sb("rz16", [1, 512], BF16, 198 * KB)
    rlo16 = sb("rlo16", [1, 512], BF16, 199 * KB)
    rzf = sb("rzf", [1, 512], F32, 200 * KB)
    rzh = sb("rzh", [1, 512], F32, 202 * KB)
    rz16b = sb("rz16b", [1, 512], BF16, 158 * KB)
    rzbS2 = sb("rzbS2", [128, 512], F32, 154 * KB)
    lnzr = sb("lnzr", [1, 512], F32, 204 * KB)
    rzbS = sb("rzbS", [128, 512], F32, 144 * KB)
    EPS5, EPS12, NEGH, POSH, NEG1, KM2, QM2, TSC, CSC = range(9)

    pA = [nc.alloc_psum_tensor(f"pA{i}", [128, 1024], F32) for i in range(4)]

    def pb_view(i):
        return pA[i // 2].ap()[:, (i % 2) * 512:(i % 2) * 512 + 512]
    pb = [pb_view(i) for i in range(8)]

    def pbf(i):
        return pA[i // 2].ap().bitcast(BF16)[:, (i % 2) * 1024:(i % 2) * 1024 + 1024]

    s_p0 = nc.alloc_semaphore("s_p0")
    s_ld = nc.alloc_semaphore("s_ld")
    s_xb = [nc.alloc_semaphore("s_xb0"), nc.alloc_semaphore("s_xb1")]
    s_xq = [nc.alloc_semaphore("s_xq0"), nc.alloc_semaphore("s_xq1")]
    s_wq = [nc.alloc_semaphore("s_wq0"), nc.alloc_semaphore("s_wq1")]
    s_wo = [nc.alloc_semaphore("s_wo0"), nc.alloc_semaphore("s_wo1")]
    s_wb = nc.alloc_semaphore("s_wb")
    s_mk = [nc.alloc_semaphore("s_mk0"), nc.alloc_semaphore("s_mk1")]
    s_xr = [nc.alloc_semaphore(f"s_xr{i}") for i in range(4)]
    s_st = nc.alloc_semaphore("s_st")
    s_so = [nc.alloc_semaphore("s_so0"), nc.alloc_semaphore("s_so1")]
    m_pe = nc.alloc_semaphore("m_pe")
    m_gp = nc.alloc_semaphore("m_gp")
    m_pn = nc.alloc_semaphore("m_pn")
    m_sn = nc.alloc_semaphore("m_sn")
    m_vn = nc.alloc_semaphore("m_vn")
    a_sc = nc.alloc_semaphore("a_sc")
    v_ve = nc.alloc_semaphore("v_ve")

    Q = {"sync": [], "gpsimd": [], "tensor": [], "vector": [], "scalar": []}
    C = {"pe": 0, "sc": 0, "ve": 0, "gp": 0, "pn": 0, "sn": 0, "vn": 0}
    DC = {id(s): 0 for s in [s_p0, s_ld, s_xb[0], s_xb[1], s_xq[0], s_xq[1],
                             s_wq[0], s_wq[1], s_wo[0], s_wo[1], s_wb,
                             s_mk[0], s_mk[1], s_xr[0], s_xr[1], s_xr[2], s_xr[3],
                             s_st, s_so[0], s_so[1]]}
    SEMOBJ = {"pe": m_pe, "sc": a_sc, "ve": v_ve, "gp": m_gp, "pn": m_pn, "sn": m_sn, "vn": m_vn}

    ENG2C = {"vector": "ve", "scalar": "sc", "gpsimd": "gp"}

    def op(eng, build, waits=(), inc=None):
        w = [(s, v) for (s, v) in waits if v > 0]
        auto = ENG2C.get(eng)
        if auto is not None:
            if C[auto] > 0:
                w.append((SEMOBJ[auto], C[auto]))
            C[auto] += 1
            tgt = SEMOBJ[auto]
        elif inc:
            C[inc] += 1
            tgt = SEMOBJ[inc]
        else:
            tgt = None
        w = tuple(w)

        def fn(engine, w=w, build=build, tgt=tgt):
            for (s, v) in w:
                engine.wait_ge(s, v)
            ins = build(engine)
            if tgt is not None and ins is not None:
                ins.then_inc(tgt, 1)
        Q[eng].append(fn)
        return C[auto] if auto is not None else (C[inc] if inc else None)

    def dma(eng, out_ap, in_ap, sem, waits=()):
        DC[id(sem)] += 16
        w = tuple((s, v) for (s, v) in waits if v > 0)

        def fn(engine, w=w, out_ap=out_ap, in_ap=in_ap, sem=sem):
            for (s, v) in w:
                engine.wait_ge(s, v)
            engine.dma_start(out=out_ap, in_=in_ap).then_inc(sem, 16)
        Q[eng].append(fn)
        return DC[id(sem)]

    for (t_, s_) in ((wq_bP, d_wqb), (wo_bP, d_wob), (bb_P, d_bbb), (wkb, d_wkb),
                     (wvb, d_wvb), (identB, d_identB), (identF4, d_identF4), (identF16, d_identF16),
                     (iotaf, d_iota), (iota16b, d_iotab), (iotaR4, d_iotaR4), (R2b, d_R2),
                     (rbv, d_rb), (prv, d_pr)):
        dma("sync", t_[:, :], s_.ap()[:, :], s_ld)
    dma("sync", acts16[:, :], AP(d_act, 0, [[0, 16], [1, 4]]), s_ld)
    LD0 = DC[id(s_ld)]
    for (dst, src, width) in ((wkTb, d_wkT, HD), (wvTb, d_wvT, HD),
                              (ATb, d_AT, 128), (rTb, d_rT, 16)):
        for h4 in range(4):
            dma("gpsimd", dst[:, h4 * 8 * width:(h4 + 1) * 8 * width],
                AP(src, h4 * 8 * 128 * width,
                   [[width, 128], [128 * width, 8], [1, width]]), s_p0)
    P0C = DC[id(s_p0)]

    def vset(ap_, val):
        op("vector", lambda e, a=ap_, v=val: e.memset(a, v))
    vset(ones_col[:, :], 1.0)
    vset(ones_row[:, :], 1.0)
    vset(escale[:, :], RSQ)
    vset(ftiny[:, EPS5:EPS5 + 1], 1e-5)
    vset(ftiny[:, EPS12:EPS12 + 1], 1e-12)
    vset(eps12c[:, :], 1e-12)
    vset(neg1c[:, :], -1.0)
    vset(ftiny[:, NEGH:NEGH + 1], -0.5)
    vset(ftiny[:, POSH:POSH + 1], 0.5)
    vset(ftiny[:, NEG1:NEG1 + 1], -1.0)
    vset(ftiny[:, KM2:KM2 + 1], 0.0)
    VE_CONST = op("vector", lambda e: e.memset(ftiny[:, QM2:QM2 + 1], 0.0), inc="ve")
    op("vector", lambda e: e.tensor_scalar(S_f[:, :], acts16[:, :], iotaf[:, :],
                                           None, ALU.is_equal), waits=[(s_ld, LD0)])
    VE_SE = op("vector", lambda e: e.tensor_scalar(S_e16[:, :], acts16[:, :],
                                                   iotaf[:, :], None, ALU.is_equal),
               inc="ve")
    if DEBUG:
        dma("sync", d_dbg["dbg_Se16e"].ap()[:, :], S_e16[:, :], s_st,
            waits=[(v_ve, VE_SE)])
        dma("sync", d_dbg["dbg_can1"].ap()[:, :], canary[:, :], s_st)
    for dc in range(NCH):
        op("tensor", lambda e, dc=dc: e.matmul(
            pb[6][0:1, 0:128], ones_col[:, :], wkTb[:, dc * HD:(dc + 1) * HD],
            start=(dc == 0), stop=(dc == NCH - 1)),
           waits=[(s_p0, P0C), (v_ve, VE_CONST)] if dc == 0 else ())
    for dc in range(NCH):
        PE_CS = op("tensor", lambda e, dc=dc: e.matmul(
            pb[7][0:1, 0:128], ones_col[:, :], wvTb[:, dc * HD:(dc + 1) * HD],
            start=(dc == 0), stop=(dc == NCH - 1)),
            inc="pe" if dc == NCH - 1 else None)
    op("vector", lambda e: e.tensor_copy(csk16[:, :], pb[6][0:1, 0:128]),
       waits=[(m_pe, PE_CS)])
    VE_CS = op("vector", lambda e: e.tensor_copy(csv16[:, :], pb[7][0:1, 0:128]),
               inc="ve")

    TRK = {"stats_pe": 0, "sq_ve": {}, "kv_pe": 0, "t1_ve": {}, "tr_pe": {},
           "tr_ve": {-1: VE_CS, 3: VE_CS}, "rb_ve": 0, "nb_ve": 0,
           "kv_ve": 0, "k2_ve": VE_CS, "x2_pe": 0, "rstd_sc": {},
           "x2mark": {}, "kvmark": {}, "sq_sc": {}, "sqsc": 0}
    TRK["stats_ve"] = {}

    def rowslots(rows):
        if isinstance(rows, tuple):
            return tuple(r[:, :] for r in rows)
        return (rows[:, 0:512], rows[:, 512:1024], rows[:, 1024:1536],
                rows[:, 1536:2048])

    def ln_statsA_p1(t, XBuf, sxb_sem, sxb_cnt, first=False, banks=(2, 3)):
        bx, bx2 = banks
        for dc in range(NCH):
            w = []
            if dc == 0:
                w = [(sxb_sem, sxb_cnt), (v_ve, TRK["stats_ve"].get(banks, 0))]
                if first:
                    w.append((v_ve, VE_CONST))
            op("tensor", lambda e, dc=dc, bx=bx: e.matmul(
                pb[bx][0:1, :], ones_col[:, :], XBuf[:, dc * 512:(dc + 1) * 512],
                start=(dc == 0), stop=(dc == NCH - 1)), waits=w)
        def emit_sq(dc):
            gdc = t * NCH + dc
            sl = (gdc % 8) * 512
            wv = []
            if dc == 0:
                wv = [(sxb_sem, sxb_cnt)]
            if gdc >= 8:
                wv.append((m_pe, TRK["x2mark"].get((gdc - 8 + 3) // 4, 0)))
            op("scalar", lambda e, dc=dc, sl=sl: e.activation(
                X2R[:, sl:sl + 512], XBuf[:, dc * 512:(dc + 1) * 512], AF.Square),
               waits=wv)
            TRK["sq_sc"][dc] = C["sc"]
        emit_sq(0)
        for dc in range(NCH):
            gdc = t * NCH + dc
            sl = (gdc % 8) * 512
            if dc + 1 < NCH:
                emit_sq(dc + 1)
            wv2 = [(a_sc, TRK["sq_sc"][min(dc + 1, NCH - 1)])]
            ii = op("tensor", lambda e, dc=dc, sl=sl, bx2=bx2: e.matmul(
                pb[bx2][0:1, :], ones_col[:, :], X2R[:, sl:sl + 512],
                start=(dc == 0), stop=(dc == NCH - 1)),
                waits=wv2, inc="pe" if dc % 4 == 3 else None)
            if dc % 4 == 3:
                TRK["x2mark"][gdc // 4] = ii
                TRK["x2_pe"] = ii
        return TRK["x2_pe"]

    def ln_statsA(XBuf, X2Buf, sxb_sem, sxb_cnt, first=False, banks=(2, 3)):
        bx, bx2 = banks
        for dc in range(NCH):
            wv = []
            if dc == 0:
                wv = [(sxb_sem, sxb_cnt), (m_pe, TRK["x2_pe"])]
            TRK["sq_ve"][dc] = op("vector", lambda e, dc=dc: e.tensor_tensor(
                X2Buf[:, dc * 512:(dc + 1) * 512], XBuf[:, dc * 512:(dc + 1) * 512],
                XBuf[:, dc * 512:(dc + 1) * 512], ALU.mult), waits=wv)
        for dc in range(NCH):
            w = []
            if dc == 0:
                w = [(sxb_sem, sxb_cnt), (v_ve, TRK["stats_ve"].get(banks, 0))]
                if first:
                    w.append((v_ve, VE_CONST))
            op("tensor", lambda e, dc=dc, bx=bx: e.matmul(
                pb[bx][0:1, :], ones_col[:, :], XBuf[:, dc * 512:(dc + 1) * 512],
                start=(dc == 0), stop=(dc == NCH - 1)), waits=w)
        for dc in range(NCH):
            wv = []
            if dc % 4 == 0:
                wv = [(v_ve, TRK["sq_ve"][min(dc + 3, NCH - 1)])]
            TRK["x2_pe"] = op("tensor", lambda e, dc=dc, bx2=bx2: e.matmul(
                pb[bx2][0:1, :], ones_col[:, :], X2Buf[:, dc * 512:(dc + 1) * 512],
                start=(dc == 0), stop=(dc == NCH - 1)),
                waits=wv, inc="pe")
        return TRK["x2_pe"]

    def ln_statsB(PE_STATS, rows, brow, rstdB_t, nmrB_t, kv, banks=(2, 3)):
        bx, bx2 = banks
        mS, e2S, varr, scr = rowslots(rows)
        rstd16, nmr16 = brow[:, 0:512], brow[:, 512:1024]
        VE_MS = op("vector", lambda e, bx=bx: e.tensor_scalar(
            mS, pb[bx][0:1, :], 1.0 / D, None, ALU.mult),
           waits=[(m_pe, PE_STATS), (a_sc, TRK["rstd_sc"].get("last", 0))])
        op("vector", lambda e, bx2=bx2: e.tensor_scalar(
            e2S, pb[bx2][0:1, :], 1.0 / D, None, ALU.mult))
        op("vector", lambda e: e.tensor_tensor(varr, mS, mS, ALU.mult))
        VE_VAR = op("vector", lambda e: e.tensor_tensor(varr, e2S, varr, ALU.subtract))
        TRK["stats_ve"][banks] = VE_VAR
        op("scalar", lambda e: e.activation(scr, varr, AF.Ln,
                                            bias=ftiny[:, EPS5:EPS5 + 1]),
           waits=[(v_ve, VE_VAR)])
        SC_R16 = op("scalar", lambda e: e.activation(
            rstd16, scr, AF.Exp, scale=ftiny[:, NEGH:NEGH + 1]), inc="sc")
        SC_RF = op("scalar", lambda e: e.activation(
            varr, scr, AF.Exp, scale=ftiny[:, NEGH:NEGH + 1]), inc="sc")
        TRK["rstd_sc"]["last"] = SC_RF
        op("vector", lambda e: e.tensor_tensor(scr, mS, varr, ALU.mult),
           waits=[(a_sc, SC_RF)])
        VE_NMR = op("vector", lambda e: e.tensor_scalar(nmr16, scr, -1.0, None,
                                                        ALU.mult))
        PE_BC = op("tensor", lambda e: e.matmul(
            pb[4][:, :], ones_row[:, :], rstd16, start=True, stop=True),
            waits=[(a_sc, SC_R16), (v_ve, TRK["rb_ve"]), (v_ve, TRK["k2_ve"])],
            inc="pe")
        VE_RB = op("vector", lambda e, r=rstdB_t: e.tensor_copy(r[:, :], pb[4][:, :]),
                   waits=[(m_pe, PE_BC)])
        TRK["rb_ve"] = VE_RB
        if not kv:
            PE_BC2 = op("tensor", lambda e: e.matmul(
                pb[5][:, :], ones_row[:, :], nmr16, start=True, stop=True),
                waits=[(v_ve, VE_NMR), (v_ve, TRK["nb_ve"])], inc="pe")
            VE_NB = op("vector", lambda e, r=nmrB_t: e.tensor_copy(r[:, :], pb[5][:, :]),
                       waits=[(m_pe, PE_BC2)])
            TRK["nb_ve"] = VE_NB
        return VE_NMR, VE_RB

    def ln_kv(t, XBuf, rstdB_t, nmr16, VE_NMR, VE_RB, first):
        def emit_t1(dc):
            gdc = t * NCH + dc
            sl = (gdc % 8) * 512
            wv = [(v_ve, VE_RB)] if dc == 0 else []
            if gdc >= 8:
                wv.append((m_pe, TRK["kvmark"].get((gdc - 8 + 3) // 4, 0)))
            TRK["t1_ve"][dc] = op("vector", lambda e, dc=dc, sl=sl: e.tensor_tensor(
                T1R[:, sl:sl + 512], XBuf[:, dc * 512:(dc + 1) * 512],
                rstdB_t[:, :], ALU.mult), waits=wv)
        emit_t1(0)
        for dc in range(NCH):
            gdc = t * NCH + dc
            sl = (gdc % 8) * 512
            if dc + 1 < NCH:
                emit_t1(dc + 1)
            w = [(v_ve, TRK["t1_ve"][min(dc + 1, NCH - 1)])]
            if dc == 0 and first:
                w += [(v_ve, VE_CS), (s_p0, P0C)]
            op("tensor", lambda e, dc=dc, sl=sl: e.matmul(
                pb[0][:, :], wkTb[:, dc * HD:(dc + 1) * HD],
                T1R[:, sl:sl + 512],
                start=(dc == 0), stop=False), waits=w)
            ii = op("tensor", lambda e, dc=dc, sl=sl: e.matmul(
                pb[1][:, :], wvTb[:, dc * HD:(dc + 1) * HD],
                T1R[:, sl:sl + 512],
                start=(dc == 0), stop=False),
                inc="pe" if dc % 4 == 3 else None)
            if dc % 4 == 3:
                TRK["kvmark"][gdc // 4] = ii
        op("tensor", lambda e, nmr16=nmr16: e.matmul(pb[0][:, :], csk16[:, :],
                                                     nmr16, start=False, stop=True),
           waits=[(v_ve, VE_NMR)])
        PE_KV = op("tensor", lambda e, nmr16=nmr16: e.matmul(
            pb[1][:, :], csv16[:, :], nmr16, start=False, stop=True), inc="pe")
        TRK["kv_pe"] = PE_KV
        op("vector", lambda e, t=t: e.tensor_scalar(
            kT[:, t * 512:(t + 1) * 512], pb[0][:, :], wkb[:, :], None, ALU.add),
           waits=[(m_pe, PE_KV)])
        VE_KV = op("vector", lambda e: e.tensor_scalar(
            p1VTs[:, :], pb[1][:, :], wvb[:, :], None, ALU.add))
        TRK["kv_ve"] = VE_KV
        for j in range(4):
            TRK["tr_pe"][j] = op("tensor", lambda e, j=j: e.transpose(
                pbf(6)[:, 0:128], p1VTs[:, j * 128:(j + 1) * 128], identB[:, :]),
                waits=[(v_ve, VE_KV), (v_ve, TRK["tr_ve"][(j - 1) % 4])], inc="pe")
            TRK["tr_ve"][j] = op("vector", lambda e, t=t, j=j: e.tensor_copy(
                Vt[:, (4 * t + j) * 128:(4 * t + j + 1) * 128], pbf(6)[:, 0:128]),
                waits=[(m_pe, TRK["tr_pe"][j])])
        VE_K2B = op("vector", lambda e, t=t: e.tensor_tensor(
            p1k2b[:, :], kT[:, t * 512:(t + 1) * 512],
            kT[:, t * 512:(t + 1) * 512], ALU.mult))
        PE_K2 = op("tensor", lambda e: e.matmul(
            pb[4][0:1, :], ones_col[:, :], p1k2b[:, :], start=True, stop=True),
            waits=[(v_ve, VE_K2B), (v_ve, TRK["rb_ve"])], inc="pe")
        TRK["k2_ve"] = op("vector", lambda e, t=t: e.reduce_max(
            krow[:, t:t + 1], pb[4][0:1, :], axis=mybir.AxisListType.X),
            waits=[(m_pe, PE_K2)])

    def ln_tile(tile, XBuf, X2Buf, sxb_sem, sxb_cnt, rstdB_t, nmrB_t,
                rows, brow, kv, first):
        t = tile
        PE_STATS = ln_statsA(XBuf, X2Buf, sxb_sem, sxb_cnt, first)
        ln_statsB(PE_STATS, rows, brow, rstdB_t, nmrB_t, kv)
        for dc in range(NCH):
            op("vector", lambda e, dc=dc, t=t: e.tensor_tensor(
                xnbq[:, dc * NQ + t * 512: dc * NQ + t * 512 + 512],
                XBuf[:, dc * 512:(dc + 1) * 512], rstdB_t[:, :], ALU.mult))
            op("vector", lambda e, dc=dc, t=t: e.tensor_tensor(
                xnbq[:, dc * NQ + t * 512: dc * NQ + t * 512 + 512],
                xnbq[:, dc * NQ + t * 512: dc * NQ + t * 512 + 512],
                nmrB_t[:, :], ALU.add))

    VE_XB_DONE = [0, 0]
    pestats = {}
    bctx = {}
    for t in range(9):
        if t < 8:
            par = t % 2
            w = [(v_ve, VE_XB_DONE[par])] if t >= 2 else ()
            for dc in range(NCH):
                dma("gpsimd", XB[par][:, dc * 512:(dc + 1) * 512],
                    AP(d_xT, dc * 128 * T + t * 512, [[T, 128], [1, 512]]),
                    s_xb[par], waits=w if dc == 0 else ())
            pestats[t] = ln_statsA_p1(t, XB[par], s_xb[par], DC[id(s_xb[par])],
                                      first=(t == 0),
                                      banks=(2, 3) if par == 0 else (5, 7))
        if t >= 1:
            tp = t - 1
            parp = tp % 2
            VE_NMR, VE_RB = bctx[tp]
            ln_kv(tp, XB[parp], p1rstdB[parp], p1brow[parp][:, 512:1024],
                  VE_NMR, VE_RB, first=(tp == 0))
            VE_XB_DONE[parp] = C["ve"]
        if t < 8:
            bctx[t] = ln_statsB(pestats[t], p1rows[par], p1brow[par],
                                p1rstdB[par], None, kv=True,
                                banks=(2, 3) if par == 0 else (5, 7))
    P1_PE, P1_SC, P1_VE = C["pe"], C["sc"], C["ve"]

    VE_XQ_DONE = 0
    for t in range(2):
        par = t % 2
        for dc in range(NCH):
            w = []
            if dc == 0:
                w = ([(v_ve, P1_VE), (a_sc, P1_SC)] if t == 0
                     else [(v_ve, VE_XQ_DONE)])
                if t == 0 and DEBUG:
                    w.append((s_st, DBG_P1))
            dma("gpsimd", XQB[:, dc * 512:(dc + 1) * 512],
                AP(d_xqT, dc * 128 * NQ + t * 512, [[NQ, 128], [1, 512]]),
                s_xq[par], waits=w)
        ln_tile(t, XQB, X2F2, s_xq[par], DC[id(s_xq[par])], p2rstdB, p2nmrB,
                p2rows, p2brow, kv=False, first=False)
        VE_XQ_DONE = C["ve"]
    P2A_VE = C["ve"]

    VE_Z = {}
    for qt in range(2):
        for dc in range(NCH):
            w = []
            if dc == 0:
                w = [(v_ve, max(P2A_VE, P1_VE) if qt == 0 else VE_Z[0])]
            op("tensor", lambda e, qt=qt, dc=dc: e.matmul(
                pb[0][:, :], ATb[:, dc * 128:(dc + 1) * 128],
                xnbq[:, dc * NQ + qt * 512: dc * NQ + qt * 512 + 512],
                start=(dc == 0), stop=(dc == NCH - 1)),
               waits=w, inc="pe" if dc == NCH - 1 else None)
        PE_Z = C["pe"]
        VE_Z[qt] = op("vector", lambda e, qt=qt: e.tensor_copy(
            z16[:, qt * 512:(qt + 1) * 512], pb[0][:, :]),
            waits=[(m_pe, PE_Z)], inc="ve")
    SC_LGv = {}
    for qt in range(2):
        for dc in range(NCH):
            w = []
            if dc == 0 and qt == 1:
                w = [(v_ve, SC_LGv[0])]
            op("tensor", lambda e, qt=qt, dc=dc: e.matmul(
                pb[1][0:16, :], rTb[:, dc * 16:(dc + 1) * 16],
                xnbq[:, dc * NQ + qt * 512: dc * NQ + qt * 512 + 512],
                start=(dc == 0), stop=(dc == NCH - 1)),
               waits=w, inc="pe" if dc == NCH - 1 else None)
        PE_R = C["pe"]
        SC_LGv[qt] = op("vector", lambda e, qt=qt: e.tensor_scalar(
            lg16[0:16, qt * 512:(qt + 1) * 512], pb[1][0:16, :], rbv[:, :],
            None, ALU.add), waits=[(m_pe, PE_R), (s_ld, LD0)], inc="ve")
    VE_LG = SC_LGv[1]
    SC_LP = op("scalar", lambda e: e.activation(lp16[:, :], prv[:, :], AF.Ln,
                                                bias=eps12c[:, :]),
               waits=[(s_ld, LD0), (v_ve, VE_CONST)], inc="sc")
    PE_SEL = op("tensor", lambda e: e.matmul(pb[2][0:4, 0:1], S_e16[:, :],
                                             lp16[:, :], start=True, stop=True),
                waits=[(a_sc, SC_LP), (v_ve, VE_SE)], inc="pe")
    VE_A4 = op("vector", lambda e: e.tensor_copy(lp4[:, :], pb[2][0:4, 0:1]),
               waits=[(m_pe, PE_SEL)], inc="ve")
    PE_SFT = op("tensor", lambda e: e.transpose(pb[5][0:4, 0:16], S_f[:, :],
                                                identF16[:, :]),
                waits=[(v_ve, VE_SE)], inc="pe")
    VE_SET = op("vector", lambda e: e.tensor_copy(S_eT16[:, :], pb[5][0:4, 0:16]),
                waits=[(m_pe, PE_SFT)], inc="ve")
    VE_PRE_prev = 0
    SC_E4_prev = 0
    VE_ALT_prev = 0
    for qt in range(2):
        PE_SL = op("tensor", lambda e, qt=qt: e.matmul(
            pb[4][0:4, :], S_e16[:, :], lg16[0:16, qt * 512:(qt + 1) * 512],
            start=True, stop=True),
            waits=[(v_ve, VE_LG), (v_ve, VE_SET), (v_ve, VE_PRE_prev)], inc="pe")
        VE_PRE = op("vector", lambda e: e.tensor_scalar(
            pre_f[:, :], pb[4][0:4, :], lp4[:, :], None, ALU.add),
            waits=[(m_pe, PE_SL)], inc="ve")
        VE_PRE_prev = VE_PRE
        for sx in range(4):
            PE_T1 = op("tensor", lambda e, sx=sx: e.transpose(
                pb[5][0:128, 0:4], pre_f[:, sx * 128:(sx + 1) * 128], identF4[:, :]),
                waits=[(v_ve, VE_PRE), (a_sc, SC_E4_prev)], inc="pe")
            VE_MX = op("vector", lambda e: e.reduce_max(
                nmxc[:, :], pb[5][0:128, 0:4], axis=mybir.AxisListType.X,
                negate=True), waits=[(m_pe, PE_T1)], inc="ve")
            SC_E4 = op("scalar", lambda e: e.activation(
                e4t[:, :], pb[5][0:128, 0:4], AF.Exp, bias=nmxc[:, :]),
                waits=[(v_ve, VE_MX)], inc="sc")
            SC_E4_prev = SC_E4
            VE_S4 = op("vector", lambda e: e.reduce_sum(
                s4c[:, :], e4t[:, :], axis=mybir.AxisListType.X),
                waits=[(a_sc, SC_E4)], inc="ve")
            op("scalar", lambda e: e.activation(mxc[:, :], s4c[:, :], AF.Ln),
               waits=[(v_ve, VE_S4)])
            SC_RS = op("scalar", lambda e: e.activation(
                rsc[:, :], mxc[:, :], AF.Exp, scale=neg1c[:, :]), inc="sc")
            VE_AL = op("vector", lambda e: e.tensor_scalar(
                alpha16[:, :], e4t[:, :], rsc[:, :], None, ALU.mult),
                waits=[(a_sc, SC_RS)], inc="ve")
            PE_T2 = op("tensor", lambda e: e.transpose(
                pbf(6)[0:4, 0:128], alpha16[:, :], identB[:, :]),
                waits=[(v_ve, VE_AL), (v_ve, VE_ALT_prev)], inc="pe")
            VE_ALT_prev = op("vector", lambda e, qt=qt, sx=sx: e.tensor_copy(
                alphaT16[:, qt * 512 + sx * 128: qt * 512 + sx * 128 + 128],
                pbf(6)[0:4, 0:128]), waits=[(m_pe, PE_T2)], inc="ve")
    VE_ALT = VE_ALT_prev
    VE_AE = {}
    for qt in range(2):
        PE_AE = op("tensor", lambda e, qt=qt: e.matmul(
            pb[7][0:16, :], S_eT16[:, :], alphaT16[:, qt * 512:(qt + 1) * 512],
            start=True, stop=True),
            waits=[(v_ve, VE_ALT if qt == 0 else VE_AE[0]),
                   (v_ve, TRK["k2_ve"])], inc="pe")
        VE_AE[qt] = op("vector", lambda e, qt=qt: e.tensor_copy(
            aE16[0:16, qt * 512:(qt + 1) * 512], pb[7][0:16, :]),
            waits=[(m_pe, PE_AE)], inc="ve")
    VE_ZP = 0
    VE_AB_prev = 0
    for qt in range(2):
        PE_AB = op("tensor", lambda e, qt=qt: e.matmul(
            pb[0][:, :], R2b[:, :], aE16[0:16, qt * 512:(qt + 1) * 512],
            start=True, stop=True),
            waits=[(v_ve, VE_AE[1]), (v_ve, VE_AB_prev)], inc="pe")
        VE_AB = op("vector", lambda e: e.tensor_copy(aB16[:, :], pb[0][:, :]),
                   waits=[(m_pe, PE_AB)], inc="ve")
        VE_AB_prev = VE_AB
        VE_ZP = op("vector", lambda e, qt=qt: e.tensor_tensor(
            zp16[:, qt * 512:(qt + 1) * 512], z16[:, qt * 512:(qt + 1) * 512],
            aB16[:, :], ALU.mult), waits=[(v_ve, VE_AB)], inc="ve")
    P2B_PE = C["pe"]

    if DEBUG:
        dma("sync", d_dbg["dbg_rows"].ap()[:, :], p2rows[:, :], s_st, waits=[(v_ve, VE_ZP)])
        dma("sync", d_dbg["dbg_brow"].ap()[:, :], p2brow[:, :], s_st)
        dma("sync", d_dbg["dbg_rB"].ap()[:, :], p2rstdB[:, :], s_st)
        dma("sync", d_dbg["dbg_nB"].ap()[:, :], p2nmrB[:, :], s_st)
        dma("sync", d_dbg["dbg_z"].ap()[:, :], z16[:, :], s_st)
        dma("sync", d_dbg["dbg_pre"].ap()[:, :], pre_f[:, :], s_st)
        dma("sync", d_dbg["dbg_lp4"].ap()[:, :], lp4[:, :], s_st)
        dma("sync", d_dbg["dbg_Sf"].ap()[:, :], S_f[:, :], s_st)
        dma("sync", d_dbg["dbg_e4"].ap()[:, :], e4t[:, :], s_st)
        dma("sync", d_dbg["dbg_lpv"].ap()[:, :], lpv[:, :], s_st)
        dma("sync", d_dbg["dbg_lp16"].ap()[:, :], lp16[:, :], s_st)
        dma("sync", d_dbg["dbg_Se16"].ap()[:, :], S_e16[:, :], s_st)
        dma("sync", d_dbg["dbg_a4"].ap()[:, :], lp4[:, :], s_st)
        dma("sync", d_dbg["dbg_can2"].ap()[:, :], canary[:, :], s_st)
        dma("sync", d_dbg["dbg_zp"].ap()[:, :], zp16[:, :], s_st)
        dma("sync", d_dbg["dbg_lg"].ap()[:, :], lg16[:, :], s_st)
        dma("sync", d_dbg["dbg_alT"].ap()[:, :], alphaT16[:, :], s_st)
        dma("sync", d_dbg["dbg_aE"].ap()[:, :], aE16[:, :], s_st)
        dma("sync", d_dbg["dbg_kT"].ap()[:, :], kT[:, :], s_st)
        dma("sync", d_dbg["dbg_Vt"].ap()[:, :], Vt[:, :], s_st)
        dma("sync", d_dbg["dbg_xn"].ap()[:, :], xnbq[:, :], s_st)

    SC_Qog = {}
    SC_Qev = {}
    for og in range(16):
        par = og % 2
        w = []
        if og == 0:
            w = [(m_pe, P2B_PE)]
        elif og >= 2:
            w = [(v_ve, SC_Qog[og - 2])]
        for dc in range(NCH):
            dma("gpsimd", WQP[par][:, dc * 256:(dc + 1) * 256],
                AP(d_wqT, dc * 128 * D + og * 256, [[D, 128], [1, 256]]),
                s_wq[par], waits=w if dc == 0 else ())
        cnt = DC[id(s_wq[par])]
        for oc in range(2):
            for qt in range(2):
                ocg = og * 2 + oc
                bank = 2 * oc + qt
                for dc in range(NCH):
                    ww = []
                    if dc == 0:
                        ww = [(s_wq[par], cnt),
                              (v_ve, VE_ZP if og == 0 else SC_Qev.get((og - 1, bank), 0))]
                    op("tensor", lambda e, dc=dc, oc=oc, qt=qt, bank=bank, par=par:
                       e.matmul(
                        pb[bank][:, :],
                        WQP[par][:, dc * 256 + oc * 128: dc * 256 + oc * 128 + 128],
                        xnbq[:, dc * NQ + qt * 512: dc * NQ + qt * 512 + 512],
                        start=(dc == 0), stop=(dc == NCH - 1)),
                       waits=ww, inc="pe" if dc == NCH - 1 else None)
                PE_Q = C["pe"]
                SC_Qev[(og, bank)] = op("vector", lambda e, ocg=ocg, qt=qt, bank=bank:
                                        e.tensor_scalar(
                    qTat[:, ocg * NQ + qt * 512: ocg * NQ + qt * 512 + 512],
                    pb[bank][:, :], wq_bP[:, ocg:ocg + 1], None, ALU.add),
                    waits=[(m_pe, PE_Q)], inc="ve")
        SC_Qog[og] = SC_Qev[(og, 3)]
    VE_QP = C["ve"]
    VE_Q2 = 0
    PE_Q2 = 0
    for hh in range(H):
        VE_SQH = op("vector", lambda e, hh=hh: e.tensor_tensor(
            q2b[:, :], qTat[:, hh * NQ:(hh + 1) * NQ],
            qTat[:, hh * NQ:(hh + 1) * NQ], ALU.mult),
            waits=[(v_ve, VE_QP), (m_pe, PE_Q2)], inc="ve")
        for qt in range(2):
            PE_Q2 = op("tensor", lambda e, qt=qt: e.matmul(
                pb[4][0:1, :], ones_col[:, :], q2b[:, qt * 512:(qt + 1) * 512],
                start=True, stop=True),
                waits=[(v_ve, VE_SQH), (v_ve, VE_Q2)], inc="pe")
            VE_Q2 = op("vector", lambda e, hh=hh, qt=qt: e.reduce_max(
                qrow[:, 2 * hh + qt:2 * hh + qt + 1], pb[4][0:1, :],
                axis=mybir.AxisListType.X), waits=[(m_pe, PE_Q2)], inc="ve")
    VE_KMX = op("vector", lambda e: e.reduce_max(
        ftiny[:, KM2:KM2 + 1], krow[:, :], axis=mybir.AxisListType.X), inc="ve")
    VE_QMX = op("vector", lambda e: e.reduce_max(
        ftiny[:, QM2:QM2 + 1], qrow[:, :], axis=mybir.AxisListType.X),
        waits=[(v_ve, VE_Q2)], inc="ve")
    op("scalar", lambda e: e.activation(ftiny[:, TSC:TSC + 1],
                                        ftiny[:, KM2:KM2 + 1], AF.Ln),
       waits=[(v_ve, VE_QMX)])
    SC_LNS = op("scalar", lambda e: e.activation(ftiny[:, CSC:CSC + 1],
                                                 ftiny[:, QM2:QM2 + 1], AF.Ln),
                inc="sc")
    VE_C1 = op("vector", lambda e: e.tensor_tensor(
        ftiny[:, TSC:TSC + 1], ftiny[:, TSC:TSC + 1], ftiny[:, CSC:CSC + 1],
        ALU.add), waits=[(a_sc, SC_LNS)], inc="ve")
    SC_C = op("scalar", lambda e: e.activation(
        ftiny[:, CSC:CSC + 1], ftiny[:, TSC:TSC + 1], AF.Exp,
        scale=ftiny[:, POSH:POSH + 1]), waits=[(v_ve, VE_C1)], inc="sc")
    VE_C2 = op("vector", lambda e: e.tensor_scalar(
        btiny[:, 0:1], ftiny[:, CSC:CSC + 1], -RSQ, None, ALU.mult),
        waits=[(a_sc, SC_C)], inc="ve")
    PE_C = op("tensor", lambda e: e.matmul(
        pb[4][0:128, 0:1], ones_row[:, :], btiny[0:1, 0:1], start=True, stop=True),
        waits=[(v_ve, VE_C2)], inc="pe")
    VE_CB = op("vector", lambda e: e.tensor_copy(cbias[:, :], pb[4][:, 0:1]),
               waits=[(m_pe, PE_C)], inc="ve")

    DBG_GATE = 0
    if DEBUG:
        dma("sync", d_dbg["dbg_ft"].ap()[:, :], ftiny[:, :], s_st, waits=[(v_ve, VE_CB)])
        dma("sync", d_dbg["dbg_cb"].ap()[:, :], cbias[:, :], s_st)
        dma("sync", d_dbg["dbg_qT"].ap()[:, :], qTat[:, :], s_st, waits=[(v_ve, VE_QP)])
        DBG_GATE = DC[id(s_st)]

    pq = [pA[0], pA[1]]
    pzavs = [pA[2], pA[3]]
    MK_DONE = [0, 0]
    itctr = [0]
    last_exp_pq = {0: 0, 1: 0}
    last_av_pb = {0: 0, 1: 0, 2: 0, 3: 0}
    last_attn_ve = {0: 0, 1: 0}
    last_rzb_ve = {0: 0, 1: 0}
    zav_pend = []
    first_exp = True
    pend = {"pe": [], "ve": [], "sc": []}
    pendq = []
    DEPTH = {"pe": 1, "ve": 1, "sc": 1}

    def flushpend():
        for k, eng in (("sc", "scalar"), ("pe", "tensor"), ("ve", "vector")):
            if len(pendq) >= DEPTH[k]:
                for qi in range(len(pendq) - DEPTH[k] + 1):
                    for fn in pendq[qi][k]:
                        Q[eng].append(fn)
                    pendq[qi][k] = []

    def rotatepend():
        pendq.append({"pe": pend["pe"], "ve": pend["ve"], "sc": pend["sc"]})
        pend["pe"] = []
        pend["ve"] = []
        pend["sc"] = []
        while pendq and not (pendq[0]["pe"] or pendq[0]["ve"] or pendq[0]["sc"]):
            pendq.pop(0)

    def flushall():
        rotatepend()
        for q in pendq:
            for k, eng in (("sc", "scalar"), ("pe", "tensor"), ("ve", "vector")):
                for fn in q[k]:
                    Q[eng].append(fn)
        pendq.clear()

    def pe_pend(build, waits=(), inc=None):
        w = tuple((sm, v) for (sm, v) in waits if v > 0)
        tgt = None
        if inc:
            C[inc] += 1
            tgt = SEMOBJ[inc]

        def fn(engine, w=w, build=build, tgt=tgt):
            for (sm, v) in w:
                engine.wait_ge(sm, v)
            ins = build(engine)
            if tgt is not None and ins is not None:
                ins.then_inc(tgt, 1)
        pend["pe"].append(fn)
        return C[inc] if inc else None

    def sc_pend(build, waits=(), inc=None):
        w = tuple((sm, v) for (sm, v) in waits if v > 0)
        tgt = None
        if inc:
            C[inc] += 1
            tgt = SEMOBJ[inc]

        def fn(engine, w=w, build=build, tgt=tgt):
            for (sm, v) in w:
                engine.wait_ge(sm, v)
            ins = build(engine)
            if tgt is not None and ins is not None:
                ins.then_inc(tgt, 1)
        pend["sc"].append(fn)
        return C[inc] if inc else None

    def ve_pend(build, waits=()):
        w = [(sm, v) for (sm, v) in waits if v > 0]
        if C["vn"] > 0:
            w.append((m_vn, C["vn"]))
        C["vn"] += 1
        tgt = m_vn
        w = tuple(w)

        def fn(engine, w=w, build=build, tgt=tgt):
            for (sm, v) in w:
                engine.wait_ge(sm, v)
            ins = build(engine)
            if ins is not None:
                ins.then_inc(tgt, 1)
        pend["ve"].append(fn)
        return C["vn"]

    norm_jobs = []

    def emit_norm():
        if not norm_jobs:
            return
        qoff, zpar, pzav, npair_ = norm_jobs.pop(0)
        while zav_pend:
            emit_zav(zav_pend.pop(0))
        PE_AVL = last_av_pb[(npair_ - 1) % 4]
        rzv = rz16 if zpar == 0 else rz16b
        rzbv = rzbS if zpar == 0 else rzbS2
        SN_LN = sc_pend(lambda e, pzav=pzav: e.activation(
            lnzr[:, :], pzav.ap()[0:1, 0:512], AF.Ln),
            waits=[(m_pe, PE_AVL), (m_sn, C["sn"])], inc="sn")
        last_lnz = sc_pend(lambda e, rzv=rzv: e.activation(
            rzv[:, :], lnzr[:, :], AF.Exp, scale=ftiny[:, NEG1:NEG1 + 1]),
            waits=[(m_sn, SN_LN)], inc="sn")
        PE_RB = pe_pend(lambda e, pzav=pzav, rzv=rzv: e.matmul(
            pzav.ap()[:, 0:512], ones_row[:, :], rzv[:, :],
            start=True, stop=True),
            waits=[(m_sn, last_lnz)], inc="pn")
        last_rzb_ve[zpar] = ve_pend(lambda e, pzav=pzav, rzbv=rzbv: e.tensor_copy(
            rzbv[:, :], pzav.ap()[:, 0:512]),
            waits=[(m_pn, PE_RB)])
        out_ap = AP(qTat, qoff, [[NCH * NQ, 128], [NQ, 4], [1, 128]])
        last_attn_ve[zpar] = ve_pend(lambda e, out_ap=out_ap, pzav=pzav,
                                     rzbv=rzbv: e.tensor_tensor(
            out_ap, pzav.ap()[:, 512:1024], rzbv[:, :], ALU.mult))
        rotatepend()

    for g in range(NSLOT):
        nkc = 4 * g + 4
        npair = nkc // 2
        par = g % 2
        wm0 = [(v_ve, MK_DONE[par])] if g >= 2 else [(v_ve, max(VE_ZP, VE_QP)),
                                                     (m_pe, P2B_PE)]
        for j in range(4):
            dma("sync", MB[par][:, j * 512:(j + 1) * 512],
                d_masks.ap()[4 * g + j, :, :], s_mk[par], waits=wm0 if j == 0 else ())
        mkcnt = DC[id(s_mk[par])]
        for hg in range(NHG):
            qoff = (4 * hg) * NQ + g * 128
            rhs_q = AP(qTat, qoff, [[NCH * NQ, 128], [NQ, 4], [1, 128]])
            zpar = itctr[0] % 2
            pzav = pzavs[zpar]

            def emit_zav(job):
                p, dep, nkc_, pz, zp = job
                wz = [dep, (m_vn, last_rzb_ve[zp])] if p == 0 else [dep]
                op("tensor", lambda e, p=p, nkc_=nkc_, pz=pz: e.matmul(
                    pz.ap()[0:1, 0:512], ones_col[:, :],
                    Pbp[p % 4][:, 0:512],
                    start=(2 * p == 0), stop=False), waits=wz)
                op("tensor", lambda e, p=p, nkc_=nkc_, pz=pz: e.matmul(
                    pz.ap()[0:1, 0:512], ones_col[:, :],
                    Pbp[p % 4][:, 512:1024],
                    start=False, stop=(2 * p + 1 == nkc_ - 1)))
                wa = [(m_vn, last_attn_ve[zp])] if p == 0 else []
                op("tensor", lambda e, p=p, nkc_=nkc_, pz=pz: e.matmul(
                    pz.ap()[:, 512:1024],
                    Vt[:, (2 * p) * 128:(2 * p) * 128 + 128],
                    Pbp[p % 4][:, 0:512],
                    start=(2 * p == 0), stop=False), waits=wa)
                last_av_pb[p % 4] = op("tensor",
                                       lambda e, p=p, nkc_=nkc_, pz=pz: e.matmul(
                    pz.ap()[:, 512:1024],
                    Vt[:, (2 * p + 1) * 128:(2 * p + 1) * 128 + 128],
                    Pbp[p % 4][:, 512:1024],
                    start=False, stop=(2 * p + 1 == nkc_ - 1)), inc="pe")

            for p in range(npair):
                pi = p % 2
                t = pq[pi]
                ws = [(a_sc, last_exp_pq[pi])]
                if first_exp:
                    ws.append((v_ve, VE_CB))
                op("tensor", lambda e, p=p, t=t, rhs_q=rhs_q: e.matmul(
                    t.ap()[:, 0:512], kT[:, (2 * p) * 128:(2 * p) * 128 + 128],
                    rhs_q, start=True, stop=True), waits=ws)
                PE_S = op("tensor", lambda e, p=p, t=t, rhs_q=rhs_q: e.matmul(
                    t.ap()[:, 512:1024], kT[:, (2 * p + 1) * 128:(2 * p + 1) * 128 + 128],
                    rhs_q, start=True, stop=True), inc="pe")
                if p == 0:
                    flushpend()
                we = [(m_pe, PE_S), (m_pe, last_av_pb[p % 4])]
                SC_E = op("scalar", lambda e, p=p, t=t: e.activation(
                    Pbp[p % 4][:, :], t.ap()[:, :], AF.Exp, bias=cbias[:, :],
                    scale=escale[:, :]), waits=we, inc="sc")
                last_exp_pq[pi] = SC_E
                first_exp = False
                dep = (a_sc, SC_E)
                if p >= 2 * g:
                    jj = 2 * p - 4 * g
                    wmm = [(a_sc, SC_E)]
                    if hg == 0 and p == 2 * g:
                        wmm.append((s_mk[par], mkcnt))
                    VE_M = op("vector", lambda e, p=p, jj=jj, par=par: e.tensor_tensor(
                        Pbp[p % 4][:, :], Pbp[p % 4][:, :],
                        MB[par][:, jj * 512:(jj + 2) * 512], ALU.mult),
                        waits=wmm)
                    dep = (v_ve, VE_M)
                zav_pend.append((p, dep, nkc, pzav, zpar))
                if len(zav_pend) > 2:
                    emit_zav(zav_pend.pop(0))
            norm_jobs.append((qoff, zpar, pzav, npair))
            emit_norm()
            itctr[0] += 1
            PE_AVL = last_av_pb[(npair - 1) % 4]
        MK_DONE[par] = C["ve"]
    while norm_jobs:
        emit_norm()
    flushall()
    P3_PE, P3_SC, P3_VE = C["pe"], C["sc"], C["ve"]
    P3_VN = C["vn"]

    if DEBUG:
        dma("sync", d_dbg["dbg_at"].ap()[:, :], qTat[:, :], s_st, waits=[(v_ve, P3_VE)])

    SC_P4og = {}
    VE_P4 = {}
    last_xrs = {0: P3_VE, 1: P3_VE, 2: P3_VE, 3: P3_VE}
    first_p4 = True
    last_sev = {0: 0, 1: 0, 2: 0, 3: 0}
    last_yev = {0: 0, 1: 0, 2: 0, 3: 0}
    last_sg = {0: 0, 1: 0}
    last_ymm = 0
    for og in range(16):
        par = og % 2
        w = []
        if og == 0:
            w = [(v_ve, P3_VE), (m_vn, P3_VN)]
        elif og >= 2:
            w = [(a_sc, SC_P4og[og - 2])]
        for dc in range(NCH):
            dma("gpsimd", WOP[par][:, dc * 256:(dc + 1) * 256],
                AP(d_woT, dc * 128 * D + og * 256, [[D, 128], [1, 256]]),
                s_wo[par], waits=w if dc == 0 else ())
        wocnt = DC[id(s_wo[par])]
        wwb = [(v_ve, P3_VE), (m_vn, P3_VN)] if og == 0 else [(m_pe, last_ymm)]
        for dc in range(NCH):
            dma("gpsimd", WBP[:, dc * 256:(dc + 1) * 256],
                AP(d_WbT, dc * 128 * D + og * 256, [[D, 128], [1, 256]]),
                s_wb, waits=wwb if dc == 0 else ())
        dma("gpsimd", WBP[:, NCH * 256:NCH * 256 + 256],
            AP(d_Bt, og * 256, [[D, 128], [1, 256]]), s_wb)
        wbcnt = DC[id(s_wb)]
        SC_SE = {}
        for oc in range(2):
            for qt in range(2):
                ocg = og * 2 + oc
                bk = 2 * oc + qt
                xw = [(v_ve, last_xrs[bk])]
                if og == 0:
                    xw.append((m_vn, P3_VN))
                xrcnt = dma("sync", XRS[bk][:, :],
                            AP(d_xqT, ocg * 128 * NQ + qt * 512, [[NQ, 128], [1, 512]]),
                            s_xr[bk], waits=xw)
                XRC = xrcnt
                for dc in range(NCH):
                    ws = []
                    if dc == 0:
                        ws = [(s_wo[par], wocnt), (v_ve, last_sev[bk])]
                    op("tensor", lambda e, dc=dc, oc=oc, qt=qt, bk=bk, par=par:
                       e.matmul(
                        pb[bk][:, :],
                        WOP[par][:, dc * 256 + oc * 128: dc * 256 + oc * 128 + 128],
                        qTat[:, dc * NQ + qt * 512: dc * NQ + qt * 512 + 512],
                        start=(dc == 0), stop=(dc == NCH - 1)),
                       waits=ws, inc="pe" if dc == NCH - 1 else None)
                PE_S4 = C["pe"]
                sw = [(m_pe, PE_S4)]
                if og == 0:
                    sw.append((m_vn, P3_VN))
                last_sev[bk] = op("vector", lambda e, ocg=ocg, bk=bk:
                                  e.tensor_scalar(
                    SGT[bk][:, :], pb[bk][:, :], wo_bP[:, ocg:ocg + 1],
                    None, ALU.add), waits=sw)
                SC_SE[(oc, qt)] = (last_sev[bk], XRC)
        for oc in range(2):
            for qt in range(2):
                ocg = og * 2 + oc
                bk = 2 * oc + qt
                ybk = 4 + bk
                for dc in range(NCH):
                    wy = []
                    if dc == 0:
                        wy = [(s_wb, wbcnt), (a_sc, last_yev[bk])]
                    op("tensor", lambda e, dc=dc, oc=oc, qt=qt, ybk=ybk: e.matmul(
                        pb[ybk][:, :],
                        WBP[:, dc * 256 + oc * 128: dc * 256 + oc * 128 + 128],
                        xnbq[:, dc * NQ + qt * 512: dc * NQ + qt * 512 + 512],
                        start=(dc == 0), stop=False), waits=wy)
                last_ymm = op("tensor", lambda e, oc=oc, qt=qt, ybk=ybk: e.matmul(
                    pb[ybk][:, :], WBP[:, NCH * 256 + oc * 128: NCH * 256 + oc * 128 + 128],
                    zp16[:, qt * 512:(qt + 1) * 512], start=False, stop=True),
                    inc="pe")
                PE_Y4 = last_ymm
                SEV, XRC = SC_SE[(oc, qt)]
                last_yev[bk] = op("scalar", lambda e, ocg=ocg, ybk=ybk, qt=qt:
                                  e.activation(
                    SGT2[qt][:, :], pb[ybk][:, :], AF.Silu,
                    bias=bb_P[:, ocg:ocg + 1]),
                    waits=[(m_pe, PE_Y4), (s_ld, LD0), (v_ve, last_sg[qt])], inc="sc")
                SC_EE = last_yev[bk]
                last_sg[qt] = op("vector", lambda e, qt=qt, bk=bk: e.tensor_tensor(
                    OUTS[qt][:, :], SGT[bk][:, :], SGT2[qt][:, :], ALU.mult),
                    waits=[(a_sc, SC_EE), (s_so[qt], DC[id(s_so[qt])])])
                VE_P4[(og, oc, qt)] = op("vector", lambda e, qt=qt, bk=bk:
                                         e.tensor_tensor(
                    OUTS[qt][:, :], OUTS[qt][:, :], XRS[bk][:, :], ALU.add),
                    waits=[(s_xr[bk], XRC)])
                last_xrs[bk] = VE_P4[(og, oc, qt)]
                dma("sync", AP(d_out, ocg * 128 * NQ + qt * 512, [[NQ, 128], [1, 512]]),
                    OUTS[qt][:, :], s_so[qt], waits=[(v_ve, VE_P4[(og, oc, qt)])])
        SC_P4og[og] = C["sc"]
    ST0, ST1, STD = DC[id(s_so[0])], DC[id(s_so[1])], DC[id(s_st)]
    Q["sync"].append(lambda e: e.wait_ge(s_so[0], ST0))
    Q["sync"].append(lambda e: e.wait_ge(s_so[1], ST1))
    if STD:
        Q["sync"].append(lambda e: e.wait_ge(s_st, STD))

    with nc.Block() as block:
        @block.sync
        def _(eng):
            for fn in Q["sync"]:
                fn(eng)

        @block.gpsimd
        def _(eng):
            for fn in Q["gpsimd"]:
                fn(eng)

        @block.tensor
        def _(eng):
            for fn in Q["tensor"]:
                fn(eng)

        @block.vector
        def _(eng):
            for fn in Q["vector"]:
                fn(eng)

        @block.scalar
        def _(eng):
            for fn in Q["scalar"]:
                fn(eng)

    return nc


_CACHE = {}


def _get_nc():
    if "nc" not in _CACHE:
        _CACHE["nc"] = build_program()
    return _CACHE["nc"]


def kernel(L0_attn_x, L0_attn_wq_w, L0_attn_wq_b, L0_attn_wk_w, L0_attn_wk_b,
           L0_attn_wv_w, L0_attn_wv_b, L0_attn_wo_w, L0_attn_wo_b,
           L0_ffn_W_base, L0_ffn_b_base, L0_ffn_lora_A, L0_ffn_lora_B,
           L0_ffn_router_W, L0_ffn_router_b, L0_ffn_domain_prior, L0_ffn_active):
    x = np.asarray(L0_attn_x, dtype=np.float32)

    def f32(a):
        return np.ascontiguousarray(np.asarray(a, dtype=np.float32))

    wq, wk, wv = f32(L0_attn_wq_w), f32(L0_attn_wk_w), f32(L0_attn_wv_w)
    wo, Wb = f32(L0_attn_wo_w), f32(L0_ffn_W_base)
    xT = [np.ascontiguousarray(x[b].T) for b in range(2)]
    wqT = np.ascontiguousarray(wq.T)
    wkT = np.ascontiguousarray(wk.T)
    wvT = np.ascontiguousarray(wv.T)
    woT = np.ascontiguousarray(wo.T)
    WbT = np.ascontiguousarray(Wb.T)
    wq_bP = np.ascontiguousarray(f32(L0_attn_wq_b).reshape(NCH, 128).T)
    wo_bP = np.ascontiguousarray(f32(L0_attn_wo_b).reshape(NCH, 128).T)
    bb_P = np.ascontiguousarray(f32(L0_ffn_b_base).reshape(NCH, 128).T)
    wk_bP = f32(L0_attn_wk_b).reshape(128, 1)
    wv_bP = f32(L0_attn_wv_b).reshape(128, 1)
    AT = np.ascontiguousarray(f32(L0_ffn_lora_A).reshape(128, D).T)
    Bt = np.ascontiguousarray(f32(L0_ffn_lora_B).transpose(0, 2, 1).reshape(128, D))
    rT = np.ascontiguousarray(f32(L0_ffn_router_W).T)
    rb_P = f32(L0_ffn_router_b).reshape(16, 1)
    prior_P = f32(L0_ffn_domain_prior).reshape(16, 1)
    act_i = np.ascontiguousarray(np.asarray(L0_ffn_active, dtype=np.int32))
    iota16 = np.arange(16, dtype=np.float32).reshape(16, 1)
    iotaR4 = np.ascontiguousarray(np.tile(np.arange(16, dtype=np.float32), (4, 1)))
    identB = np.eye(128, dtype=np.float32).astype(bf16)
    identF4 = np.eye(4, dtype=np.float32)
    identF16 = np.eye(16, dtype=np.float32)
    R2 = np.zeros((16, 128), dtype=np.float32)
    for e in range(16):
        R2[e, e * 8:(e + 1) * 8] = 2.0
    R2 = R2.astype(bf16)

    tri = (np.arange(128)[:, None] <= np.arange(128)[None, :]).astype(np.float32)
    in_maps = []
    qrows_all = []
    for c in range(8):
        b, cm = c // 4, c % 4
        qrows = np.concatenate(
            [np.arange(128) + (4 * g + cm) * 128 for g in range(NSLOT)])
        qrows_all.append(qrows)
        xqT = np.ascontiguousarray(x[b][qrows].T)
        masks = np.zeros((NSLOT * 4, 128, 512), dtype=np.float32)
        for g in range(NSLOT):
            for j in range(4):
                if j < cm:
                    masks[4 * g + j] = 1.0
                elif j == cm:
                    masks[4 * g + j] = np.tile(tri, (1, 4))
        masks = masks.astype(bf16)
        in_maps.append({
            "xT": xT[b], "xqT": xqT, "masks": masks, "identF16": identF16,
            "wqT": wqT, "wkT": wkT, "wvT": wvT, "woT": woT, "WbT": WbT,
            "wq_bP": wq_bP, "wo_bP": wo_bP, "bb_P": bb_P,
            "wk_bP": wk_bP, "wv_bP": wv_bP,
            "AT": AT, "Bt": Bt, "rT": rT, "rb_P": rb_P, "prior_P": prior_P,
            "act_i": act_i, "iota16": iota16, "iota16b": iota16.astype(bf16), "iotaR4": iotaR4,
            "identB": identB, "identF4": identF4, "R2": R2,
        })

    nc = _get_nc()
    res = run_bass_kernel_spmd(nc, in_maps, core_ids=list(range(8)))
    out = np.empty((2, T, D), dtype=np.float32)
    for c in range(8):
        b = c // 4
        outT = np.asarray(res.results[c]["outT"])
        out[b, qrows_all[c], :] = outT.T
    return out


# version 25
# speedup vs baseline: 1.2177x; 1.2177x over previous
import numpy as np
import ml_dtypes
import concourse.bass as bass
import concourse.mybir as mybir
from concourse.bass_utils import run_bass_kernel_spmd

F32 = mybir.dt.float32
BF16 = mybir.dt.bfloat16
I32 = mybir.dt.int32
AF = mybir.ActivationFunctionType
ALU = mybir.AluOpType
bf16 = ml_dtypes.bfloat16
AP = bass.AP

D = 4096
T = 4096
NCH = 32
NQ = 1024
H = 32
HD = 128
NSLOT = 8
NHG = 8
RSQ = 0.08838834764831845
KB = 1024
DEBUG = False


def build_program():
    nc = bass.Bass("TRN2", target_bir_lowering=False)

    def din(name, shape, dt):
        return nc.dram_tensor(name, shape, dt, kind="ExternalInput")

    d_xT = din("xT", [D, T], F32)
    d_xqT = din("xqT", [D, NQ], F32)
    d_masks = din("masks", [NSLOT * 4, 128, 512], BF16)
    d_wqT = din("wqT", [D, D], F32)
    d_wkT = din("wkT", [D, HD], F32)
    d_wvT = din("wvT", [D, HD], F32)
    d_woT = din("woT", [D, D], F32)
    d_WbT = din("WbT", [D, D], F32)
    d_wqb = din("wq_bP", [128, NCH], F32)
    d_wob = din("wo_bP", [128, NCH], F32)
    d_bbb = din("bb_P", [128, NCH], F32)
    d_wkb = din("wk_bP", [128, 1], F32)
    d_wvb = din("wv_bP", [128, 1], F32)
    d_AT = din("AT", [D, 128], F32)
    d_Bt = din("Bt", [128, D], F32)
    d_rT = din("rT", [D, 16], F32)
    d_rb = din("rb_P", [16, 1], F32)
    d_pr = din("prior_P", [16, 1], F32)
    d_act = din("act_i", [4], I32)
    d_iota = din("iota16", [16, 1], F32)
    d_iotab = din("iota16b", [16, 1], BF16)
    d_iotaR4 = din("iotaR4", [4, 16], F32)
    d_identB = din("identB", [128, 128], BF16)
    d_identF4 = din("identF4", [4, 4], F32)
    d_identF16 = din("identF16", [16, 16], F32)
    d_R2 = din("R2", [16, 128], BF16)
    d_out = nc.dram_tensor("outT", [D, NQ], F32, kind="ExternalOutput")
    if DEBUG:
        d_dbg = {
            "dbg_ft": nc.dram_tensor("dbg_ft", [1, 16], F32, kind="ExternalOutput"),
            "dbg_cb": nc.dram_tensor("dbg_cb", [128, 1], F32, kind="ExternalOutput"),
            "dbg_kT": nc.dram_tensor("dbg_kT", [128, T], BF16, kind="ExternalOutput"),
            "dbg_Vt": nc.dram_tensor("dbg_Vt", [128, T], BF16, kind="ExternalOutput"),
            "dbg_xn": nc.dram_tensor("dbg_xn", [128, NCH * NQ], BF16, kind="ExternalOutput"),
            "dbg_qT": nc.dram_tensor("dbg_qT", [128, NCH * NQ], BF16, kind="ExternalOutput"),
            "dbg_at": nc.dram_tensor("dbg_at", [128, NCH * NQ], BF16, kind="ExternalOutput"),
            "dbg_z": nc.dram_tensor("dbg_z", [128, NQ], BF16, kind="ExternalOutput"),
            "dbg_zp": nc.dram_tensor("dbg_zp", [128, NQ], BF16, kind="ExternalOutput"),
            "dbg_lg": nc.dram_tensor("dbg_lg", [16, NQ], BF16, kind="ExternalOutput"),
            "dbg_alT": nc.dram_tensor("dbg_alT", [4, NQ], BF16, kind="ExternalOutput"),
            "dbg_aE": nc.dram_tensor("dbg_aE", [16, NQ], BF16, kind="ExternalOutput"),
            "dbg_rows": nc.dram_tensor("dbg_rows", [1, 2048], F32, kind="ExternalOutput"),
            "dbg_brow": nc.dram_tensor("dbg_brow", [1, 1024], BF16, kind="ExternalOutput"),
            "dbg_rB": nc.dram_tensor("dbg_rB", [128, 512], BF16, kind="ExternalOutput"),
            "dbg_nB": nc.dram_tensor("dbg_nB", [128, 512], BF16, kind="ExternalOutput"),
            "dbg_wk": nc.dram_tensor("dbg_wk", [128, NCH * HD], BF16, kind="ExternalOutput"),
            "dbg_T1": nc.dram_tensor("dbg_T1", [128, 512], BF16, kind="ExternalOutput"),
            "dbg_prB": nc.dram_tensor("dbg_prB", [128, 512], BF16, kind="ExternalOutput"),
            "dbg_csk": nc.dram_tensor("dbg_csk", [1, 128], BF16, kind="ExternalOutput"),
            "dbg_pre": nc.dram_tensor("dbg_pre", [4, 512], F32, kind="ExternalOutput"),
            "dbg_lp4": nc.dram_tensor("dbg_lp4", [4, 1], F32, kind="ExternalOutput"),
            "dbg_Sf": nc.dram_tensor("dbg_Sf", [16, 4], F32, kind="ExternalOutput"),
            "dbg_e4": nc.dram_tensor("dbg_e4", [128, 4], F32, kind="ExternalOutput"),
            "dbg_lpv": nc.dram_tensor("dbg_lpv", [16, 1], F32, kind="ExternalOutput"),
            "dbg_lp16": nc.dram_tensor("dbg_lp16", [16, 1], BF16, kind="ExternalOutput"),
            "dbg_Se16": nc.dram_tensor("dbg_Se16", [16, 4], BF16, kind="ExternalOutput"),
            "dbg_a4": nc.dram_tensor("dbg_a4", [4, 1], F32, kind="ExternalOutput"),
            "dbg_Se16e": nc.dram_tensor("dbg_Se16e", [16, 4], BF16, kind="ExternalOutput"),
            "dbg_lp16e": nc.dram_tensor("dbg_lp16e", [16, 1], BF16, kind="ExternalOutput"),
            "dbg_can1": nc.dram_tensor("dbg_can1", [128, 512], F32, kind="ExternalOutput"),
            "dbg_can2": nc.dram_tensor("dbg_can2", [128, 512], F32, kind="ExternalOutput"),
        }

    SBB = 16512

    def sb(name, shape, dt, off):
        return nc.alloc_sbuf_tensor_at(name, shape, dt, offset=SBB + off)

    xnbq = sb("xnbq", [128, NCH * NQ], BF16, 0)
    qTat = sb("qTat", [128, NCH * NQ], BF16, 64 * KB)
    kT = sb("kT", [128, T], BF16, 128 * KB)
    Vt = sb("Vt", [128, T], BF16, 136 * KB)
    wkTb = sb("wkTb", [128, NCH * HD], BF16, 144 * KB)
    wvTb = sb("wvTb", [128, NCH * HD], BF16, 152 * KB)
    ATb = sb("ATb", [128, NCH * 128], BF16, 160 * KB)
    XB = [sb("XB0", [128, NCH * 512], BF16, 0),
          sb("XB1", [128, NCH * 512], BF16, 32 * KB)]
    X2R = sb("X2R", [128, 8 * 512], BF16, 64 * KB)
    T1R = sb("T1R", [128, 8 * 512], BF16, 72 * KB)
    p1rstdB = [sb("p1rstdB0", [128, 512], BF16, 190 * KB),
               sb("p1rstdB1", [128, 512], BF16, 193 * KB)]
    p1k2b = sb("p1k2b", [128, 512], BF16, 191 * KB)
    p1VTs = sb("p1VTs", [128, 512], BF16, 192 * KB)
    p1rows = [sb("p1rows0", [1, 2048], F32, 198 * KB),
              (sb("p1r1a", [1, 512], F32, 181 * KB),
               sb("p1r1b", [1, 512], F32, 183 * KB),
               sb("p1r1c", [1, 512], F32, 185 * KB),
               sb("p1r1d", [1, 512], F32, 179 * KB))]
    p1brow = [sb("p1brow0", [1, 1024], BF16, 177 * KB),
              sb("p1brow1", [1, 1024], BF16, 187 * KB)]
    XQB = sb("XQB", [128, NCH * 512], BF16, 64 * KB)
    X2F2 = sb("X2F2", [128, NCH * 512], BF16, 96 * KB)
    p2rstdB = sb("p2rstdB", [128, 512], BF16, 190 * KB)
    p2nmrB = sb("p2nmrB", [128, 512], BF16, 191 * KB)
    p2rows = sb("p2rows", [1, 2048], F32, 198 * KB)
    p2brow = sb("p2brow", [1, 1024], BF16, 179 * KB)
    WQP = [sb("WQP0", [128, NCH * 256], BF16, 144 * KB),
           sb("WQP1", [128, NCH * 256], BF16, 160 * KB)]
    WOP = [sb("WOP0", [128, NCH * 256], BF16, 128 * KB),
           sb("WOP1", [128, NCH * 256], BF16, 144 * KB)]
    WBP = sb("WBP", [128, NCH * 256 + 256], BF16, 160 * KB)
    rTb = sb("rTb", [128, NCH * 16], BF16, 176 * KB)
    z16 = sb("z16", [128, NQ], BF16, 177 * KB)
    zp16 = sb("zp16", [128, NQ], BF16, 179 * KB)
    lg16 = sb("lg16", [16, NQ], BF16, 181 * KB)
    alphaT16 = sb("alphaT16", [4, NQ], BF16, 183 * KB)
    aE16 = sb("aE16", [16, NQ], BF16, 185 * KB)
    aB16 = sb("aB16", [128, 512], BF16, 187 * KB)
    pre_f = sb("pre_f", [4, 512], F32, 188 * KB)
    MB = [sb("MB0", [128, 4 * 512], BF16, 181 * KB),
          sb("MB1", [128, 4 * 512], BF16, 185 * KB)]
    XRS = [sb("XRS0", [128, 512], F32, 181 * KB),
           sb("XRS1", [128, 512], F32, 183 * KB),
           sb("XRS2", [128, 512], F32, 198 * KB),
           sb("XRS3", [128, 512], F32, 200 * KB)]
    OUTS = [sb("OUTS0", [128, 512], F32, 185 * KB),
            sb("OUTS1", [128, 512], F32, 187 * KB)]
    Pbp = [sb("Pbp0", [128, 1024], BF16, 190 * KB),
           sb("Pbp1", [128, 1024], BF16, 192 * KB),
           sb("Pbp2", [128, 1024], BF16, 146 * KB),
           sb("Pbp3", [128, 1024], BF16, 148 * KB)]
    q2b = sb("q2b", [128, NQ], BF16, 190 * KB)
    SGT = [sb("SGT0", [128, 512], BF16, 190 * KB), sb("SGT1", [128, 512], BF16, 191 * KB),
           sb("SGT2", [128, 512], BF16, 192 * KB), sb("SGT3", [128, 512], BF16, 193 * KB)]
    SGT2 = [sb("SGT2a", [128, 512], BF16, 189 * KB),
            sb("SGT2b", [128, 512], BF16, 177 * KB)]
    b0 = 194 * KB
    cbias = sb("cbias", [128, 1], F32, b0 + 0)
    escale = sb("escale", [128, 1], F32, b0 + 32)
    wkb = sb("wkb", [128, 1], F32, b0 + 64)
    wvb = sb("wvb", [128, 1], F32, b0 + 96)
    ones_col = sb("ones_col", [128, 1], BF16, b0 + 128)
    mxc = sb("mxc", [128, 1], F32, b0 + 160)
    nmxc = sb("nmxc", [128, 1], F32, b0 + 192)
    s4c = sb("s4c", [128, 1], F32, b0 + 224)
    rsc = sb("rsc", [128, 1], F32, b0 + 256)
    e4t = sb("e4t", [128, 4], F32, b0 + 288)
    alpha16 = sb("alpha16", [128, 4], BF16, b0 + 320)
    wq_bP = sb("wq_bPs", [128, NCH], F32, b0 + 384)
    wo_bP = sb("wo_bPs", [128, NCH], F32, b0 + 512)
    bb_P = sb("bb_Ps", [128, NCH], F32, b0 + 640)
    identB = sb("identBs", [128, 128], BF16, b0 + 768)
    b2 = 196 * KB
    ones_row = sb("ones_row", [1, 128], BF16, b2 + 0)
    csk16 = sb("csk16", [1, 128], BF16, b2 + 256)
    csv16 = sb("csv16", [1, 128], BF16, b2 + 512)
    identF4 = sb("identF4s", [4, 4], F32, b2 + 768)
    identF16 = sb("identF16s", [16, 16], F32, 206 * KB)
    iotaf = sb("iotaf", [16, 1], F32, b2 + 832)
    acts16 = sb("acts16", [16, 4], I32, b2 + 864)
    S_f = sb("S_f", [16, 4], F32, b2 + 896)
    S_e16 = sb("S_e16", [16, 4], BF16, b2 + 928)
    S_fT = sb("S_fT", [4, 16], F32, b2 + 960)
    S_eT16 = sb("S_eT16", [4, 16], BF16, b2 + 1024)
    iotaR4 = sb("iotaR4s", [4, 16], F32, b2 + 1056)
    R2b = sb("R2b", [16, 128], BF16, b2 + 1152)
    lpv = sb("lpv", [16, 1], F32, b2 + 1408)
    lp16 = sb("lp16", [16, 1], BF16, b2 + 1440)
    lp4 = sb("lp4", [4, 1], F32, b2 + 1472)
    actf4 = sb("actf4", [4, 1], F32, b2 + 1504)
    rbv = sb("rbv", [16, 1], F32, b2 + 1536)
    prv = sb("prv", [16, 1], F32, b2 + 1568)
    ftiny = sb("ftiny", [1, 16], F32, b2 + 1600)
    krow = sb("krow", [1, 8], F32, b2 + 1920)
    qrow = sb("qrow", [1, 64], F32, 207 * KB)
    btiny = sb("btiny", [1, 16], BF16, b2 + 1664)
    iota16b = sb("iota16b", [16, 1], BF16, b2 + 1696)
    eps12c = sb("eps12c", [16, 1], F32, b2 + 1792)
    neg1c = sb("neg1c", [128, 1], F32, b2 + 1856)
    canary = sb("canary", [128, 512], F32, 196 * KB)
    rz16 = sb("rz16", [1, 512], BF16, 198 * KB)
    rlo16 = sb("rlo16", [1, 512], BF16, 199 * KB)
    rzf = sb("rzf", [1, 512], F32, 200 * KB)
    rzh = sb("rzh", [1, 512], F32, 202 * KB)
    rz16b = sb("rz16b", [1, 512], BF16, 158 * KB)
    rzbS2 = sb("rzbS2", [128, 512], F32, 154 * KB)
    lnzr = sb("lnzr", [1, 512], F32, 204 * KB)
    rzbS = sb("rzbS", [128, 512], F32, 144 * KB)
    EPS5, EPS12, NEGH, POSH, NEG1, KM2, QM2, TSC, CSC = range(9)

    pA = [nc.alloc_psum_tensor(f"pA{i}", [128, 1024], F32) for i in range(4)]

    def pb_view(i):
        return pA[i // 2].ap()[:, (i % 2) * 512:(i % 2) * 512 + 512]
    pb = [pb_view(i) for i in range(8)]

    def pbf(i):
        return pA[i // 2].ap().bitcast(BF16)[:, (i % 2) * 1024:(i % 2) * 1024 + 1024]

    s_p0 = nc.alloc_semaphore("s_p0")
    s_ld = nc.alloc_semaphore("s_ld")
    s_xb = [nc.alloc_semaphore("s_xb0"), nc.alloc_semaphore("s_xb1")]
    s_xq = [nc.alloc_semaphore("s_xq0"), nc.alloc_semaphore("s_xq1")]
    s_wq = [nc.alloc_semaphore("s_wq0"), nc.alloc_semaphore("s_wq1")]
    s_wo = [nc.alloc_semaphore("s_wo0"), nc.alloc_semaphore("s_wo1")]
    s_wb = nc.alloc_semaphore("s_wb")
    s_mk = [nc.alloc_semaphore("s_mk0"), nc.alloc_semaphore("s_mk1")]
    s_xr = [nc.alloc_semaphore(f"s_xr{i}") for i in range(4)]
    s_st = nc.alloc_semaphore("s_st")
    s_so = [nc.alloc_semaphore("s_so0"), nc.alloc_semaphore("s_so1")]
    m_pe = nc.alloc_semaphore("m_pe")
    m_gp = nc.alloc_semaphore("m_gp")
    m_pn = nc.alloc_semaphore("m_pn")
    m_sn = nc.alloc_semaphore("m_sn")
    m_vn = nc.alloc_semaphore("m_vn")
    a_sc = nc.alloc_semaphore("a_sc")
    v_ve = nc.alloc_semaphore("v_ve")

    Q = {"sync": [], "gpsimd": [], "tensor": [], "vector": [], "scalar": []}
    C = {"pe": 0, "sc": 0, "ve": 0, "gp": 0, "pn": 0, "sn": 0, "vn": 0}
    DC = {id(s): 0 for s in [s_p0, s_ld, s_xb[0], s_xb[1], s_xq[0], s_xq[1],
                             s_wq[0], s_wq[1], s_wo[0], s_wo[1], s_wb,
                             s_mk[0], s_mk[1], s_xr[0], s_xr[1], s_xr[2], s_xr[3],
                             s_st, s_so[0], s_so[1]]}
    SEMOBJ = {"pe": m_pe, "sc": a_sc, "ve": v_ve, "gp": m_gp, "pn": m_pn, "sn": m_sn, "vn": m_vn}

    ENG2C = {"vector": "ve", "scalar": "sc", "gpsimd": "gp"}

    def op(eng, build, waits=(), inc=None):
        w = [(s, v) for (s, v) in waits if v > 0]
        auto = ENG2C.get(eng)
        if auto is not None:
            if C[auto] > 0:
                w.append((SEMOBJ[auto], C[auto]))
            C[auto] += 1
            tgt = SEMOBJ[auto]
        elif inc:
            C[inc] += 1
            tgt = SEMOBJ[inc]
        else:
            tgt = None
        w = tuple(w)

        def fn(engine, w=w, build=build, tgt=tgt):
            for (s, v) in w:
                engine.wait_ge(s, v)
            ins = build(engine)
            if tgt is not None and ins is not None:
                ins.then_inc(tgt, 1)
        Q[eng].append(fn)
        return C[auto] if auto is not None else (C[inc] if inc else None)

    def dma(eng, out_ap, in_ap, sem, waits=()):
        DC[id(sem)] += 16
        w = tuple((s, v) for (s, v) in waits if v > 0)

        def fn(engine, w=w, out_ap=out_ap, in_ap=in_ap, sem=sem):
            for (s, v) in w:
                engine.wait_ge(s, v)
            engine.dma_start(out=out_ap, in_=in_ap).then_inc(sem, 16)
        Q[eng].append(fn)
        return DC[id(sem)]

    for (t_, s_) in ((wq_bP, d_wqb), (wo_bP, d_wob), (bb_P, d_bbb), (wkb, d_wkb),
                     (wvb, d_wvb), (identB, d_identB), (identF4, d_identF4), (identF16, d_identF16),
                     (iotaf, d_iota), (iota16b, d_iotab), (iotaR4, d_iotaR4), (R2b, d_R2),
                     (rbv, d_rb), (prv, d_pr)):
        dma("sync", t_[:, :], s_.ap()[:, :], s_ld)
    dma("sync", acts16[:, :], AP(d_act, 0, [[0, 16], [1, 4]]), s_ld)
    LD0 = DC[id(s_ld)]
    for (dst, src, width) in ((wkTb, d_wkT, HD), (wvTb, d_wvT, HD),
                              (ATb, d_AT, 128), (rTb, d_rT, 16)):
        for h4 in range(4):
            dma("gpsimd", dst[:, h4 * 8 * width:(h4 + 1) * 8 * width],
                AP(src, h4 * 8 * 128 * width,
                   [[width, 128], [128 * width, 8], [1, width]]), s_p0)
    P0C = DC[id(s_p0)]

    def vset(ap_, val):
        op("vector", lambda e, a=ap_, v=val: e.memset(a, v))
    vset(ones_col[:, :], 1.0)
    vset(ones_row[:, :], 1.0)
    vset(escale[:, :], RSQ)
    vset(ftiny[:, EPS5:EPS5 + 1], 1e-5)
    vset(ftiny[:, EPS12:EPS12 + 1], 1e-12)
    vset(eps12c[:, :], 1e-12)
    vset(neg1c[:, :], -1.0)
    vset(ftiny[:, NEGH:NEGH + 1], -0.5)
    vset(ftiny[:, POSH:POSH + 1], 0.5)
    vset(ftiny[:, NEG1:NEG1 + 1], -1.0)
    vset(ftiny[:, KM2:KM2 + 1], 0.0)
    VE_CONST = op("vector", lambda e: e.memset(ftiny[:, QM2:QM2 + 1], 0.0), inc="ve")
    op("vector", lambda e: e.tensor_scalar(S_f[:, :], acts16[:, :], iotaf[:, :],
                                           None, ALU.is_equal), waits=[(s_ld, LD0)])
    VE_SE = op("vector", lambda e: e.tensor_scalar(S_e16[:, :], acts16[:, :],
                                                   iotaf[:, :], None, ALU.is_equal),
               inc="ve")
    if DEBUG:
        dma("sync", d_dbg["dbg_Se16e"].ap()[:, :], S_e16[:, :], s_st,
            waits=[(v_ve, VE_SE)])
        dma("sync", d_dbg["dbg_can1"].ap()[:, :], canary[:, :], s_st)
    for dc in range(NCH):
        op("tensor", lambda e, dc=dc: e.matmul(
            pb[6][0:1, 0:128], ones_col[:, :], wkTb[:, dc * HD:(dc + 1) * HD],
            start=(dc == 0), stop=(dc == NCH - 1)),
           waits=[(s_p0, P0C), (v_ve, VE_CONST)] if dc == 0 else ())
    for dc in range(NCH):
        PE_CS = op("tensor", lambda e, dc=dc: e.matmul(
            pb[7][0:1, 0:128], ones_col[:, :], wvTb[:, dc * HD:(dc + 1) * HD],
            start=(dc == 0), stop=(dc == NCH - 1)),
            inc="pe" if dc == NCH - 1 else None)
    op("vector", lambda e: e.tensor_copy(csk16[:, :], pb[6][0:1, 0:128]),
       waits=[(m_pe, PE_CS)])
    VE_CS = op("vector", lambda e: e.tensor_copy(csv16[:, :], pb[7][0:1, 0:128]),
               inc="ve")

    TRK = {"stats_pe": 0, "sq_ve": {}, "kv_pe": 0, "t1_ve": {}, "tr_pe": {},
           "tr_ve": {-1: VE_CS, 3: VE_CS}, "rb_ve": 0, "nb_ve": 0,
           "kv_ve": 0, "k2_ve": VE_CS, "x2_pe": 0, "rstd_sc": {},
           "x2mark": {}, "kvmark": {}, "sq_sc": {}, "sqsc": 0}
    TRK["stats_ve"] = {}

    def rowslots(rows):
        if isinstance(rows, tuple):
            return tuple(r[:, :] for r in rows)
        return (rows[:, 0:512], rows[:, 512:1024], rows[:, 1024:1536],
                rows[:, 1536:2048])

    def ln_statsA_p1(t, XBuf, sxb_sem, sxb_cnt, first=False, banks=(2, 3)):
        bx, bx2 = banks
        for dc in range(NCH):
            w = []
            if dc == 0:
                w = [(sxb_sem, sxb_cnt), (v_ve, TRK["stats_ve"].get(banks, 0))]
                if first:
                    w.append((v_ve, VE_CONST))
            op("tensor", lambda e, dc=dc, bx=bx: e.matmul(
                pb[bx][0:1, :], ones_col[:, :], XBuf[:, dc * 512:(dc + 1) * 512],
                start=(dc == 0), stop=(dc == NCH - 1)), waits=w)
        def emit_sq(dc):
            gdc = t * NCH + dc
            sl = (gdc % 8) * 512
            wv = []
            if dc == 0:
                wv = [(sxb_sem, sxb_cnt)]
            if gdc >= 8:
                wv.append((m_pe, TRK["x2mark"].get((gdc - 8 + 3) // 4, 0)))
            op("vector", lambda e, dc=dc, sl=sl: e.tensor_tensor(
                X2R[:, sl:sl + 512], XBuf[:, dc * 512:(dc + 1) * 512],
                XBuf[:, dc * 512:(dc + 1) * 512], ALU.mult), waits=wv)
            TRK["sq_sc"][dc] = C["ve"]
        emit_sq(0)
        for dc in range(NCH):
            gdc = t * NCH + dc
            sl = (gdc % 8) * 512
            if dc + 1 < NCH:
                emit_sq(dc + 1)
            wv2 = [(v_ve, TRK["sq_sc"][min(dc + 1, NCH - 1)])]
            ii = op("tensor", lambda e, dc=dc, sl=sl, bx2=bx2: e.matmul(
                pb[bx2][0:1, :], ones_col[:, :], X2R[:, sl:sl + 512],
                start=(dc == 0), stop=(dc == NCH - 1)),
                waits=wv2, inc="pe" if dc % 4 == 3 else None)
            if dc % 4 == 3:
                TRK["x2mark"][gdc // 4] = ii
                TRK["x2_pe"] = ii
        return TRK["x2_pe"]

    def ln_statsA(XBuf, X2Buf, sxb_sem, sxb_cnt, first=False, banks=(2, 3)):
        bx, bx2 = banks
        for dc in range(NCH):
            wv = []
            if dc == 0:
                wv = [(sxb_sem, sxb_cnt), (m_pe, TRK["x2_pe"])]
            TRK["sq_ve"][dc] = op("vector", lambda e, dc=dc: e.tensor_tensor(
                X2Buf[:, dc * 512:(dc + 1) * 512], XBuf[:, dc * 512:(dc + 1) * 512],
                XBuf[:, dc * 512:(dc + 1) * 512], ALU.mult), waits=wv)
        for dc in range(NCH):
            w = []
            if dc == 0:
                w = [(sxb_sem, sxb_cnt), (v_ve, TRK["stats_ve"].get(banks, 0))]
                if first:
                    w.append((v_ve, VE_CONST))
            op("tensor", lambda e, dc=dc, bx=bx: e.matmul(
                pb[bx][0:1, :], ones_col[:, :], XBuf[:, dc * 512:(dc + 1) * 512],
                start=(dc == 0), stop=(dc == NCH - 1)), waits=w)
        for dc in range(NCH):
            wv = []
            if dc % 4 == 0:
                wv = [(v_ve, TRK["sq_ve"][min(dc + 3, NCH - 1)])]
            TRK["x2_pe"] = op("tensor", lambda e, dc=dc, bx2=bx2: e.matmul(
                pb[bx2][0:1, :], ones_col[:, :], X2Buf[:, dc * 512:(dc + 1) * 512],
                start=(dc == 0), stop=(dc == NCH - 1)),
                waits=wv, inc="pe")
        return TRK["x2_pe"]

    def ln_statsB(PE_STATS, rows, brow, rstdB_t, nmrB_t, kv, banks=(2, 3)):
        bx, bx2 = banks
        mS, e2S, varr, scr = rowslots(rows)
        rstd16, nmr16 = brow[:, 0:512], brow[:, 512:1024]
        VE_MS = op("vector", lambda e, bx=bx: e.tensor_scalar(
            mS, pb[bx][0:1, :], 1.0 / D, None, ALU.mult),
           waits=[(m_pe, PE_STATS), (a_sc, TRK["rstd_sc"].get("last", 0))])
        op("vector", lambda e, bx2=bx2: e.tensor_scalar(
            e2S, pb[bx2][0:1, :], 1.0 / D, None, ALU.mult))
        op("vector", lambda e: e.tensor_tensor(varr, mS, mS, ALU.mult))
        VE_VAR = op("vector", lambda e: e.tensor_tensor(varr, e2S, varr, ALU.subtract))
        TRK["stats_ve"][banks] = VE_VAR
        op("scalar", lambda e: e.activation(scr, varr, AF.Ln,
                                            bias=ftiny[:, EPS5:EPS5 + 1]),
           waits=[(v_ve, VE_VAR)])
        SC_R16 = op("scalar", lambda e: e.activation(
            rstd16, scr, AF.Exp, scale=ftiny[:, NEGH:NEGH + 1]), inc="sc")
        SC_RF = op("scalar", lambda e: e.activation(
            varr, scr, AF.Exp, scale=ftiny[:, NEGH:NEGH + 1]), inc="sc")
        TRK["rstd_sc"]["last"] = SC_RF
        op("vector", lambda e: e.tensor_tensor(scr, mS, varr, ALU.mult),
           waits=[(a_sc, SC_RF)])
        VE_NMR = op("vector", lambda e: e.tensor_scalar(nmr16, scr, -1.0, None,
                                                        ALU.mult))
        PE_BC = op("tensor", lambda e: e.matmul(
            pb[4][:, :], ones_row[:, :], rstd16, start=True, stop=True),
            waits=[(a_sc, SC_R16), (v_ve, TRK["rb_ve"]), (v_ve, TRK["k2_ve"])],
            inc="pe")
        VE_RB = op("vector", lambda e, r=rstdB_t: e.tensor_copy(r[:, :], pb[4][:, :]),
                   waits=[(m_pe, PE_BC)])
        TRK["rb_ve"] = VE_RB
        if not kv:
            PE_BC2 = op("tensor", lambda e: e.matmul(
                pb[5][:, :], ones_row[:, :], nmr16, start=True, stop=True),
                waits=[(v_ve, VE_NMR), (v_ve, TRK["nb_ve"])], inc="pe")
            VE_NB = op("vector", lambda e, r=nmrB_t: e.tensor_copy(r[:, :], pb[5][:, :]),
                       waits=[(m_pe, PE_BC2)])
            TRK["nb_ve"] = VE_NB
        return VE_NMR, VE_RB

    def ln_kv(t, XBuf, rstdB_t, nmr16, VE_NMR, VE_RB, first):
        def emit_t1(dc):
            gdc = t * NCH + dc
            sl = (gdc % 8) * 512
            wv = [(v_ve, VE_RB)] if dc == 0 else []
            if gdc >= 8:
                wv.append((m_pe, TRK["kvmark"].get((gdc - 8 + 3) // 4, 0)))
            TRK["t1_ve"][dc] = op("vector", lambda e, dc=dc, sl=sl: e.tensor_tensor(
                T1R[:, sl:sl + 512], XBuf[:, dc * 512:(dc + 1) * 512],
                rstdB_t[:, :], ALU.mult), waits=wv)
        emit_t1(0)
        for dc in range(NCH):
            gdc = t * NCH + dc
            sl = (gdc % 8) * 512
            if dc + 1 < NCH:
                emit_t1(dc + 1)
            w = [(v_ve, TRK["t1_ve"][min(dc + 1, NCH - 1)])]
            if dc == 0 and first:
                w += [(v_ve, VE_CS), (s_p0, P0C)]
            op("tensor", lambda e, dc=dc, sl=sl: e.matmul(
                pb[0][:, :], wkTb[:, dc * HD:(dc + 1) * HD],
                T1R[:, sl:sl + 512],
                start=(dc == 0), stop=False), waits=w)
            ii = op("tensor", lambda e, dc=dc, sl=sl: e.matmul(
                pb[1][:, :], wvTb[:, dc * HD:(dc + 1) * HD],
                T1R[:, sl:sl + 512],
                start=(dc == 0), stop=False),
                inc="pe" if dc % 4 == 3 else None)
            if dc % 4 == 3:
                TRK["kvmark"][gdc // 4] = ii
        op("tensor", lambda e, nmr16=nmr16: e.matmul(pb[0][:, :], csk16[:, :],
                                                     nmr16, start=False, stop=True),
           waits=[(v_ve, VE_NMR)])
        PE_KV = op("tensor", lambda e, nmr16=nmr16: e.matmul(
            pb[1][:, :], csv16[:, :], nmr16, start=False, stop=True), inc="pe")
        TRK["kv_pe"] = PE_KV
        op("vector", lambda e, t=t: e.tensor_scalar(
            kT[:, t * 512:(t + 1) * 512], pb[0][:, :], wkb[:, :], None, ALU.add),
           waits=[(m_pe, PE_KV)])
        VE_KV = op("vector", lambda e: e.tensor_scalar(
            p1VTs[:, :], pb[1][:, :], wvb[:, :], None, ALU.add))
        TRK["kv_ve"] = VE_KV
        for j in range(4):
            TRK["tr_pe"][j] = op("tensor", lambda e, j=j: e.transpose(
                pbf(6)[:, 0:128], p1VTs[:, j * 128:(j + 1) * 128], identB[:, :]),
                waits=[(v_ve, VE_KV), (v_ve, TRK["tr_ve"][(j - 1) % 4])], inc="pe")
            TRK["tr_ve"][j] = op("vector", lambda e, t=t, j=j: e.tensor_copy(
                Vt[:, (4 * t + j) * 128:(4 * t + j + 1) * 128], pbf(6)[:, 0:128]),
                waits=[(m_pe, TRK["tr_pe"][j])])
        VE_K2B = op("vector", lambda e, t=t: e.tensor_tensor(
            p1k2b[:, :], kT[:, t * 512:(t + 1) * 512],
            kT[:, t * 512:(t + 1) * 512], ALU.mult))
        PE_K2 = op("tensor", lambda e: e.matmul(
            pb[4][0:1, :], ones_col[:, :], p1k2b[:, :], start=True, stop=True),
            waits=[(v_ve, VE_K2B), (v_ve, TRK["rb_ve"])], inc="pe")
        TRK["k2_ve"] = op("vector", lambda e, t=t: e.reduce_max(
            krow[:, t:t + 1], pb[4][0:1, :], axis=mybir.AxisListType.X),
            waits=[(m_pe, PE_K2)])

    def ln_tile(tile, XBuf, X2Buf, sxb_sem, sxb_cnt, rstdB_t, nmrB_t,
                rows, brow, kv, first):
        t = tile
        PE_STATS = ln_statsA(XBuf, X2Buf, sxb_sem, sxb_cnt, first)
        ln_statsB(PE_STATS, rows, brow, rstdB_t, nmrB_t, kv)
        for dc in range(NCH):
            op("vector", lambda e, dc=dc, t=t: e.tensor_tensor(
                xnbq[:, dc * NQ + t * 512: dc * NQ + t * 512 + 512],
                XBuf[:, dc * 512:(dc + 1) * 512], rstdB_t[:, :], ALU.mult))
            op("vector", lambda e, dc=dc, t=t: e.tensor_tensor(
                xnbq[:, dc * NQ + t * 512: dc * NQ + t * 512 + 512],
                xnbq[:, dc * NQ + t * 512: dc * NQ + t * 512 + 512],
                nmrB_t[:, :], ALU.add))

    VE_XB_DONE = [0, 0]
    pestats = {}
    bctx = {}
    for t in range(9):
        if t < 8:
            par = t % 2
            w = [(v_ve, VE_XB_DONE[par])] if t >= 2 else ()
            for dc in range(NCH):
                dma("gpsimd", XB[par][:, dc * 512:(dc + 1) * 512],
                    AP(d_xT, dc * 128 * T + t * 512, [[T, 128], [1, 512]]),
                    s_xb[par], waits=w if dc == 0 else ())
            pestats[t] = ln_statsA_p1(t, XB[par], s_xb[par], DC[id(s_xb[par])],
                                      first=(t == 0),
                                      banks=(2, 3) if par == 0 else (5, 7))
        if t >= 1:
            tp = t - 1
            parp = tp % 2
            VE_NMR, VE_RB = bctx[tp]
            ln_kv(tp, XB[parp], p1rstdB[parp], p1brow[parp][:, 512:1024],
                  VE_NMR, VE_RB, first=(tp == 0))
            VE_XB_DONE[parp] = C["ve"]
        if t < 8:
            bctx[t] = ln_statsB(pestats[t], p1rows[par], p1brow[par],
                                p1rstdB[par], None, kv=True,
                                banks=(2, 3) if par == 0 else (5, 7))
    P1_PE, P1_SC, P1_VE = C["pe"], C["sc"], C["ve"]

    VE_XQ_DONE = 0
    for t in range(2):
        par = t % 2
        for dc in range(NCH):
            w = []
            if dc == 0:
                w = ([(v_ve, P1_VE), (a_sc, P1_SC)] if t == 0
                     else [(v_ve, VE_XQ_DONE)])
                if t == 0 and DEBUG:
                    w.append((s_st, DBG_P1))
            dma("gpsimd", XQB[:, dc * 512:(dc + 1) * 512],
                AP(d_xqT, dc * 128 * NQ + t * 512, [[NQ, 128], [1, 512]]),
                s_xq[par], waits=w)
        ln_tile(t, XQB, X2F2, s_xq[par], DC[id(s_xq[par])], p2rstdB, p2nmrB,
                p2rows, p2brow, kv=False, first=False)
        VE_XQ_DONE = C["ve"]
    P2A_VE = C["ve"]

    VE_Z = {}
    for qt in range(2):
        for dc in range(NCH):
            w = []
            if dc == 0:
                w = [(v_ve, max(P2A_VE, P1_VE) if qt == 0 else VE_Z[0])]
            op("tensor", lambda e, qt=qt, dc=dc: e.matmul(
                pb[0][:, :], ATb[:, dc * 128:(dc + 1) * 128],
                xnbq[:, dc * NQ + qt * 512: dc * NQ + qt * 512 + 512],
                start=(dc == 0), stop=(dc == NCH - 1)),
               waits=w, inc="pe" if dc == NCH - 1 else None)
        PE_Z = C["pe"]
        VE_Z[qt] = op("vector", lambda e, qt=qt: e.tensor_copy(
            z16[:, qt * 512:(qt + 1) * 512], pb[0][:, :]),
            waits=[(m_pe, PE_Z)], inc="ve")
    SC_LGv = {}
    for qt in range(2):
        for dc in range(NCH):
            w = []
            if dc == 0 and qt == 1:
                w = [(v_ve, SC_LGv[0])]
            op("tensor", lambda e, qt=qt, dc=dc: e.matmul(
                pb[1][0:16, :], rTb[:, dc * 16:(dc + 1) * 16],
                xnbq[:, dc * NQ + qt * 512: dc * NQ + qt * 512 + 512],
                start=(dc == 0), stop=(dc == NCH - 1)),
               waits=w, inc="pe" if dc == NCH - 1 else None)
        PE_R = C["pe"]
        SC_LGv[qt] = op("vector", lambda e, qt=qt: e.tensor_scalar(
            lg16[0:16, qt * 512:(qt + 1) * 512], pb[1][0:16, :], rbv[:, :],
            None, ALU.add), waits=[(m_pe, PE_R), (s_ld, LD0)], inc="ve")
    VE_LG = SC_LGv[1]
    SC_LP = op("scalar", lambda e: e.activation(lp16[:, :], prv[:, :], AF.Ln,
                                                bias=eps12c[:, :]),
               waits=[(s_ld, LD0), (v_ve, VE_CONST)], inc="sc")
    PE_SEL = op("tensor", lambda e: e.matmul(pb[2][0:4, 0:1], S_e16[:, :],
                                             lp16[:, :], start=True, stop=True),
                waits=[(a_sc, SC_LP), (v_ve, VE_SE)], inc="pe")
    VE_A4 = op("vector", lambda e: e.tensor_copy(lp4[:, :], pb[2][0:4, 0:1]),
               waits=[(m_pe, PE_SEL)], inc="ve")
    PE_SFT = op("tensor", lambda e: e.transpose(pb[5][0:4, 0:16], S_f[:, :],
                                                identF16[:, :]),
                waits=[(v_ve, VE_SE)], inc="pe")
    VE_SET = op("vector", lambda e: e.tensor_copy(S_eT16[:, :], pb[5][0:4, 0:16]),
                waits=[(m_pe, PE_SFT)], inc="ve")
    VE_PRE_prev = 0
    SC_E4_prev = 0
    VE_ALT_prev = 0
    for qt in range(2):
        PE_SL = op("tensor", lambda e, qt=qt: e.matmul(
            pb[4][0:4, :], S_e16[:, :], lg16[0:16, qt * 512:(qt + 1) * 512],
            start=True, stop=True),
            waits=[(v_ve, VE_LG), (v_ve, VE_SET), (v_ve, VE_PRE_prev)], inc="pe")
        VE_PRE = op("vector", lambda e: e.tensor_scalar(
            pre_f[:, :], pb[4][0:4, :], lp4[:, :], None, ALU.add),
            waits=[(m_pe, PE_SL)], inc="ve")
        VE_PRE_prev = VE_PRE
        for sx in range(4):
            PE_T1 = op("tensor", lambda e, sx=sx: e.transpose(
                pb[5][0:128, 0:4], pre_f[:, sx * 128:(sx + 1) * 128], identF4[:, :]),
                waits=[(v_ve, VE_PRE), (a_sc, SC_E4_prev)], inc="pe")
            VE_MX = op("vector", lambda e: e.reduce_max(
                nmxc[:, :], pb[5][0:128, 0:4], axis=mybir.AxisListType.X,
                negate=True), waits=[(m_pe, PE_T1)], inc="ve")
            SC_E4 = op("scalar", lambda e: e.activation(
                e4t[:, :], pb[5][0:128, 0:4], AF.Exp, bias=nmxc[:, :]),
                waits=[(v_ve, VE_MX)], inc="sc")
            SC_E4_prev = SC_E4
            VE_S4 = op("vector", lambda e: e.reduce_sum(
                s4c[:, :], e4t[:, :], axis=mybir.AxisListType.X),
                waits=[(a_sc, SC_E4)], inc="ve")
            op("scalar", lambda e: e.activation(mxc[:, :], s4c[:, :], AF.Ln),
               waits=[(v_ve, VE_S4)])
            SC_RS = op("scalar", lambda e: e.activation(
                rsc[:, :], mxc[:, :], AF.Exp, scale=neg1c[:, :]), inc="sc")
            VE_AL = op("vector", lambda e: e.tensor_scalar(
                alpha16[:, :], e4t[:, :], rsc[:, :], None, ALU.mult),
                waits=[(a_sc, SC_RS)], inc="ve")
            PE_T2 = op("tensor", lambda e: e.transpose(
                pbf(6)[0:4, 0:128], alpha16[:, :], identB[:, :]),
                waits=[(v_ve, VE_AL), (v_ve, VE_ALT_prev)], inc="pe")
            VE_ALT_prev = op("vector", lambda e, qt=qt, sx=sx: e.tensor_copy(
                alphaT16[:, qt * 512 + sx * 128: qt * 512 + sx * 128 + 128],
                pbf(6)[0:4, 0:128]), waits=[(m_pe, PE_T2)], inc="ve")
    VE_ALT = VE_ALT_prev
    VE_AE = {}
    for qt in range(2):
        PE_AE = op("tensor", lambda e, qt=qt: e.matmul(
            pb[7][0:16, :], S_eT16[:, :], alphaT16[:, qt * 512:(qt + 1) * 512],
            start=True, stop=True),
            waits=[(v_ve, VE_ALT if qt == 0 else VE_AE[0]),
                   (v_ve, TRK["k2_ve"])], inc="pe")
        VE_AE[qt] = op("vector", lambda e, qt=qt: e.tensor_copy(
            aE16[0:16, qt * 512:(qt + 1) * 512], pb[7][0:16, :]),
            waits=[(m_pe, PE_AE)], inc="ve")
    VE_ZP = 0
    VE_AB_prev = 0
    for qt in range(2):
        PE_AB = op("tensor", lambda e, qt=qt: e.matmul(
            pb[0][:, :], R2b[:, :], aE16[0:16, qt * 512:(qt + 1) * 512],
            start=True, stop=True),
            waits=[(v_ve, VE_AE[1]), (v_ve, VE_AB_prev)], inc="pe")
        VE_AB = op("vector", lambda e: e.tensor_copy(aB16[:, :], pb[0][:, :]),
                   waits=[(m_pe, PE_AB)], inc="ve")
        VE_AB_prev = VE_AB
        VE_ZP = op("vector", lambda e, qt=qt: e.tensor_tensor(
            zp16[:, qt * 512:(qt + 1) * 512], z16[:, qt * 512:(qt + 1) * 512],
            aB16[:, :], ALU.mult), waits=[(v_ve, VE_AB)], inc="ve")
    P2B_PE = C["pe"]

    if DEBUG:
        dma("sync", d_dbg["dbg_rows"].ap()[:, :], p2rows[:, :], s_st, waits=[(v_ve, VE_ZP)])
        dma("sync", d_dbg["dbg_brow"].ap()[:, :], p2brow[:, :], s_st)
        dma("sync", d_dbg["dbg_rB"].ap()[:, :], p2rstdB[:, :], s_st)
        dma("sync", d_dbg["dbg_nB"].ap()[:, :], p2nmrB[:, :], s_st)
        dma("sync", d_dbg["dbg_z"].ap()[:, :], z16[:, :], s_st)
        dma("sync", d_dbg["dbg_pre"].ap()[:, :], pre_f[:, :], s_st)
        dma("sync", d_dbg["dbg_lp4"].ap()[:, :], lp4[:, :], s_st)
        dma("sync", d_dbg["dbg_Sf"].ap()[:, :], S_f[:, :], s_st)
        dma("sync", d_dbg["dbg_e4"].ap()[:, :], e4t[:, :], s_st)
        dma("sync", d_dbg["dbg_lpv"].ap()[:, :], lpv[:, :], s_st)
        dma("sync", d_dbg["dbg_lp16"].ap()[:, :], lp16[:, :], s_st)
        dma("sync", d_dbg["dbg_Se16"].ap()[:, :], S_e16[:, :], s_st)
        dma("sync", d_dbg["dbg_a4"].ap()[:, :], lp4[:, :], s_st)
        dma("sync", d_dbg["dbg_can2"].ap()[:, :], canary[:, :], s_st)
        dma("sync", d_dbg["dbg_zp"].ap()[:, :], zp16[:, :], s_st)
        dma("sync", d_dbg["dbg_lg"].ap()[:, :], lg16[:, :], s_st)
        dma("sync", d_dbg["dbg_alT"].ap()[:, :], alphaT16[:, :], s_st)
        dma("sync", d_dbg["dbg_aE"].ap()[:, :], aE16[:, :], s_st)
        dma("sync", d_dbg["dbg_kT"].ap()[:, :], kT[:, :], s_st)
        dma("sync", d_dbg["dbg_Vt"].ap()[:, :], Vt[:, :], s_st)
        dma("sync", d_dbg["dbg_xn"].ap()[:, :], xnbq[:, :], s_st)

    SC_Qog = {}
    SC_Qev = {}
    for og in range(16):
        par = og % 2
        w = []
        if og == 0:
            w = [(m_pe, P2B_PE)]
        elif og >= 2:
            w = [(v_ve, SC_Qog[og - 2])]
        for dc in range(NCH):
            dma("gpsimd", WQP[par][:, dc * 256:(dc + 1) * 256],
                AP(d_wqT, dc * 128 * D + og * 256, [[D, 128], [1, 256]]),
                s_wq[par], waits=w if dc == 0 else ())
        cnt = DC[id(s_wq[par])]
        for oc in range(2):
            for qt in range(2):
                ocg = og * 2 + oc
                bank = 2 * oc + qt
                for dc in range(NCH):
                    ww = []
                    if dc == 0:
                        ww = [(s_wq[par], cnt),
                              (v_ve, VE_ZP if og == 0 else SC_Qev.get((og - 1, bank), 0))]
                    op("tensor", lambda e, dc=dc, oc=oc, qt=qt, bank=bank, par=par:
                       e.matmul(
                        pb[bank][:, :],
                        WQP[par][:, dc * 256 + oc * 128: dc * 256 + oc * 128 + 128],
                        xnbq[:, dc * NQ + qt * 512: dc * NQ + qt * 512 + 512],
                        start=(dc == 0), stop=(dc == NCH - 1)),
                       waits=ww, inc="pe" if dc == NCH - 1 else None)
                PE_Q = C["pe"]
                SC_Qev[(og, bank)] = op("vector", lambda e, ocg=ocg, qt=qt, bank=bank:
                                        e.tensor_scalar(
                    qTat[:, ocg * NQ + qt * 512: ocg * NQ + qt * 512 + 512],
                    pb[bank][:, :], wq_bP[:, ocg:ocg + 1], None, ALU.add),
                    waits=[(m_pe, PE_Q)], inc="ve")
        SC_Qog[og] = SC_Qev[(og, 3)]
    VE_QP = C["ve"]
    VE_Q2 = 0
    PE_Q2 = 0
    for hh in range(H):
        VE_SQH = op("vector", lambda e, hh=hh: e.tensor_tensor(
            q2b[:, :], qTat[:, hh * NQ:(hh + 1) * NQ],
            qTat[:, hh * NQ:(hh + 1) * NQ], ALU.mult),
            waits=[(v_ve, VE_QP), (m_pe, PE_Q2)], inc="ve")
        for qt in range(2):
            PE_Q2 = op("tensor", lambda e, qt=qt: e.matmul(
                pb[4][0:1, :], ones_col[:, :], q2b[:, qt * 512:(qt + 1) * 512],
                start=True, stop=True),
                waits=[(v_ve, VE_SQH), (v_ve, VE_Q2)], inc="pe")
            VE_Q2 = op("vector", lambda e, hh=hh, qt=qt: e.reduce_max(
                qrow[:, 2 * hh + qt:2 * hh + qt + 1], pb[4][0:1, :],
                axis=mybir.AxisListType.X), waits=[(m_pe, PE_Q2)], inc="ve")
    VE_KMX = op("vector", lambda e: e.reduce_max(
        ftiny[:, KM2:KM2 + 1], krow[:, :], axis=mybir.AxisListType.X), inc="ve")
    VE_QMX = op("vector", lambda e: e.reduce_max(
        ftiny[:, QM2:QM2 + 1], qrow[:, :], axis=mybir.AxisListType.X),
        waits=[(v_ve, VE_Q2)], inc="ve")
    op("scalar", lambda e: e.activation(ftiny[:, TSC:TSC + 1],
                                        ftiny[:, KM2:KM2 + 1], AF.Ln),
       waits=[(v_ve, VE_QMX)])
    SC_LNS = op("scalar", lambda e: e.activation(ftiny[:, CSC:CSC + 1],
                                                 ftiny[:, QM2:QM2 + 1], AF.Ln),
                inc="sc")
    VE_C1 = op("vector", lambda e: e.tensor_tensor(
        ftiny[:, TSC:TSC + 1], ftiny[:, TSC:TSC + 1], ftiny[:, CSC:CSC + 1],
        ALU.add), waits=[(a_sc, SC_LNS)], inc="ve")
    SC_C = op("scalar", lambda e: e.activation(
        ftiny[:, CSC:CSC + 1], ftiny[:, TSC:TSC + 1], AF.Exp,
        scale=ftiny[:, POSH:POSH + 1]), waits=[(v_ve, VE_C1)], inc="sc")
    VE_C2 = op("vector", lambda e: e.tensor_scalar(
        btiny[:, 0:1], ftiny[:, CSC:CSC + 1], -RSQ, None, ALU.mult),
        waits=[(a_sc, SC_C)], inc="ve")
    PE_C = op("tensor", lambda e: e.matmul(
        pb[4][0:128, 0:1], ones_row[:, :], btiny[0:1, 0:1], start=True, stop=True),
        waits=[(v_ve, VE_C2)], inc="pe")
    VE_CB = op("vector", lambda e: e.tensor_copy(cbias[:, :], pb[4][:, 0:1]),
               waits=[(m_pe, PE_C)], inc="ve")

    DBG_GATE = 0
    if DEBUG:
        dma("sync", d_dbg["dbg_ft"].ap()[:, :], ftiny[:, :], s_st, waits=[(v_ve, VE_CB)])
        dma("sync", d_dbg["dbg_cb"].ap()[:, :], cbias[:, :], s_st)
        dma("sync", d_dbg["dbg_qT"].ap()[:, :], qTat[:, :], s_st, waits=[(v_ve, VE_QP)])
        DBG_GATE = DC[id(s_st)]

    pq = [pA[0], pA[1]]
    pzavs = [pA[2], pA[3]]
    MK_DONE = [0, 0]
    itctr = [0]
    last_exp_pq = {0: 0, 1: 0}
    last_av_pb = {0: 0, 1: 0, 2: 0, 3: 0}
    last_attn_ve = {0: 0, 1: 0}
    last_rzb_ve = {0: 0, 1: 0}
    zav_pend = []
    first_exp = True
    pend = {"pe": [], "ve": [], "sc": []}
    pendq = []
    DEPTH = {"pe": 1, "ve": 1, "sc": 1}

    def flushpend():
        for k, eng in (("sc", "scalar"), ("pe", "tensor"), ("ve", "vector")):
            if len(pendq) >= DEPTH[k]:
                for qi in range(len(pendq) - DEPTH[k] + 1):
                    for fn in pendq[qi][k]:
                        Q[eng].append(fn)
                    pendq[qi][k] = []

    def rotatepend():
        pendq.append({"pe": pend["pe"], "ve": pend["ve"], "sc": pend["sc"]})
        pend["pe"] = []
        pend["ve"] = []
        pend["sc"] = []
        while pendq and not (pendq[0]["pe"] or pendq[0]["ve"] or pendq[0]["sc"]):
            pendq.pop(0)

    def flushall():
        rotatepend()
        for q in pendq:
            for k, eng in (("sc", "scalar"), ("pe", "tensor"), ("ve", "vector")):
                for fn in q[k]:
                    Q[eng].append(fn)
        pendq.clear()

    def pe_pend(build, waits=(), inc=None):
        w = tuple((sm, v) for (sm, v) in waits if v > 0)
        tgt = None
        if inc:
            C[inc] += 1
            tgt = SEMOBJ[inc]

        def fn(engine, w=w, build=build, tgt=tgt):
            for (sm, v) in w:
                engine.wait_ge(sm, v)
            ins = build(engine)
            if tgt is not None and ins is not None:
                ins.then_inc(tgt, 1)
        pend["pe"].append(fn)
        return C[inc] if inc else None

    def sc_pend(build, waits=(), inc=None):
        w = tuple((sm, v) for (sm, v) in waits if v > 0)
        tgt = None
        if inc:
            C[inc] += 1
            tgt = SEMOBJ[inc]

        def fn(engine, w=w, build=build, tgt=tgt):
            for (sm, v) in w:
                engine.wait_ge(sm, v)
            ins = build(engine)
            if tgt is not None and ins is not None:
                ins.then_inc(tgt, 1)
        pend["sc"].append(fn)
        return C[inc] if inc else None

    def ve_pend(build, waits=()):
        w = [(sm, v) for (sm, v) in waits if v > 0]
        if C["vn"] > 0:
            w.append((m_vn, C["vn"]))
        C["vn"] += 1
        tgt = m_vn
        w = tuple(w)

        def fn(engine, w=w, build=build, tgt=tgt):
            for (sm, v) in w:
                engine.wait_ge(sm, v)
            ins = build(engine)
            if ins is not None:
                ins.then_inc(tgt, 1)
        pend["ve"].append(fn)
        return C["vn"]

    norm_jobs = []

    def emit_norm():
        if not norm_jobs:
            return
        qoff, zpar, pzav, npair_ = norm_jobs.pop(0)
        while zav_pend:
            emit_zav(zav_pend.pop(0))
        PE_AVL = last_av_pb[(npair_ - 1) % 4]
        rzv = rz16 if zpar == 0 else rz16b
        rzbv = rzbS if zpar == 0 else rzbS2
        SN_LN = sc_pend(lambda e, pzav=pzav: e.activation(
            lnzr[:, :], pzav.ap()[0:1, 0:512], AF.Ln),
            waits=[(m_pe, PE_AVL), (m_sn, C["sn"])], inc="sn")
        last_lnz = sc_pend(lambda e, rzv=rzv: e.activation(
            rzv[:, :], lnzr[:, :], AF.Exp, scale=ftiny[:, NEG1:NEG1 + 1]),
            waits=[(m_sn, SN_LN)], inc="sn")
        PE_RB = pe_pend(lambda e, pzav=pzav, rzv=rzv: e.matmul(
            pzav.ap()[:, 0:512], ones_row[:, :], rzv[:, :],
            start=True, stop=True),
            waits=[(m_sn, last_lnz)], inc="pn")
        last_rzb_ve[zpar] = ve_pend(lambda e, pzav=pzav, rzbv=rzbv: e.tensor_copy(
            rzbv[:, :], pzav.ap()[:, 0:512]),
            waits=[(m_pn, PE_RB)])
        out_ap = AP(qTat, qoff, [[NCH * NQ, 128], [NQ, 4], [1, 128]])
        last_attn_ve[zpar] = ve_pend(lambda e, out_ap=out_ap, pzav=pzav,
                                     rzbv=rzbv: e.tensor_tensor(
            out_ap, pzav.ap()[:, 512:1024], rzbv[:, :], ALU.mult))
        rotatepend()

    for g in range(NSLOT):
        nkc = 4 * g + 4
        npair = nkc // 2
        par = g % 2
        wm0 = [(v_ve, MK_DONE[par])] if g >= 2 else [(v_ve, max(VE_ZP, VE_QP)),
                                                     (m_pe, P2B_PE)]
        for j in range(4):
            dma("sync", MB[par][:, j * 512:(j + 1) * 512],
                d_masks.ap()[4 * g + j, :, :], s_mk[par], waits=wm0 if j == 0 else ())
        mkcnt = DC[id(s_mk[par])]
        for hg in range(NHG):
            qoff = (4 * hg) * NQ + g * 128
            rhs_q = AP(qTat, qoff, [[NCH * NQ, 128], [NQ, 4], [1, 128]])
            zpar = itctr[0] % 2
            pzav = pzavs[zpar]

            def emit_zav(job):
                p, dep, nkc_, pz, zp = job
                wz = [dep, (m_vn, last_rzb_ve[zp])] if p == 0 else [dep]
                op("tensor", lambda e, p=p, nkc_=nkc_, pz=pz: e.matmul(
                    pz.ap()[0:1, 0:512], ones_col[:, :],
                    Pbp[p % 4][:, 0:512],
                    start=(2 * p == 0), stop=False), waits=wz)
                op("tensor", lambda e, p=p, nkc_=nkc_, pz=pz: e.matmul(
                    pz.ap()[0:1, 0:512], ones_col[:, :],
                    Pbp[p % 4][:, 512:1024],
                    start=False, stop=(2 * p + 1 == nkc_ - 1)))
                wa = [(m_vn, last_attn_ve[zp])] if p == 0 else []
                op("tensor", lambda e, p=p, nkc_=nkc_, pz=pz: e.matmul(
                    pz.ap()[:, 512:1024],
                    Vt[:, (2 * p) * 128:(2 * p) * 128 + 128],
                    Pbp[p % 4][:, 0:512],
                    start=(2 * p == 0), stop=False), waits=wa)
                last_av_pb[p % 4] = op("tensor",
                                       lambda e, p=p, nkc_=nkc_, pz=pz: e.matmul(
                    pz.ap()[:, 512:1024],
                    Vt[:, (2 * p + 1) * 128:(2 * p + 1) * 128 + 128],
                    Pbp[p % 4][:, 512:1024],
                    start=False, stop=(2 * p + 1 == nkc_ - 1)), inc="pe")

            for p in range(npair):
                pi = p % 2
                t = pq[pi]
                ws = [(a_sc, last_exp_pq[pi])]
                if first_exp:
                    ws.append((v_ve, VE_CB))
                op("tensor", lambda e, p=p, t=t, rhs_q=rhs_q: e.matmul(
                    t.ap()[:, 0:512], kT[:, (2 * p) * 128:(2 * p) * 128 + 128],
                    rhs_q, start=True, stop=True), waits=ws)
                PE_S = op("tensor", lambda e, p=p, t=t, rhs_q=rhs_q: e.matmul(
                    t.ap()[:, 512:1024], kT[:, (2 * p + 1) * 128:(2 * p + 1) * 128 + 128],
                    rhs_q, start=True, stop=True), inc="pe")
                if p == 0:
                    flushpend()
                we = [(m_pe, PE_S), (m_pe, last_av_pb[p % 4])]
                SC_E = op("scalar", lambda e, p=p, t=t: e.activation(
                    Pbp[p % 4][:, :], t.ap()[:, :], AF.Exp, bias=cbias[:, :],
                    scale=escale[:, :]), waits=we, inc="sc")
                last_exp_pq[pi] = SC_E
                first_exp = False
                dep = (a_sc, SC_E)
                if p >= 2 * g:
                    jj = 2 * p - 4 * g
                    wmm = [(a_sc, SC_E)]
                    if hg == 0 and p == 2 * g:
                        wmm.append((s_mk[par], mkcnt))
                    VE_M = op("vector", lambda e, p=p, jj=jj, par=par: e.tensor_tensor(
                        Pbp[p % 4][:, :], Pbp[p % 4][:, :],
                        MB[par][:, jj * 512:(jj + 2) * 512], ALU.mult),
                        waits=wmm)
                    dep = (v_ve, VE_M)
                zav_pend.append((p, dep, nkc, pzav, zpar))
                if len(zav_pend) > 2:
                    emit_zav(zav_pend.pop(0))
            norm_jobs.append((qoff, zpar, pzav, npair))
            emit_norm()
            itctr[0] += 1
            PE_AVL = last_av_pb[(npair - 1) % 4]
        MK_DONE[par] = C["ve"]
    while norm_jobs:
        emit_norm()
    flushall()
    P3_PE, P3_SC, P3_VE = C["pe"], C["sc"], C["ve"]
    P3_VN = C["vn"]

    if DEBUG:
        dma("sync", d_dbg["dbg_at"].ap()[:, :], qTat[:, :], s_st, waits=[(v_ve, P3_VE)])

    SC_P4og = {}
    VE_P4 = {}
    last_xrs = {0: P3_VE, 1: P3_VE, 2: P3_VE, 3: P3_VE}
    first_p4 = True
    last_sev = {0: 0, 1: 0, 2: 0, 3: 0}
    last_yev = {0: 0, 1: 0, 2: 0, 3: 0}
    last_sg = {0: 0, 1: 0}
    last_ymm = 0
    for og in range(16):
        par = og % 2
        w = []
        if og == 0:
            w = [(v_ve, P3_VE), (m_vn, P3_VN)]
        elif og >= 2:
            w = [(a_sc, SC_P4og[og - 2])]
        for dc in range(NCH):
            dma("gpsimd", WOP[par][:, dc * 256:(dc + 1) * 256],
                AP(d_woT, dc * 128 * D + og * 256, [[D, 128], [1, 256]]),
                s_wo[par], waits=w if dc == 0 else ())
        wocnt = DC[id(s_wo[par])]
        wwb = [(v_ve, P3_VE), (m_vn, P3_VN)] if og == 0 else [(m_pe, last_ymm)]
        for dc in range(NCH):
            dma("gpsimd", WBP[:, dc * 256:(dc + 1) * 256],
                AP(d_WbT, dc * 128 * D + og * 256, [[D, 128], [1, 256]]),
                s_wb, waits=wwb if dc == 0 else ())
        dma("gpsimd", WBP[:, NCH * 256:NCH * 256 + 256],
            AP(d_Bt, og * 256, [[D, 128], [1, 256]]), s_wb)
        wbcnt = DC[id(s_wb)]
        SC_SE = {}
        for oc in range(2):
            for qt in range(2):
                ocg = og * 2 + oc
                bk = 2 * oc + qt
                xw = [(v_ve, last_xrs[bk])]
                if og == 0:
                    xw.append((m_vn, P3_VN))
                xrcnt = dma("sync", XRS[bk][:, :],
                            AP(d_xqT, ocg * 128 * NQ + qt * 512, [[NQ, 128], [1, 512]]),
                            s_xr[bk], waits=xw)
                XRC = xrcnt
                for dc in range(NCH):
                    ws = []
                    if dc == 0:
                        ws = [(s_wo[par], wocnt), (v_ve, last_sev[bk])]
                    op("tensor", lambda e, dc=dc, oc=oc, qt=qt, bk=bk, par=par:
                       e.matmul(
                        pb[bk][:, :],
                        WOP[par][:, dc * 256 + oc * 128: dc * 256 + oc * 128 + 128],
                        qTat[:, dc * NQ + qt * 512: dc * NQ + qt * 512 + 512],
                        start=(dc == 0), stop=(dc == NCH - 1)),
                       waits=ws, inc="pe" if dc == NCH - 1 else None)
                PE_S4 = C["pe"]
                sw = [(m_pe, PE_S4)]
                if og == 0:
                    sw.append((m_vn, P3_VN))
                last_sev[bk] = op("vector", lambda e, ocg=ocg, bk=bk:
                                  e.tensor_scalar(
                    SGT[bk][:, :], pb[bk][:, :], wo_bP[:, ocg:ocg + 1],
                    None, ALU.add), waits=sw)
                SC_SE[(oc, qt)] = (last_sev[bk], XRC)
        for oc in range(2):
            for qt in range(2):
                ocg = og * 2 + oc
                bk = 2 * oc + qt
                ybk = 4 + bk
                for dc in range(NCH):
                    wy = []
                    if dc == 0:
                        wy = [(s_wb, wbcnt), (a_sc, last_yev[bk])]
                    op("tensor", lambda e, dc=dc, oc=oc, qt=qt, ybk=ybk: e.matmul(
                        pb[ybk][:, :],
                        WBP[:, dc * 256 + oc * 128: dc * 256 + oc * 128 + 128],
                        xnbq[:, dc * NQ + qt * 512: dc * NQ + qt * 512 + 512],
                        start=(dc == 0), stop=False), waits=wy)
                last_ymm = op("tensor", lambda e, oc=oc, qt=qt, ybk=ybk: e.matmul(
                    pb[ybk][:, :], WBP[:, NCH * 256 + oc * 128: NCH * 256 + oc * 128 + 128],
                    zp16[:, qt * 512:(qt + 1) * 512], start=False, stop=True),
                    inc="pe")
                PE_Y4 = last_ymm
                SEV, XRC = SC_SE[(oc, qt)]
                last_yev[bk] = op("scalar", lambda e, ocg=ocg, ybk=ybk, qt=qt:
                                  e.activation(
                    SGT2[qt][:, :], pb[ybk][:, :], AF.Silu,
                    bias=bb_P[:, ocg:ocg + 1]),
                    waits=[(m_pe, PE_Y4), (s_ld, LD0), (v_ve, last_sg[qt])], inc="sc")
                SC_EE = last_yev[bk]
                last_sg[qt] = op("vector", lambda e, qt=qt, bk=bk: e.tensor_tensor(
                    OUTS[qt][:, :], SGT[bk][:, :], SGT2[qt][:, :], ALU.mult),
                    waits=[(a_sc, SC_EE), (s_so[qt], DC[id(s_so[qt])])])
                VE_P4[(og, oc, qt)] = op("vector", lambda e, qt=qt, bk=bk:
                                         e.tensor_tensor(
                    OUTS[qt][:, :], OUTS[qt][:, :], XRS[bk][:, :], ALU.add),
                    waits=[(s_xr[bk], XRC)])
                last_xrs[bk] = VE_P4[(og, oc, qt)]
                dma("sync", AP(d_out, ocg * 128 * NQ + qt * 512, [[NQ, 128], [1, 512]]),
                    OUTS[qt][:, :], s_so[qt], waits=[(v_ve, VE_P4[(og, oc, qt)])])
        SC_P4og[og] = C["sc"]
    ST0, ST1, STD = DC[id(s_so[0])], DC[id(s_so[1])], DC[id(s_st)]
    Q["sync"].append(lambda e: e.wait_ge(s_so[0], ST0))
    Q["sync"].append(lambda e: e.wait_ge(s_so[1], ST1))
    if STD:
        Q["sync"].append(lambda e: e.wait_ge(s_st, STD))

    with nc.Block() as block:
        @block.sync
        def _(eng):
            for fn in Q["sync"]:
                fn(eng)

        @block.gpsimd
        def _(eng):
            for fn in Q["gpsimd"]:
                fn(eng)

        @block.tensor
        def _(eng):
            for fn in Q["tensor"]:
                fn(eng)

        @block.vector
        def _(eng):
            for fn in Q["vector"]:
                fn(eng)

        @block.scalar
        def _(eng):
            for fn in Q["scalar"]:
                fn(eng)

    return nc


_CACHE = {}


def _get_nc():
    if "nc" not in _CACHE:
        _CACHE["nc"] = build_program()
    return _CACHE["nc"]


def kernel(L0_attn_x, L0_attn_wq_w, L0_attn_wq_b, L0_attn_wk_w, L0_attn_wk_b,
           L0_attn_wv_w, L0_attn_wv_b, L0_attn_wo_w, L0_attn_wo_b,
           L0_ffn_W_base, L0_ffn_b_base, L0_ffn_lora_A, L0_ffn_lora_B,
           L0_ffn_router_W, L0_ffn_router_b, L0_ffn_domain_prior, L0_ffn_active):
    x = np.asarray(L0_attn_x, dtype=np.float32)

    def f32(a):
        return np.ascontiguousarray(np.asarray(a, dtype=np.float32))

    wq, wk, wv = f32(L0_attn_wq_w), f32(L0_attn_wk_w), f32(L0_attn_wv_w)
    wo, Wb = f32(L0_attn_wo_w), f32(L0_ffn_W_base)
    xT = [np.ascontiguousarray(x[b].T) for b in range(2)]
    wqT = np.ascontiguousarray(wq.T)
    wkT = np.ascontiguousarray(wk.T)
    wvT = np.ascontiguousarray(wv.T)
    woT = np.ascontiguousarray(wo.T)
    WbT = np.ascontiguousarray(Wb.T)
    wq_bP = np.ascontiguousarray(f32(L0_attn_wq_b).reshape(NCH, 128).T)
    wo_bP = np.ascontiguousarray(f32(L0_attn_wo_b).reshape(NCH, 128).T)
    bb_P = np.ascontiguousarray(f32(L0_ffn_b_base).reshape(NCH, 128).T)
    wk_bP = f32(L0_attn_wk_b).reshape(128, 1)
    wv_bP = f32(L0_attn_wv_b).reshape(128, 1)
    AT = np.ascontiguousarray(f32(L0_ffn_lora_A).reshape(128, D).T)
    Bt = np.ascontiguousarray(f32(L0_ffn_lora_B).transpose(0, 2, 1).reshape(128, D))
    rT = np.ascontiguousarray(f32(L0_ffn_router_W).T)
    rb_P = f32(L0_ffn_router_b).reshape(16, 1)
    prior_P = f32(L0_ffn_domain_prior).reshape(16, 1)
    act_i = np.ascontiguousarray(np.asarray(L0_ffn_active, dtype=np.int32))
    iota16 = np.arange(16, dtype=np.float32).reshape(16, 1)
    iotaR4 = np.ascontiguousarray(np.tile(np.arange(16, dtype=np.float32), (4, 1)))
    identB = np.eye(128, dtype=np.float32).astype(bf16)
    identF4 = np.eye(4, dtype=np.float32)
    identF16 = np.eye(16, dtype=np.float32)
    R2 = np.zeros((16, 128), dtype=np.float32)
    for e in range(16):
        R2[e, e * 8:(e + 1) * 8] = 2.0
    R2 = R2.astype(bf16)

    tri = (np.arange(128)[:, None] <= np.arange(128)[None, :]).astype(np.float32)
    in_maps = []
    qrows_all = []
    for c in range(8):
        b, cm = c // 4, c % 4
        qrows = np.concatenate(
            [np.arange(128) + (4 * g + cm) * 128 for g in range(NSLOT)])
        qrows_all.append(qrows)
        xqT = np.ascontiguousarray(x[b][qrows].T)
        masks = np.zeros((NSLOT * 4, 128, 512), dtype=np.float32)
        for g in range(NSLOT):
            for j in range(4):
                if j < cm:
                    masks[4 * g + j] = 1.0
                elif j == cm:
                    masks[4 * g + j] = np.tile(tri, (1, 4))
        masks = masks.astype(bf16)
        in_maps.append({
            "xT": xT[b], "xqT": xqT, "masks": masks, "identF16": identF16,
            "wqT": wqT, "wkT": wkT, "wvT": wvT, "woT": woT, "WbT": WbT,
            "wq_bP": wq_bP, "wo_bP": wo_bP, "bb_P": bb_P,
            "wk_bP": wk_bP, "wv_bP": wv_bP,
            "AT": AT, "Bt": Bt, "rT": rT, "rb_P": rb_P, "prior_P": prior_P,
            "act_i": act_i, "iota16": iota16, "iota16b": iota16.astype(bf16), "iotaR4": iotaR4,
            "identB": identB, "identF4": identF4, "R2": R2,
        })

    nc = _get_nc()
    res = run_bass_kernel_spmd(nc, in_maps, core_ids=list(range(8)))
    out = np.empty((2, T, D), dtype=np.float32)
    for c in range(8):
        b = c // 4
        outT = np.asarray(res.results[c]["outT"])
        out[b, qrows_all[c], :] = outT.T
    return out


# version 26 (final)
# speedup vs baseline: 1.3203x; 1.0843x over previous
import numpy as np
import ml_dtypes
import concourse.bass as bass
import concourse.mybir as mybir
from concourse.bass_utils import run_bass_kernel_spmd

F32 = mybir.dt.float32
BF16 = mybir.dt.bfloat16
I32 = mybir.dt.int32
AF = mybir.ActivationFunctionType
ALU = mybir.AluOpType
bf16 = ml_dtypes.bfloat16
AP = bass.AP

D = 4096
T = 4096
NCH = 32
NQ = 1024
H = 32
HD = 128
NSLOT = 8
NHG = 8
RSQ = 0.08838834764831845
KB = 1024
DEBUG = False


def build_program():
    nc = bass.Bass("TRN2", target_bir_lowering=False)

    def din(name, shape, dt):
        return nc.dram_tensor(name, shape, dt, kind="ExternalInput")

    d_xT = din("xT", [D, T], F32)
    d_xqT = din("xqT", [D, NQ], F32)
    d_masks = din("masks", [NSLOT * 4, 128, 512], BF16)
    d_wqT = din("wqT", [D, D], F32)
    d_wkT = din("wkT", [D, HD], F32)
    d_wvT = din("wvT", [D, HD], F32)
    d_woT = din("woT", [D, D], F32)
    d_WbT = din("WbT", [D, D], F32)
    d_wqb = din("wq_bP", [128, NCH], F32)
    d_wob = din("wo_bP", [128, NCH], F32)
    d_bbb = din("bb_P", [128, NCH], F32)
    d_wkb = din("wk_bP", [128, 1], F32)
    d_wvb = din("wv_bP", [128, 1], F32)
    d_AT = din("AT", [D, 128], F32)
    d_Bt = din("Bt", [128, D], F32)
    d_rT = din("rT", [D, 16], F32)
    d_rb = din("rb_P", [16, 1], F32)
    d_pr = din("prior_P", [16, 1], F32)
    d_act = din("act_i", [4], I32)
    d_iota = din("iota16", [16, 1], F32)
    d_iotab = din("iota16b", [16, 1], BF16)
    d_iotaR4 = din("iotaR4", [4, 16], F32)
    d_identB = din("identB", [128, 128], BF16)
    d_identF4 = din("identF4", [4, 4], F32)
    d_identF16 = din("identF16", [16, 16], F32)
    d_R2 = din("R2", [16, 128], BF16)
    d_out = nc.dram_tensor("outT", [D, NQ], F32, kind="ExternalOutput")
    if DEBUG:
        d_dbg = {
            "dbg_ft": nc.dram_tensor("dbg_ft", [1, 16], F32, kind="ExternalOutput"),
            "dbg_cb": nc.dram_tensor("dbg_cb", [128, 1], F32, kind="ExternalOutput"),
            "dbg_kT": nc.dram_tensor("dbg_kT", [128, T], BF16, kind="ExternalOutput"),
            "dbg_Vt": nc.dram_tensor("dbg_Vt", [128, T], BF16, kind="ExternalOutput"),
            "dbg_xn": nc.dram_tensor("dbg_xn", [128, NCH * NQ], BF16, kind="ExternalOutput"),
            "dbg_qT": nc.dram_tensor("dbg_qT", [128, NCH * NQ], BF16, kind="ExternalOutput"),
            "dbg_at": nc.dram_tensor("dbg_at", [128, NCH * NQ], BF16, kind="ExternalOutput"),
            "dbg_z": nc.dram_tensor("dbg_z", [128, NQ], BF16, kind="ExternalOutput"),
            "dbg_zp": nc.dram_tensor("dbg_zp", [128, NQ], BF16, kind="ExternalOutput"),
            "dbg_lg": nc.dram_tensor("dbg_lg", [16, NQ], BF16, kind="ExternalOutput"),
            "dbg_alT": nc.dram_tensor("dbg_alT", [4, NQ], BF16, kind="ExternalOutput"),
            "dbg_aE": nc.dram_tensor("dbg_aE", [16, NQ], BF16, kind="ExternalOutput"),
            "dbg_rows": nc.dram_tensor("dbg_rows", [1, 2048], F32, kind="ExternalOutput"),
            "dbg_brow": nc.dram_tensor("dbg_brow", [1, 1024], BF16, kind="ExternalOutput"),
            "dbg_rB": nc.dram_tensor("dbg_rB", [128, 512], BF16, kind="ExternalOutput"),
            "dbg_nB": nc.dram_tensor("dbg_nB", [128, 512], BF16, kind="ExternalOutput"),
            "dbg_wk": nc.dram_tensor("dbg_wk", [128, NCH * HD], BF16, kind="ExternalOutput"),
            "dbg_T1": nc.dram_tensor("dbg_T1", [128, 512], BF16, kind="ExternalOutput"),
            "dbg_prB": nc.dram_tensor("dbg_prB", [128, 512], BF16, kind="ExternalOutput"),
            "dbg_csk": nc.dram_tensor("dbg_csk", [1, 128], BF16, kind="ExternalOutput"),
            "dbg_pre": nc.dram_tensor("dbg_pre", [4, 512], F32, kind="ExternalOutput"),
            "dbg_lp4": nc.dram_tensor("dbg_lp4", [4, 1], F32, kind="ExternalOutput"),
            "dbg_Sf": nc.dram_tensor("dbg_Sf", [16, 4], F32, kind="ExternalOutput"),
            "dbg_e4": nc.dram_tensor("dbg_e4", [128, 4], F32, kind="ExternalOutput"),
            "dbg_lpv": nc.dram_tensor("dbg_lpv", [16, 1], F32, kind="ExternalOutput"),
            "dbg_lp16": nc.dram_tensor("dbg_lp16", [16, 1], BF16, kind="ExternalOutput"),
            "dbg_Se16": nc.dram_tensor("dbg_Se16", [16, 4], BF16, kind="ExternalOutput"),
            "dbg_a4": nc.dram_tensor("dbg_a4", [4, 1], F32, kind="ExternalOutput"),
            "dbg_Se16e": nc.dram_tensor("dbg_Se16e", [16, 4], BF16, kind="ExternalOutput"),
            "dbg_lp16e": nc.dram_tensor("dbg_lp16e", [16, 1], BF16, kind="ExternalOutput"),
            "dbg_can1": nc.dram_tensor("dbg_can1", [128, 512], F32, kind="ExternalOutput"),
            "dbg_can2": nc.dram_tensor("dbg_can2", [128, 512], F32, kind="ExternalOutput"),
        }

    SBB = 16512

    def sb(name, shape, dt, off):
        return nc.alloc_sbuf_tensor_at(name, shape, dt, offset=SBB + off)

    xnbq = sb("xnbq", [128, NCH * NQ], BF16, 0)
    qTat = sb("qTat", [128, NCH * NQ], BF16, 64 * KB)
    kT = sb("kT", [128, T], BF16, 128 * KB)
    Vt = sb("Vt", [128, T], BF16, 136 * KB)
    wkTb = sb("wkTb", [128, NCH * HD], BF16, 144 * KB)
    wvTb = sb("wvTb", [128, NCH * HD], BF16, 152 * KB)
    ATb = sb("ATb", [128, NCH * 128], BF16, 160 * KB)
    XB = [sb("XB0", [128, NCH * 512], BF16, 0),
          sb("XB1", [128, NCH * 512], BF16, 32 * KB)]
    X2F = sb("X2F", [128, NCH * 512], BF16, 64 * KB)
    T1F = sb("T1F", [128, NCH * 512], BF16, 96 * KB)
    p1rstdB = sb("p1rstdB", [128, 512], BF16, 190 * KB)
    p1k2b = sb("p1k2b", [128, 512], BF16, 191 * KB)
    p1VTs = sb("p1VTs", [128, 512], BF16, 192 * KB)
    p1rows = sb("p1rows", [1, 2048], F32, 198 * KB)
    p1brow = sb("p1brow", [1, 1024], BF16, 177 * KB)
    XQB = sb("XQB", [128, NCH * 512], BF16, 64 * KB)
    X2F2 = sb("X2F2", [128, NCH * 512], BF16, 96 * KB)
    p2rstdB = sb("p2rstdB", [128, 512], BF16, 190 * KB)
    p2nmrB = sb("p2nmrB", [128, 512], BF16, 191 * KB)
    p2rows = sb("p2rows", [1, 2048], F32, 198 * KB)
    p2brow = sb("p2brow", [1, 1024], BF16, 179 * KB)
    WQP = [sb("WQP0", [128, NCH * 256], BF16, 144 * KB),
           sb("WQP1", [128, NCH * 256], BF16, 160 * KB)]
    WOP = [sb("WOP0", [128, NCH * 256], BF16, 128 * KB),
           sb("WOP1", [128, NCH * 256], BF16, 144 * KB)]
    WBP = sb("WBP", [128, NCH * 256 + 256], BF16, 160 * KB)
    rTb = sb("rTb", [128, NCH * 16], BF16, 176 * KB)
    z16 = sb("z16", [128, NQ], BF16, 177 * KB)
    zp16 = sb("zp16", [128, NQ], BF16, 179 * KB)
    lg16 = sb("lg16", [16, NQ], BF16, 181 * KB)
    alphaT16 = sb("alphaT16", [4, NQ], BF16, 183 * KB)
    aE16 = sb("aE16", [16, NQ], BF16, 185 * KB)
    aB16 = sb("aB16", [128, 512], BF16, 187 * KB)
    pre_f = sb("pre_f", [4, 512], F32, 188 * KB)
    MB = [sb("MB0", [128, 4 * 512], BF16, 181 * KB),
          sb("MB1", [128, 4 * 512], BF16, 185 * KB)]
    XRS = [sb("XRS0", [128, 512], F32, 181 * KB),
           sb("XRS1", [128, 512], F32, 183 * KB),
           sb("XRS2", [128, 512], F32, 198 * KB),
           sb("XRS3", [128, 512], F32, 200 * KB)]
    OUTS = [sb("OUTS0", [128, 512], F32, 185 * KB),
            sb("OUTS1", [128, 512], F32, 187 * KB)]
    Pbp = [sb("Pbp0", [128, 1024], BF16, 190 * KB),
           sb("Pbp1", [128, 1024], BF16, 192 * KB),
           sb("Pbp2", [128, 1024], BF16, 146 * KB),
           sb("Pbp3", [128, 1024], BF16, 148 * KB)]
    q2b = sb("q2b", [128, NQ], BF16, 190 * KB)
    SGT = [sb("SGT0", [128, 512], BF16, 190 * KB), sb("SGT1", [128, 512], BF16, 191 * KB),
           sb("SGT2", [128, 512], BF16, 192 * KB), sb("SGT3", [128, 512], BF16, 193 * KB)]
    SGT2 = [sb("SGT2a", [128, 512], BF16, 189 * KB),
            sb("SGT2b", [128, 512], BF16, 177 * KB)]
    b0 = 194 * KB
    cbias = sb("cbias", [128, 1], F32, b0 + 0)
    escale = sb("escale", [128, 1], F32, b0 + 32)
    wkb = sb("wkb", [128, 1], F32, b0 + 64)
    wvb = sb("wvb", [128, 1], F32, b0 + 96)
    ones_col = sb("ones_col", [128, 1], BF16, b0 + 128)
    mxc = sb("mxc", [128, 1], F32, b0 + 160)
    nmxc = sb("nmxc", [128, 1], F32, b0 + 192)
    s4c = sb("s4c", [128, 1], F32, b0 + 224)
    rsc = sb("rsc", [128, 1], F32, b0 + 256)
    e4t = sb("e4t", [128, 4], F32, b0 + 288)
    alpha16 = sb("alpha16", [128, 4], BF16, b0 + 320)
    wq_bP = sb("wq_bPs", [128, NCH], F32, b0 + 384)
    wo_bP = sb("wo_bPs", [128, NCH], F32, b0 + 512)
    bb_P = sb("bb_Ps", [128, NCH], F32, b0 + 640)
    identB = sb("identBs", [128, 128], BF16, b0 + 768)
    b2 = 196 * KB
    ones_row = sb("ones_row", [1, 128], BF16, b2 + 0)
    csk16 = sb("csk16", [1, 128], BF16, b2 + 256)
    csv16 = sb("csv16", [1, 128], BF16, b2 + 512)
    identF4 = sb("identF4s", [4, 4], F32, b2 + 768)
    identF16 = sb("identF16s", [16, 16], F32, 206 * KB)
    iotaf = sb("iotaf", [16, 1], F32, b2 + 832)
    acts16 = sb("acts16", [16, 4], I32, b2 + 864)
    S_f = sb("S_f", [16, 4], F32, b2 + 896)
    S_e16 = sb("S_e16", [16, 4], BF16, b2 + 928)
    S_fT = sb("S_fT", [4, 16], F32, b2 + 960)
    S_eT16 = sb("S_eT16", [4, 16], BF16, b2 + 1024)
    iotaR4 = sb("iotaR4s", [4, 16], F32, b2 + 1056)
    R2b = sb("R2b", [16, 128], BF16, b2 + 1152)
    lpv = sb("lpv", [16, 1], F32, b2 + 1408)
    lp16 = sb("lp16", [16, 1], BF16, b2 + 1440)
    lp4 = sb("lp4", [4, 1], F32, b2 + 1472)
    actf4 = sb("actf4", [4, 1], F32, b2 + 1504)
    rbv = sb("rbv", [16, 1], F32, b2 + 1536)
    prv = sb("prv", [16, 1], F32, b2 + 1568)
    ftiny = sb("ftiny", [1, 16], F32, b2 + 1600)
    krow = sb("krow", [1, 8], F32, b2 + 1920)
    qrow = sb("qrow", [1, 64], F32, 207 * KB)
    btiny = sb("btiny", [1, 16], BF16, b2 + 1664)
    iota16b = sb("iota16b", [16, 1], BF16, b2 + 1696)
    eps12c = sb("eps12c", [16, 1], F32, b2 + 1792)
    neg1c = sb("neg1c", [128, 1], F32, b2 + 1856)
    canary = sb("canary", [128, 512], F32, 196 * KB)
    rz16 = sb("rz16", [1, 512], BF16, 198 * KB)
    rlo16 = sb("rlo16", [1, 512], BF16, 199 * KB)
    rzf = sb("rzf", [1, 512], F32, 200 * KB)
    rzh = sb("rzh", [1, 512], F32, 202 * KB)
    rz16b = sb("rz16b", [1, 512], BF16, 156 * KB)
    rlo16b = sb("rlo16b", [1, 512], BF16, 157 * KB)
    rzfb = sb("rzfb", [1, 512], F32, 150 * KB)
    rzhb = sb("rzhb", [1, 512], F32, 152 * KB)
    rzbS2 = sb("rzbS2", [128, 512], F32, 154 * KB)
    lnzr = sb("lnzr", [1, 512], F32, 204 * KB)
    rzbS = sb("rzbS", [128, 512], F32, 144 * KB)
    EPS5, EPS12, NEGH, POSH, NEG1, KM2, QM2, TSC, CSC = range(9)

    pA = [nc.alloc_psum_tensor(f"pA{i}", [128, 1024], F32) for i in range(4)]

    def pb_view(i):
        return pA[i // 2].ap()[:, (i % 2) * 512:(i % 2) * 512 + 512]
    pb = [pb_view(i) for i in range(8)]

    def pbf(i):
        return pA[i // 2].ap().bitcast(BF16)[:, (i % 2) * 1024:(i % 2) * 1024 + 1024]

    s_p0 = nc.alloc_semaphore("s_p0")
    s_ld = nc.alloc_semaphore("s_ld")
    s_xb = [nc.alloc_semaphore("s_xb0"), nc.alloc_semaphore("s_xb1")]
    s_xq = [nc.alloc_semaphore("s_xq0"), nc.alloc_semaphore("s_xq1")]
    s_wq = [nc.alloc_semaphore("s_wq0"), nc.alloc_semaphore("s_wq1")]
    s_wo = [nc.alloc_semaphore("s_wo0"), nc.alloc_semaphore("s_wo1")]
    s_wb = nc.alloc_semaphore("s_wb")
    s_mk = [nc.alloc_semaphore("s_mk0"), nc.alloc_semaphore("s_mk1")]
    s_xr = [nc.alloc_semaphore(f"s_xr{i}") for i in range(4)]
    s_st = nc.alloc_semaphore("s_st")
    s_so = [nc.alloc_semaphore("s_so0"), nc.alloc_semaphore("s_so1")]
    m_pe = nc.alloc_semaphore("m_pe")
    m_gp = nc.alloc_semaphore("m_gp")
    m_pn = nc.alloc_semaphore("m_pn")
    m_sn = nc.alloc_semaphore("m_sn")
    m_vn = nc.alloc_semaphore("m_vn")
    a_sc = nc.alloc_semaphore("a_sc")
    v_ve = nc.alloc_semaphore("v_ve")

    Q = {"sync": [], "gpsimd": [], "tensor": [], "vector": [], "scalar": []}
    C = {"pe": 0, "sc": 0, "ve": 0, "gp": 0, "pn": 0, "sn": 0, "vn": 0}
    DC = {id(s): 0 for s in [s_p0, s_ld, s_xb[0], s_xb[1], s_xq[0], s_xq[1],
                             s_wq[0], s_wq[1], s_wo[0], s_wo[1], s_wb,
                             s_mk[0], s_mk[1], s_xr[0], s_xr[1], s_xr[2], s_xr[3],
                             s_st, s_so[0], s_so[1]]}
    SEMOBJ = {"pe": m_pe, "sc": a_sc, "ve": v_ve, "gp": m_gp, "pn": m_pn, "sn": m_sn, "vn": m_vn}

    ENG2C = {"vector": "ve", "scalar": "sc", "gpsimd": "gp"}

    def op(eng, build, waits=(), inc=None):
        w = [(s, v) for (s, v) in waits if v > 0]
        auto = ENG2C.get(eng)
        if auto is not None:
            if C[auto] > 0:
                w.append((SEMOBJ[auto], C[auto]))
            C[auto] += 1
            tgt = SEMOBJ[auto]
        elif inc:
            C[inc] += 1
            tgt = SEMOBJ[inc]
        else:
            tgt = None
        w = tuple(w)

        def fn(engine, w=w, build=build, tgt=tgt):
            for (s, v) in w:
                engine.wait_ge(s, v)
            ins = build(engine)
            if tgt is not None and ins is not None:
                ins.then_inc(tgt, 1)
        Q[eng].append(fn)
        return C[auto] if auto is not None else (C[inc] if inc else None)

    def dma(eng, out_ap, in_ap, sem, waits=()):
        DC[id(sem)] += 16
        w = tuple((s, v) for (s, v) in waits if v > 0)

        def fn(engine, w=w, out_ap=out_ap, in_ap=in_ap, sem=sem):
            for (s, v) in w:
                engine.wait_ge(s, v)
            engine.dma_start(out=out_ap, in_=in_ap).then_inc(sem, 16)
        Q[eng].append(fn)
        return DC[id(sem)]

    for (t_, s_) in ((wq_bP, d_wqb), (wo_bP, d_wob), (bb_P, d_bbb), (wkb, d_wkb),
                     (wvb, d_wvb), (identB, d_identB), (identF4, d_identF4), (identF16, d_identF16),
                     (iotaf, d_iota), (iota16b, d_iotab), (iotaR4, d_iotaR4), (R2b, d_R2),
                     (rbv, d_rb), (prv, d_pr)):
        dma("sync", t_[:, :], s_.ap()[:, :], s_ld)
    dma("sync", acts16[:, :], AP(d_act, 0, [[0, 16], [1, 4]]), s_ld)
    LD0 = DC[id(s_ld)]
    for (dst, src, width) in ((wkTb, d_wkT, HD), (wvTb, d_wvT, HD),
                              (ATb, d_AT, 128), (rTb, d_rT, 16)):
        for h4 in range(4):
            dma("gpsimd", dst[:, h4 * 8 * width:(h4 + 1) * 8 * width],
                AP(src, h4 * 8 * 128 * width,
                   [[width, 128], [128 * width, 8], [1, width]]), s_p0)
    P0C = DC[id(s_p0)]

    def vset(ap_, val):
        op("vector", lambda e, a=ap_, v=val: e.memset(a, v))
    vset(ones_col[:, :], 1.0)
    vset(ones_row[:, :], 1.0)
    vset(escale[:, :], RSQ)
    vset(ftiny[:, EPS5:EPS5 + 1], 1e-5)
    vset(ftiny[:, EPS12:EPS12 + 1], 1e-12)
    vset(eps12c[:, :], 1e-12)
    vset(neg1c[:, :], -1.0)
    vset(ftiny[:, NEGH:NEGH + 1], -0.5)
    vset(ftiny[:, POSH:POSH + 1], 0.5)
    vset(ftiny[:, NEG1:NEG1 + 1], -1.0)
    vset(ftiny[:, KM2:KM2 + 1], 0.0)
    VE_CONST = op("vector", lambda e: e.memset(ftiny[:, QM2:QM2 + 1], 0.0), inc="ve")
    op("vector", lambda e: e.tensor_scalar(S_f[:, :], acts16[:, :], iotaf[:, :],
                                           None, ALU.is_equal), waits=[(s_ld, LD0)])
    VE_SE = op("vector", lambda e: e.tensor_scalar(S_e16[:, :], acts16[:, :],
                                                   iotaf[:, :], None, ALU.is_equal),
               inc="ve")
    if DEBUG:
        dma("sync", d_dbg["dbg_Se16e"].ap()[:, :], S_e16[:, :], s_st,
            waits=[(v_ve, VE_SE)])
        dma("sync", d_dbg["dbg_can1"].ap()[:, :], canary[:, :], s_st)
    for dc in range(NCH):
        op("tensor", lambda e, dc=dc: e.matmul(
            pb[6][0:1, 0:128], ones_col[:, :], wkTb[:, dc * HD:(dc + 1) * HD],
            start=(dc == 0), stop=(dc == NCH - 1)),
           waits=[(s_p0, P0C), (v_ve, VE_CONST)] if dc == 0 else ())
    for dc in range(NCH):
        PE_CS = op("tensor", lambda e, dc=dc: e.matmul(
            pb[7][0:1, 0:128], ones_col[:, :], wvTb[:, dc * HD:(dc + 1) * HD],
            start=(dc == 0), stop=(dc == NCH - 1)),
            inc="pe" if dc == NCH - 1 else None)
    op("vector", lambda e: e.tensor_copy(csk16[:, :], pb[6][0:1, 0:128]),
       waits=[(m_pe, PE_CS)])
    VE_CS = op("vector", lambda e: e.tensor_copy(csv16[:, :], pb[7][0:1, 0:128]),
               inc="ve")

    TRK = {"stats_pe": 0, "sq_ve": {}, "kv_pe": 0, "t1_ve": {}, "tr_pe": {},
           "nq_ve": {},
           "tr_ve": {-1: VE_CS, 3: VE_CS}, "stats_ve": 0, "rb_ve": 0, "nb_ve": 0,
           "kv_ve": 0, "k2_ve": VE_CS, "x2_pe": 0, "rstd_sc": 0}

    def ln_tile(tile, XBuf, X2Buf, sxb_sem, sxb_cnt, rstdB_t, nmrB_t,
                rows, brow, kv, first, sq_pe_wait=0):
        t = tile
        mS, e2S, varr, scr = (rows[:, 0:512], rows[:, 512:1024],
                              rows[:, 1024:1536], rows[:, 1536:2048])
        rstd16, nmr16 = brow[:, 0:512], brow[:, 512:1024]
        for dc in range(NCH):
            wv = []
            if dc == 0:
                wv = [(sxb_sem, sxb_cnt), (m_pe, max(TRK["x2_pe"], sq_pe_wait))]
            TRK["sq_ve"][dc] = op("vector", lambda e, dc=dc: e.tensor_tensor(
                X2Buf[:, dc * 512:(dc + 1) * 512], XBuf[:, dc * 512:(dc + 1) * 512],
                XBuf[:, dc * 512:(dc + 1) * 512], ALU.mult), waits=wv)
        for dc in range(NCH):
            w = []
            if dc == 0:
                w = [(sxb_sem, sxb_cnt), (v_ve, TRK["stats_ve"])]
                if first:
                    w.append((v_ve, VE_CONST))
            op("tensor", lambda e, dc=dc: e.matmul(
                pb[2][0:1, :], ones_col[:, :], XBuf[:, dc * 512:(dc + 1) * 512],
                start=(dc == 0), stop=(dc == NCH - 1)), waits=w)
        for dc in range(NCH):
            wv = []
            if dc % 4 == 0:
                wv = [(v_ve, TRK["sq_ve"][min(dc + 3, NCH - 1)])]
            TRK["x2_pe"] = op("tensor", lambda e, dc=dc: e.matmul(
                pb[3][0:1, :], ones_col[:, :], X2Buf[:, dc * 512:(dc + 1) * 512],
                start=(dc == 0), stop=(dc == NCH - 1)),
                waits=wv, inc="pe")
        PE_STATS = TRK["x2_pe"]
        op("vector", lambda e: e.tensor_scalar(mS, pb[2][0:1, :], 1.0 / D, None,
                                               ALU.mult),
           waits=[(m_pe, PE_STATS), (a_sc, TRK["rstd_sc"])])
        op("vector", lambda e: e.tensor_scalar(e2S, pb[3][0:1, :], 1.0 / D, None,
                                               ALU.mult))
        op("vector", lambda e: e.tensor_tensor(varr, mS, mS, ALU.mult))
        VE_VAR = op("vector", lambda e: e.tensor_tensor(varr, e2S, varr, ALU.subtract))
        TRK["stats_ve"] = VE_VAR
        op("scalar", lambda e: e.activation(scr, varr, AF.Ln,
                                            bias=ftiny[:, EPS5:EPS5 + 1]),
           waits=[(v_ve, VE_VAR)])
        SC_R16 = op("scalar", lambda e: e.activation(
            rstd16, scr, AF.Exp, scale=ftiny[:, NEGH:NEGH + 1]), inc="sc")
        SC_RF = op("scalar", lambda e: e.activation(
            varr, scr, AF.Exp, scale=ftiny[:, NEGH:NEGH + 1]), inc="sc")
        TRK["rstd_sc"] = SC_RF
        op("vector", lambda e: e.tensor_tensor(scr, mS, varr, ALU.mult),
           waits=[(a_sc, SC_RF)])
        VE_NMR = op("vector", lambda e: e.tensor_scalar(nmr16, scr, -1.0, None,
                                                        ALU.mult))
        PE_BC = op("tensor", lambda e: e.matmul(
            pb[4][:, :], ones_row[:, :], rstd16, start=True, stop=True),
            waits=[(a_sc, SC_R16), (v_ve, TRK["rb_ve"])], inc="pe")
        VE_RB = op("vector", lambda e, r=rstdB_t: e.tensor_copy(r[:, :], pb[4][:, :]),
                   waits=[(m_pe, PE_BC)])
        TRK["rb_ve"] = VE_RB
        if not kv:
            PE_BC2 = op("tensor", lambda e: e.matmul(
                pb[5][:, :], ones_row[:, :], nmr16, start=True, stop=True),
                waits=[(v_ve, VE_NMR), (v_ve, TRK["nb_ve"])], inc="pe")
            VE_NB = op("vector", lambda e, r=nmrB_t: e.tensor_copy(r[:, :], pb[5][:, :]),
                       waits=[(m_pe, PE_BC2)])
            TRK["nb_ve"] = VE_NB

        if kv:
            for dc in range(NCH):
                wv = [(v_ve, VE_RB)] if dc == 0 else []
                if dc == 0:
                    wv.append((m_pe, TRK["kv_pe"]))
                TRK["t1_ve"][dc] = op("vector", lambda e, dc=dc: e.tensor_tensor(
                    T1F[:, dc * 512:(dc + 1) * 512], XBuf[:, dc * 512:(dc + 1) * 512],
                    rstdB_t[:, :], ALU.mult), waits=wv)
            for dc in range(NCH):
                w = []
                if dc % 4 == 0:
                    w = [(v_ve, TRK["t1_ve"][min(dc + 3, NCH - 1)])]
                if dc == 0 and first:
                    w += [(v_ve, VE_CS), (s_p0, P0C)]
                op("tensor", lambda e, dc=dc: e.matmul(
                    pb[0][:, :], wkTb[:, dc * HD:(dc + 1) * HD],
                    T1F[:, dc * 512:(dc + 1) * 512],
                    start=(dc == 0), stop=False), waits=w)
                op("tensor", lambda e, dc=dc: e.matmul(
                    pb[1][:, :], wvTb[:, dc * HD:(dc + 1) * HD],
                    T1F[:, dc * 512:(dc + 1) * 512],
                    start=(dc == 0), stop=False))
            op("tensor", lambda e: e.matmul(pb[0][:, :], csk16[:, :], nmr16,
                                            start=False, stop=True),
               waits=[(v_ve, VE_NMR)])
            PE_KV = op("tensor", lambda e: e.matmul(pb[1][:, :], csv16[:, :], nmr16,
                                                    start=False, stop=True), inc="pe")
            TRK["kv_pe"] = PE_KV
            op("vector", lambda e, t=t: e.tensor_scalar(
                kT[:, t * 512:(t + 1) * 512], pb[0][:, :], wkb[:, :], None, ALU.add),
               waits=[(m_pe, PE_KV)])
            VE_KV = op("vector", lambda e: e.tensor_scalar(
                p1VTs[:, :], pb[1][:, :], wvb[:, :], None, ALU.add))
            TRK["kv_ve"] = VE_KV
            for j in range(4):
                TRK["tr_pe"][j] = op("tensor", lambda e, j=j: e.transpose(
                    pbf(6)[:, 0:128], p1VTs[:, j * 128:(j + 1) * 128], identB[:, :]),
                    waits=[(v_ve, VE_KV), (v_ve, TRK["tr_ve"][(j - 1) % 4])], inc="pe")
                TRK["tr_ve"][j] = op("vector", lambda e, t=t, j=j: e.tensor_copy(
                    Vt[:, (4 * t + j) * 128:(4 * t + j + 1) * 128], pbf(6)[:, 0:128]),
                    waits=[(m_pe, TRK["tr_pe"][j])])
            VE_K2B = op("vector", lambda e, t=t: e.tensor_tensor(
                p1k2b[:, :], kT[:, t * 512:(t + 1) * 512],
                kT[:, t * 512:(t + 1) * 512], ALU.mult))
            PE_K2 = op("tensor", lambda e: e.matmul(
                pb[7][0:1, :], ones_col[:, :], p1k2b[:, :], start=True, stop=True),
                waits=[(v_ve, VE_K2B), (v_ve, TRK["k2_ve"])], inc="pe")
            TRK["k2_ve"] = op("vector", lambda e, t=t: e.reduce_max(
                krow[:, t:t + 1], pb[7][0:1, :], axis=mybir.AxisListType.X),
                waits=[(m_pe, PE_K2)])
        else:
            for dc in range(NCH):
                TRK["nq_ve"][dc] = op("vector", lambda e, dc=dc, t=t: e.tensor_tensor(
                    xnbq[:, dc * NQ + t * 512: dc * NQ + t * 512 + 512],
                    XBuf[:, dc * 512:(dc + 1) * 512], rstdB_t[:, :], ALU.mult))
                op("vector", lambda e, dc=dc, t=t: e.tensor_tensor(
                    xnbq[:, dc * NQ + t * 512: dc * NQ + t * 512 + 512],
                    xnbq[:, dc * NQ + t * 512: dc * NQ + t * 512 + 512],
                    nmrB_t[:, :], ALU.add))

    VE_XB_DONE = [0, 0]
    for t in range(8):
        par = t % 2
        w = [(v_ve, VE_XB_DONE[par])] if t >= 2 else ()
        for dc in range(NCH):
            dma("gpsimd", XB[par][:, dc * 512:(dc + 1) * 512],
                AP(d_xT, dc * 128 * T + t * 512, [[T, 128], [1, 512]]),
                s_xb[par], waits=w if dc == 0 else ())
        ln_tile(t, XB[par], X2F, s_xb[par], DC[id(s_xb[par])], p1rstdB,
                None, p1rows, p1brow, kv=True, first=(t == 0))
        VE_XB_DONE[par] = C["ve"]
    P1_PE, P1_SC, P1_VE = C["pe"], C["sc"], C["ve"]
    X2PE_T7 = TRK["x2_pe"]
    DBG_P1 = 0
    if DEBUG:
        dma("sync", d_dbg["dbg_wk"].ap()[:, :], wkTb[:, :], s_st,
            waits=[(v_ve, P1_VE), (a_sc, P1_SC)])
        dma("sync", d_dbg["dbg_T1"].ap()[:, :], p1T1[0][:, :], s_st)
        dma("sync", d_dbg["dbg_prB"].ap()[:, :], p1rstdB[:, :], s_st)
        dma("sync", d_dbg["dbg_csk"].ap()[:, :], csk16[:, :], s_st)
        DBG_P1 = DC[id(s_st)]

    VE_XQ_DONE = 0
    NQ0 = {}
    for t in range(2):
        par = t % 2
        for dc in range(NCH):
            w = []
            if t == 0:
                if dc == 0:
                    w = [(m_pe, X2PE_T7)]
                    if DEBUG:
                        w.append((s_st, DBG_P1))
            else:
                w = [(v_ve, NQ0[dc])]
            dma("gpsimd", XQB[:, dc * 512:(dc + 1) * 512],
                AP(d_xqT, dc * 128 * NQ + t * 512, [[NQ, 128], [1, 512]]),
                s_xq[par], waits=w)
        ln_tile(t, XQB, X2F2, s_xq[par], DC[id(s_xq[par])], p2rstdB, p2nmrB,
                p2rows, p2brow, kv=False, first=False,
                sq_pe_wait=(P1_PE if t == 0 else 0))
        NQ0 = dict(TRK["nq_ve"])
        VE_XQ_DONE = C["ve"]
    P2A_VE = C["ve"]

    VE_Z = {}
    for qt in range(2):
        for dc in range(NCH):
            w = []
            if dc == 0:
                w = [(v_ve, max(P2A_VE, P1_VE) if qt == 0 else VE_Z[0])]
            op("tensor", lambda e, qt=qt, dc=dc: e.matmul(
                pb[0][:, :], ATb[:, dc * 128:(dc + 1) * 128],
                xnbq[:, dc * NQ + qt * 512: dc * NQ + qt * 512 + 512],
                start=(dc == 0), stop=(dc == NCH - 1)),
               waits=w, inc="pe" if dc == NCH - 1 else None)
        PE_Z = C["pe"]
        VE_Z[qt] = op("vector", lambda e, qt=qt: e.tensor_copy(
            z16[:, qt * 512:(qt + 1) * 512], pb[0][:, :]),
            waits=[(m_pe, PE_Z)], inc="ve")
    SC_LGv = {}
    for qt in range(2):
        for dc in range(NCH):
            w = []
            if dc == 0 and qt == 1:
                w = [(v_ve, SC_LGv[0])]
            op("tensor", lambda e, qt=qt, dc=dc: e.matmul(
                pb[1][0:16, :], rTb[:, dc * 16:(dc + 1) * 16],
                xnbq[:, dc * NQ + qt * 512: dc * NQ + qt * 512 + 512],
                start=(dc == 0), stop=(dc == NCH - 1)),
               waits=w, inc="pe" if dc == NCH - 1 else None)
        PE_R = C["pe"]
        SC_LGv[qt] = op("vector", lambda e, qt=qt: e.tensor_scalar(
            lg16[0:16, qt * 512:(qt + 1) * 512], pb[1][0:16, :], rbv[:, :],
            None, ALU.add), waits=[(m_pe, PE_R), (s_ld, LD0)], inc="ve")
    VE_LG = SC_LGv[1]
    SC_LP = op("scalar", lambda e: e.activation(lp16[:, :], prv[:, :], AF.Ln,
                                                bias=eps12c[:, :]),
               waits=[(s_ld, LD0), (v_ve, VE_CONST)], inc="sc")
    PE_SEL = op("tensor", lambda e: e.matmul(pb[2][0:4, 0:1], S_e16[:, :],
                                             lp16[:, :], start=True, stop=True),
                waits=[(a_sc, SC_LP), (v_ve, VE_SE)], inc="pe")
    VE_A4 = op("vector", lambda e: e.tensor_copy(lp4[:, :], pb[2][0:4, 0:1]),
               waits=[(m_pe, PE_SEL)], inc="ve")
    PE_SFT = op("tensor", lambda e: e.transpose(pb[5][0:4, 0:16], S_f[:, :],
                                                identF16[:, :]),
                waits=[(v_ve, VE_SE)], inc="pe")
    VE_SET = op("vector", lambda e: e.tensor_copy(S_eT16[:, :], pb[5][0:4, 0:16]),
                waits=[(m_pe, PE_SFT)], inc="ve")
    VE_PRE_prev = 0
    SC_E4_prev = 0
    VE_ALT_prev = 0
    for qt in range(2):
        PE_SL = op("tensor", lambda e, qt=qt: e.matmul(
            pb[4][0:4, :], S_e16[:, :], lg16[0:16, qt * 512:(qt + 1) * 512],
            start=True, stop=True),
            waits=[(v_ve, VE_LG), (v_ve, VE_SET), (v_ve, VE_PRE_prev)], inc="pe")
        VE_PRE = op("vector", lambda e: e.tensor_scalar(
            pre_f[:, :], pb[4][0:4, :], lp4[:, :], None, ALU.add),
            waits=[(m_pe, PE_SL)], inc="ve")
        VE_PRE_prev = VE_PRE
        for sx in range(4):
            PE_T1 = op("tensor", lambda e, sx=sx: e.transpose(
                pb[5][0:128, 0:4], pre_f[:, sx * 128:(sx + 1) * 128], identF4[:, :]),
                waits=[(v_ve, VE_PRE), (a_sc, SC_E4_prev)], inc="pe")
            VE_MX = op("vector", lambda e: e.reduce_max(
                nmxc[:, :], pb[5][0:128, 0:4], axis=mybir.AxisListType.X,
                negate=True), waits=[(m_pe, PE_T1)], inc="ve")
            SC_E4 = op("scalar", lambda e: e.activation(
                e4t[:, :], pb[5][0:128, 0:4], AF.Exp, bias=nmxc[:, :]),
                waits=[(v_ve, VE_MX)], inc="sc")
            SC_E4_prev = SC_E4
            VE_S4 = op("vector", lambda e: e.reduce_sum(
                s4c[:, :], e4t[:, :], axis=mybir.AxisListType.X),
                waits=[(a_sc, SC_E4)], inc="ve")
            op("scalar", lambda e: e.activation(mxc[:, :], s4c[:, :], AF.Ln),
               waits=[(v_ve, VE_S4)])
            SC_RS = op("scalar", lambda e: e.activation(
                rsc[:, :], mxc[:, :], AF.Exp, scale=neg1c[:, :]), inc="sc")
            VE_AL = op("vector", lambda e: e.tensor_scalar(
                alpha16[:, :], e4t[:, :], rsc[:, :], None, ALU.mult),
                waits=[(a_sc, SC_RS)], inc="ve")
            PE_T2 = op("tensor", lambda e: e.transpose(
                pbf(6)[0:4, 0:128], alpha16[:, :], identB[:, :]),
                waits=[(v_ve, VE_AL), (v_ve, VE_ALT_prev)], inc="pe")
            VE_ALT_prev = op("vector", lambda e, qt=qt, sx=sx: e.tensor_copy(
                alphaT16[:, qt * 512 + sx * 128: qt * 512 + sx * 128 + 128],
                pbf(6)[0:4, 0:128]), waits=[(m_pe, PE_T2)], inc="ve")
    VE_ALT = VE_ALT_prev
    VE_AE = {}
    for qt in range(2):
        PE_AE = op("tensor", lambda e, qt=qt: e.matmul(
            pb[7][0:16, :], S_eT16[:, :], alphaT16[:, qt * 512:(qt + 1) * 512],
            start=True, stop=True),
            waits=[(v_ve, VE_ALT if qt == 0 else VE_AE[0]),
                   (v_ve, TRK["k2_ve"])], inc="pe")
        VE_AE[qt] = op("vector", lambda e, qt=qt: e.tensor_copy(
            aE16[0:16, qt * 512:(qt + 1) * 512], pb[7][0:16, :]),
            waits=[(m_pe, PE_AE)], inc="ve")
    VE_ZP = 0
    VE_AB_prev = 0
    for qt in range(2):
        PE_AB = op("tensor", lambda e, qt=qt: e.matmul(
            pb[0][:, :], R2b[:, :], aE16[0:16, qt * 512:(qt + 1) * 512],
            start=True, stop=True),
            waits=[(v_ve, VE_AE[1]), (v_ve, VE_AB_prev)], inc="pe")
        VE_AB = op("vector", lambda e: e.tensor_copy(aB16[:, :], pb[0][:, :]),
                   waits=[(m_pe, PE_AB)], inc="ve")
        VE_AB_prev = VE_AB
        VE_ZP = op("vector", lambda e, qt=qt: e.tensor_tensor(
            zp16[:, qt * 512:(qt + 1) * 512], z16[:, qt * 512:(qt + 1) * 512],
            aB16[:, :], ALU.mult), waits=[(v_ve, VE_AB)], inc="ve")
    P2B_PE = C["pe"]

    if DEBUG:
        dma("sync", d_dbg["dbg_rows"].ap()[:, :], p2rows[:, :], s_st, waits=[(v_ve, VE_ZP)])
        dma("sync", d_dbg["dbg_brow"].ap()[:, :], p2brow[:, :], s_st)
        dma("sync", d_dbg["dbg_rB"].ap()[:, :], p2rstdB[:, :], s_st)
        dma("sync", d_dbg["dbg_nB"].ap()[:, :], p2nmrB[:, :], s_st)
        dma("sync", d_dbg["dbg_z"].ap()[:, :], z16[:, :], s_st)
        dma("sync", d_dbg["dbg_pre"].ap()[:, :], pre_f[:, :], s_st)
        dma("sync", d_dbg["dbg_lp4"].ap()[:, :], lp4[:, :], s_st)
        dma("sync", d_dbg["dbg_Sf"].ap()[:, :], S_f[:, :], s_st)
        dma("sync", d_dbg["dbg_e4"].ap()[:, :], e4t[:, :], s_st)
        dma("sync", d_dbg["dbg_lpv"].ap()[:, :], lpv[:, :], s_st)
        dma("sync", d_dbg["dbg_lp16"].ap()[:, :], lp16[:, :], s_st)
        dma("sync", d_dbg["dbg_Se16"].ap()[:, :], S_e16[:, :], s_st)
        dma("sync", d_dbg["dbg_a4"].ap()[:, :], lp4[:, :], s_st)
        dma("sync", d_dbg["dbg_can2"].ap()[:, :], canary[:, :], s_st)
        dma("sync", d_dbg["dbg_zp"].ap()[:, :], zp16[:, :], s_st)
        dma("sync", d_dbg["dbg_lg"].ap()[:, :], lg16[:, :], s_st)
        dma("sync", d_dbg["dbg_alT"].ap()[:, :], alphaT16[:, :], s_st)
        dma("sync", d_dbg["dbg_aE"].ap()[:, :], aE16[:, :], s_st)
        dma("sync", d_dbg["dbg_kT"].ap()[:, :], kT[:, :], s_st)
        dma("sync", d_dbg["dbg_Vt"].ap()[:, :], Vt[:, :], s_st)
        dma("sync", d_dbg["dbg_xn"].ap()[:, :], xnbq[:, :], s_st)

    SC_Qog = {}
    SC_Qev = {}
    for og in range(16):
        par = og % 2
        w = []
        if og == 0:
            w = [(m_pe, P2B_PE)]
        elif og >= 2:
            w = [(v_ve, SC_Qog[og - 2])]
        for dc in range(NCH):
            dma("gpsimd", WQP[par][:, dc * 256:(dc + 1) * 256],
                AP(d_wqT, dc * 128 * D + og * 256, [[D, 128], [1, 256]]),
                s_wq[par], waits=w if dc == 0 else ())
        cnt = DC[id(s_wq[par])]
        for oc in range(2):
            for qt in range(2):
                ocg = og * 2 + oc
                bank = 2 * oc + qt
                for dc in range(NCH):
                    ww = []
                    if dc == 0:
                        ww = [(s_wq[par], cnt),
                              (v_ve, VE_ZP if og == 0 else SC_Qev.get((og - 1, bank), 0))]
                    op("tensor", lambda e, dc=dc, oc=oc, qt=qt, bank=bank, par=par:
                       e.matmul(
                        pb[bank][:, :],
                        WQP[par][:, dc * 256 + oc * 128: dc * 256 + oc * 128 + 128],
                        xnbq[:, dc * NQ + qt * 512: dc * NQ + qt * 512 + 512],
                        start=(dc == 0), stop=(dc == NCH - 1)),
                       waits=ww, inc="pe" if dc == NCH - 1 else None)
                PE_Q = C["pe"]
                SC_Qev[(og, bank)] = op("vector", lambda e, ocg=ocg, qt=qt, bank=bank:
                                        e.tensor_scalar(
                    qTat[:, ocg * NQ + qt * 512: ocg * NQ + qt * 512 + 512],
                    pb[bank][:, :], wq_bP[:, ocg:ocg + 1], None, ALU.add),
                    waits=[(m_pe, PE_Q)], inc="ve")
        SC_Qog[og] = SC_Qev[(og, 3)]
    VE_QP = C["ve"]
    VE_Q2 = 0
    PE_Q2 = 0
    for hh in range(H):
        VE_SQH = op("vector", lambda e, hh=hh: e.tensor_tensor(
            q2b[:, :], qTat[:, hh * NQ:(hh + 1) * NQ],
            qTat[:, hh * NQ:(hh + 1) * NQ], ALU.mult),
            waits=[(v_ve, VE_QP), (m_pe, PE_Q2)], inc="ve")
        for qt in range(2):
            PE_Q2 = op("tensor", lambda e, qt=qt: e.matmul(
                pb[4][0:1, :], ones_col[:, :], q2b[:, qt * 512:(qt + 1) * 512],
                start=True, stop=True),
                waits=[(v_ve, VE_SQH), (v_ve, VE_Q2)], inc="pe")
            VE_Q2 = op("vector", lambda e, hh=hh, qt=qt: e.reduce_max(
                qrow[:, 2 * hh + qt:2 * hh + qt + 1], pb[4][0:1, :],
                axis=mybir.AxisListType.X), waits=[(m_pe, PE_Q2)], inc="ve")
    VE_KMX = op("vector", lambda e: e.reduce_max(
        ftiny[:, KM2:KM2 + 1], krow[:, :], axis=mybir.AxisListType.X), inc="ve")
    VE_QMX = op("vector", lambda e: e.reduce_max(
        ftiny[:, QM2:QM2 + 1], qrow[:, :], axis=mybir.AxisListType.X),
        waits=[(v_ve, VE_Q2)], inc="ve")
    op("scalar", lambda e: e.activation(ftiny[:, TSC:TSC + 1],
                                        ftiny[:, KM2:KM2 + 1], AF.Ln),
       waits=[(v_ve, VE_QMX)])
    SC_LNS = op("scalar", lambda e: e.activation(ftiny[:, CSC:CSC + 1],
                                                 ftiny[:, QM2:QM2 + 1], AF.Ln),
                inc="sc")
    VE_C1 = op("vector", lambda e: e.tensor_tensor(
        ftiny[:, TSC:TSC + 1], ftiny[:, TSC:TSC + 1], ftiny[:, CSC:CSC + 1],
        ALU.add), waits=[(a_sc, SC_LNS)], inc="ve")
    SC_C = op("scalar", lambda e: e.activation(
        ftiny[:, CSC:CSC + 1], ftiny[:, TSC:TSC + 1], AF.Exp,
        scale=ftiny[:, POSH:POSH + 1]), waits=[(v_ve, VE_C1)], inc="sc")
    VE_C2 = op("vector", lambda e: e.tensor_scalar(
        btiny[:, 0:1], ftiny[:, CSC:CSC + 1], -RSQ, None, ALU.mult),
        waits=[(a_sc, SC_C)], inc="ve")
    PE_C = op("tensor", lambda e: e.matmul(
        pb[4][0:128, 0:1], ones_row[:, :], btiny[0:1, 0:1], start=True, stop=True),
        waits=[(v_ve, VE_C2)], inc="pe")
    VE_CB = op("vector", lambda e: e.tensor_copy(cbias[:, :], pb[4][:, 0:1]),
               waits=[(m_pe, PE_C)], inc="ve")

    DBG_GATE = 0
    if DEBUG:
        dma("sync", d_dbg["dbg_ft"].ap()[:, :], ftiny[:, :], s_st, waits=[(v_ve, VE_CB)])
        dma("sync", d_dbg["dbg_cb"].ap()[:, :], cbias[:, :], s_st)
        dma("sync", d_dbg["dbg_qT"].ap()[:, :], qTat[:, :], s_st, waits=[(v_ve, VE_QP)])
        DBG_GATE = DC[id(s_st)]

    pq = [pA[0], pA[1]]
    pzavs = [pA[2], pA[3]]
    MK_DONE = [0, 0]
    itctr = [0]
    last_exp_pq = {0: 0, 1: 0}
    last_av_pb = {0: 0, 1: 0, 2: 0, 3: 0}
    last_attn_ve = {0: 0, 1: 0}
    last_rzb_ve = {0: 0, 1: 0}
    zav_pend = []
    first_exp = True
    pend = {"pe": [], "ve": [], "sc": []}
    pendq = []
    DEPTH = {"pe": 1, "ve": 1, "sc": 1}

    def flushpend():
        for k, eng in (("sc", "scalar"), ("pe", "tensor"), ("ve", "vector")):
            if len(pendq) >= DEPTH[k]:
                for qi in range(len(pendq) - DEPTH[k] + 1):
                    for fn in pendq[qi][k]:
                        Q[eng].append(fn)
                    pendq[qi][k] = []

    def rotatepend():
        pendq.append({"pe": pend["pe"], "ve": pend["ve"], "sc": pend["sc"]})
        pend["pe"] = []
        pend["ve"] = []
        pend["sc"] = []
        while pendq and not (pendq[0]["pe"] or pendq[0]["ve"] or pendq[0]["sc"]):
            pendq.pop(0)

    def flushall():
        rotatepend()
        for q in pendq:
            for k, eng in (("sc", "scalar"), ("pe", "tensor"), ("ve", "vector")):
                for fn in q[k]:
                    Q[eng].append(fn)
        pendq.clear()

    def pe_pend(build, waits=(), inc=None):
        w = tuple((sm, v) for (sm, v) in waits if v > 0)
        tgt = None
        if inc:
            C[inc] += 1
            tgt = SEMOBJ[inc]

        def fn(engine, w=w, build=build, tgt=tgt):
            for (sm, v) in w:
                engine.wait_ge(sm, v)
            ins = build(engine)
            if tgt is not None and ins is not None:
                ins.then_inc(tgt, 1)
        pend["pe"].append(fn)
        return C[inc] if inc else None

    def sc_pend(build, waits=(), inc=None):
        w = tuple((sm, v) for (sm, v) in waits if v > 0)
        tgt = None
        if inc:
            C[inc] += 1
            tgt = SEMOBJ[inc]

        def fn(engine, w=w, build=build, tgt=tgt):
            for (sm, v) in w:
                engine.wait_ge(sm, v)
            ins = build(engine)
            if tgt is not None and ins is not None:
                ins.then_inc(tgt, 1)
        pend["sc"].append(fn)
        return C[inc] if inc else None

    def ve_pend(build, waits=()):
        w = [(sm, v) for (sm, v) in waits if v > 0]
        if C["vn"] > 0:
            w.append((m_vn, C["vn"]))
        C["vn"] += 1
        tgt = m_vn
        w = tuple(w)

        def fn(engine, w=w, build=build, tgt=tgt):
            for (sm, v) in w:
                engine.wait_ge(sm, v)
            ins = build(engine)
            if ins is not None:
                ins.then_inc(tgt, 1)
        pend["ve"].append(fn)
        return C["vn"]

    norm_jobs = []

    def emit_norm():
        if not norm_jobs:
            return
        qoff, zpar, pzav, npair_ = norm_jobs.pop(0)
        while zav_pend:
            emit_zav(zav_pend.pop(0))
        PE_AVL = last_av_pb[(npair_ - 1) % 4]
        rzv = rz16 if zpar == 0 else rz16b
        rzfv = rzf if zpar == 0 else rzfb
        rzhv = rzh if zpar == 0 else rzhb
        rlov = rlo16 if zpar == 0 else rlo16b
        rzbv = rzbS if zpar == 0 else rzbS2
        SN_LN = sc_pend(lambda e, pzav=pzav: e.activation(
            lnzr[:, :], pzav.ap()[0:1, 0:512], AF.Ln),
            waits=[(m_pe, PE_AVL), (m_sn, C["sn"])], inc="sn")
        last_lnz = sc_pend(lambda e, rzv=rzv: e.activation(
            rzv[:, :], lnzr[:, :], AF.Exp, scale=ftiny[:, NEG1:NEG1 + 1]),
            waits=[(m_sn, SN_LN)], inc="sn")
        PE_RB = pe_pend(lambda e, pzav=pzav, rzv=rzv: e.matmul(
            pzav.ap()[:, 0:512], ones_row[:, :], rzv[:, :],
            start=True, stop=True),
            waits=[(m_sn, last_lnz)], inc="pn")
        last_rzb_ve[zpar] = ve_pend(lambda e, pzav=pzav, rzbv=rzbv: e.tensor_copy(
            rzbv[:, :], pzav.ap()[:, 0:512]),
            waits=[(m_pn, PE_RB)])
        out_ap = AP(qTat, qoff, [[NCH * NQ, 128], [NQ, 4], [1, 128]])
        last_attn_ve[zpar] = ve_pend(lambda e, out_ap=out_ap, pzav=pzav,
                                     rzbv=rzbv: e.tensor_tensor(
            out_ap, pzav.ap()[:, 512:1024], rzbv[:, :], ALU.mult))
        rotatepend()

    for g in range(NSLOT):
        nkc = 4 * g + 4
        npair = nkc // 2
        par = g % 2
        wm0 = [(v_ve, MK_DONE[par])] if g >= 2 else [(v_ve, max(VE_ZP, VE_QP)),
                                                     (m_pe, P2B_PE)]
        for j in range(4):
            dma("sync", MB[par][:, j * 512:(j + 1) * 512],
                d_masks.ap()[4 * g + j, :, :], s_mk[par], waits=wm0 if j == 0 else ())
        mkcnt = DC[id(s_mk[par])]
        for hg in range(NHG):
            qoff = (4 * hg) * NQ + g * 128
            rhs_q = AP(qTat, qoff, [[NCH * NQ, 128], [NQ, 4], [1, 128]])
            zpar = itctr[0] % 2
            pzav = pzavs[zpar]

            def emit_zav(job):
                p, dep, nkc_, pz, zp = job
                wz = [dep, (m_vn, last_rzb_ve[zp])] if p == 0 else [dep]
                op("tensor", lambda e, p=p, nkc_=nkc_, pz=pz: e.matmul(
                    pz.ap()[0:1, 0:512], ones_col[:, :],
                    Pbp[p % 4][:, 0:512],
                    start=(2 * p == 0), stop=False), waits=wz)
                op("tensor", lambda e, p=p, nkc_=nkc_, pz=pz: e.matmul(
                    pz.ap()[0:1, 0:512], ones_col[:, :],
                    Pbp[p % 4][:, 512:1024],
                    start=False, stop=(2 * p + 1 == nkc_ - 1)))
                wa = [(m_vn, last_attn_ve[zp])] if p == 0 else []
                op("tensor", lambda e, p=p, nkc_=nkc_, pz=pz: e.matmul(
                    pz.ap()[:, 512:1024],
                    Vt[:, (2 * p) * 128:(2 * p) * 128 + 128],
                    Pbp[p % 4][:, 0:512],
                    start=(2 * p == 0), stop=False), waits=wa)
                last_av_pb[p % 4] = op("tensor",
                                       lambda e, p=p, nkc_=nkc_, pz=pz: e.matmul(
                    pz.ap()[:, 512:1024],
                    Vt[:, (2 * p + 1) * 128:(2 * p + 1) * 128 + 128],
                    Pbp[p % 4][:, 512:1024],
                    start=False, stop=(2 * p + 1 == nkc_ - 1)), inc="pe")

            for p in range(npair):
                pi = p % 2
                t = pq[pi]
                ws = [(a_sc, last_exp_pq[pi])]
                if first_exp:
                    ws.append((v_ve, VE_CB))
                op("tensor", lambda e, p=p, t=t, rhs_q=rhs_q: e.matmul(
                    t.ap()[:, 0:512], kT[:, (2 * p) * 128:(2 * p) * 128 + 128],
                    rhs_q, start=True, stop=True), waits=ws)
                PE_S = op("tensor", lambda e, p=p, t=t, rhs_q=rhs_q: e.matmul(
                    t.ap()[:, 512:1024], kT[:, (2 * p + 1) * 128:(2 * p + 1) * 128 + 128],
                    rhs_q, start=True, stop=True), inc="pe")
                if p == 0:
                    flushpend()
                we = [(m_pe, PE_S), (m_pe, last_av_pb[p % 4])]
                SC_E = op("scalar", lambda e, p=p, t=t: e.activation(
                    Pbp[p % 4][:, :], t.ap()[:, :], AF.Exp, bias=cbias[:, :],
                    scale=escale[:, :]), waits=we, inc="sc")
                last_exp_pq[pi] = SC_E
                first_exp = False
                dep = (a_sc, SC_E)
                if p >= 2 * g:
                    jj = 2 * p - 4 * g
                    wmm = [(a_sc, SC_E)]
                    if hg == 0 and p == 2 * g:
                        wmm.append((s_mk[par], mkcnt))
                    VE_M = op("vector", lambda e, p=p, jj=jj, par=par: e.tensor_tensor(
                        Pbp[p % 4][:, :], Pbp[p % 4][:, :],
                        MB[par][:, jj * 512:(jj + 2) * 512], ALU.mult),
                        waits=wmm)
                    dep = (v_ve, VE_M)
                zav_pend.append((p, dep, nkc, pzav, zpar))
                if len(zav_pend) > 2:
                    emit_zav(zav_pend.pop(0))
            norm_jobs.append((qoff, zpar, pzav, npair))
            emit_norm()
            itctr[0] += 1
            PE_AVL = last_av_pb[(npair - 1) % 4]
        MK_DONE[par] = C["ve"]
    while norm_jobs:
        emit_norm()
    flushall()
    P3_PE, P3_SC, P3_VE = C["pe"], C["sc"], C["ve"]
    P3_VN = C["vn"]

    if DEBUG:
        dma("sync", d_dbg["dbg_at"].ap()[:, :], qTat[:, :], s_st, waits=[(v_ve, P3_VE)])

    SC_P4og = {}
    VE_P4 = {}
    last_xrs = {0: P3_VE, 1: P3_VE, 2: P3_VE, 3: P3_VE}
    first_p4 = True
    last_sev = {0: 0, 1: 0, 2: 0, 3: 0}
    last_yev = {0: 0, 1: 0, 2: 0, 3: 0}
    last_sg = {0: 0, 1: 0}
    last_ymm = 0
    for og in range(16):
        par = og % 2
        w = []
        if og == 0:
            w = [(v_ve, P3_VE), (m_vn, P3_VN)]
        elif og >= 2:
            w = [(a_sc, SC_P4og[og - 2])]
        for dc in range(NCH):
            dma("gpsimd", WOP[par][:, dc * 256:(dc + 1) * 256],
                AP(d_woT, dc * 128 * D + og * 256, [[D, 128], [1, 256]]),
                s_wo[par], waits=w if dc == 0 else ())
        wocnt = DC[id(s_wo[par])]
        wwb = [(v_ve, P3_VE), (m_vn, P3_VN)] if og == 0 else [(m_pe, last_ymm)]
        for dc in range(NCH):
            dma("gpsimd", WBP[:, dc * 256:(dc + 1) * 256],
                AP(d_WbT, dc * 128 * D + og * 256, [[D, 128], [1, 256]]),
                s_wb, waits=wwb if dc == 0 else ())
        dma("gpsimd", WBP[:, NCH * 256:NCH * 256 + 256],
            AP(d_Bt, og * 256, [[D, 128], [1, 256]]), s_wb)
        wbcnt = DC[id(s_wb)]
        SC_SE = {}
        for oc in range(2):
            for qt in range(2):
                ocg = og * 2 + oc
                bk = 2 * oc + qt
                xw = [(v_ve, last_xrs[bk])]
                if og == 0:
                    xw.append((m_vn, P3_VN))
                xrcnt = dma("sync", XRS[bk][:, :],
                            AP(d_xqT, ocg * 128 * NQ + qt * 512, [[NQ, 128], [1, 512]]),
                            s_xr[bk], waits=xw)
                XRC = xrcnt
                for dc in range(NCH):
                    ws = []
                    if dc == 0:
                        ws = [(s_wo[par], wocnt), (v_ve, last_sev[bk])]
                    op("tensor", lambda e, dc=dc, oc=oc, qt=qt, bk=bk, par=par:
                       e.matmul(
                        pb[bk][:, :],
                        WOP[par][:, dc * 256 + oc * 128: dc * 256 + oc * 128 + 128],
                        qTat[:, dc * NQ + qt * 512: dc * NQ + qt * 512 + 512],
                        start=(dc == 0), stop=(dc == NCH - 1)),
                       waits=ws, inc="pe" if dc == NCH - 1 else None)
                PE_S4 = C["pe"]
                sw = [(m_pe, PE_S4)]
                if og == 0:
                    sw.append((m_vn, P3_VN))
                last_sev[bk] = op("vector", lambda e, ocg=ocg, bk=bk:
                                  e.tensor_scalar(
                    SGT[bk][:, :], pb[bk][:, :], wo_bP[:, ocg:ocg + 1],
                    None, ALU.add), waits=sw)
                SC_SE[(oc, qt)] = (last_sev[bk], XRC)
        for oc in range(2):
            for qt in range(2):
                ocg = og * 2 + oc
                bk = 2 * oc + qt
                ybk = 4 + bk
                for dc in range(NCH):
                    wy = []
                    if dc == 0:
                        wy = [(s_wb, wbcnt), (a_sc, last_yev[bk])]
                    op("tensor", lambda e, dc=dc, oc=oc, qt=qt, ybk=ybk: e.matmul(
                        pb[ybk][:, :],
                        WBP[:, dc * 256 + oc * 128: dc * 256 + oc * 128 + 128],
                        xnbq[:, dc * NQ + qt * 512: dc * NQ + qt * 512 + 512],
                        start=(dc == 0), stop=False), waits=wy)
                last_ymm = op("tensor", lambda e, oc=oc, qt=qt, ybk=ybk: e.matmul(
                    pb[ybk][:, :], WBP[:, NCH * 256 + oc * 128: NCH * 256 + oc * 128 + 128],
                    zp16[:, qt * 512:(qt + 1) * 512], start=False, stop=True),
                    inc="pe")
                PE_Y4 = last_ymm
                SEV, XRC = SC_SE[(oc, qt)]
                last_yev[bk] = op("scalar", lambda e, ocg=ocg, ybk=ybk, qt=qt:
                                  e.activation(
                    SGT2[qt][:, :], pb[ybk][:, :], AF.Silu,
                    bias=bb_P[:, ocg:ocg + 1]),
                    waits=[(m_pe, PE_Y4), (s_ld, LD0), (v_ve, last_sg[qt])], inc="sc")
                SC_EE = last_yev[bk]
                last_sg[qt] = op("vector", lambda e, qt=qt, bk=bk: e.tensor_tensor(
                    OUTS[qt][:, :], SGT[bk][:, :], SGT2[qt][:, :], ALU.mult),
                    waits=[(a_sc, SC_EE), (s_so[qt], DC[id(s_so[qt])])])
                VE_P4[(og, oc, qt)] = op("vector", lambda e, qt=qt, bk=bk:
                                         e.tensor_tensor(
                    OUTS[qt][:, :], OUTS[qt][:, :], XRS[bk][:, :], ALU.add),
                    waits=[(s_xr[bk], XRC)])
                last_xrs[bk] = VE_P4[(og, oc, qt)]
                dma("sync", AP(d_out, ocg * 128 * NQ + qt * 512, [[NQ, 128], [1, 512]]),
                    OUTS[qt][:, :], s_so[qt], waits=[(v_ve, VE_P4[(og, oc, qt)])])
        SC_P4og[og] = C["sc"]
    ST0, ST1, STD = DC[id(s_so[0])], DC[id(s_so[1])], DC[id(s_st)]
    Q["sync"].append(lambda e: e.wait_ge(s_so[0], ST0))
    Q["sync"].append(lambda e: e.wait_ge(s_so[1], ST1))
    if STD:
        Q["sync"].append(lambda e: e.wait_ge(s_st, STD))

    with nc.Block() as block:
        @block.sync
        def _(eng):
            for fn in Q["sync"]:
                fn(eng)

        @block.gpsimd
        def _(eng):
            for fn in Q["gpsimd"]:
                fn(eng)

        @block.tensor
        def _(eng):
            for fn in Q["tensor"]:
                fn(eng)

        @block.vector
        def _(eng):
            for fn in Q["vector"]:
                fn(eng)

        @block.scalar
        def _(eng):
            for fn in Q["scalar"]:
                fn(eng)

    return nc


_CACHE = {}


def _get_nc():
    if "nc" not in _CACHE:
        _CACHE["nc"] = build_program()
    return _CACHE["nc"]


def kernel(L0_attn_x, L0_attn_wq_w, L0_attn_wq_b, L0_attn_wk_w, L0_attn_wk_b,
           L0_attn_wv_w, L0_attn_wv_b, L0_attn_wo_w, L0_attn_wo_b,
           L0_ffn_W_base, L0_ffn_b_base, L0_ffn_lora_A, L0_ffn_lora_B,
           L0_ffn_router_W, L0_ffn_router_b, L0_ffn_domain_prior, L0_ffn_active):
    x = np.asarray(L0_attn_x, dtype=np.float32)

    def f32(a):
        return np.ascontiguousarray(np.asarray(a, dtype=np.float32))

    wq, wk, wv = f32(L0_attn_wq_w), f32(L0_attn_wk_w), f32(L0_attn_wv_w)
    wo, Wb = f32(L0_attn_wo_w), f32(L0_ffn_W_base)
    xT = [np.ascontiguousarray(x[b].T) for b in range(2)]
    wqT = np.ascontiguousarray(wq.T)
    wkT = np.ascontiguousarray(wk.T)
    wvT = np.ascontiguousarray(wv.T)
    woT = np.ascontiguousarray(wo.T)
    WbT = np.ascontiguousarray(Wb.T)
    wq_bP = np.ascontiguousarray(f32(L0_attn_wq_b).reshape(NCH, 128).T)
    wo_bP = np.ascontiguousarray(f32(L0_attn_wo_b).reshape(NCH, 128).T)
    bb_P = np.ascontiguousarray(f32(L0_ffn_b_base).reshape(NCH, 128).T)
    wk_bP = f32(L0_attn_wk_b).reshape(128, 1)
    wv_bP = f32(L0_attn_wv_b).reshape(128, 1)
    AT = np.ascontiguousarray(f32(L0_ffn_lora_A).reshape(128, D).T)
    Bt = np.ascontiguousarray(f32(L0_ffn_lora_B).transpose(0, 2, 1).reshape(128, D))
    rT = np.ascontiguousarray(f32(L0_ffn_router_W).T)
    rb_P = f32(L0_ffn_router_b).reshape(16, 1)
    prior_P = f32(L0_ffn_domain_prior).reshape(16, 1)
    act_i = np.ascontiguousarray(np.asarray(L0_ffn_active, dtype=np.int32))
    iota16 = np.arange(16, dtype=np.float32).reshape(16, 1)
    iotaR4 = np.ascontiguousarray(np.tile(np.arange(16, dtype=np.float32), (4, 1)))
    identB = np.eye(128, dtype=np.float32).astype(bf16)
    identF4 = np.eye(4, dtype=np.float32)
    identF16 = np.eye(16, dtype=np.float32)
    R2 = np.zeros((16, 128), dtype=np.float32)
    for e in range(16):
        R2[e, e * 8:(e + 1) * 8] = 2.0
    R2 = R2.astype(bf16)

    tri = (np.arange(128)[:, None] <= np.arange(128)[None, :]).astype(np.float32)
    in_maps = []
    qrows_all = []
    for c in range(8):
        b, cm = c // 4, c % 4
        qrows = np.concatenate(
            [np.arange(128) + (4 * g + cm) * 128 for g in range(NSLOT)])
        qrows_all.append(qrows)
        xqT = np.ascontiguousarray(x[b][qrows].T)
        masks = np.zeros((NSLOT * 4, 128, 512), dtype=np.float32)
        for g in range(NSLOT):
            for j in range(4):
                if j < cm:
                    masks[4 * g + j] = 1.0
                elif j == cm:
                    masks[4 * g + j] = np.tile(tri, (1, 4))
        masks = masks.astype(bf16)
        in_maps.append({
            "xT": xT[b], "xqT": xqT, "masks": masks, "identF16": identF16,
            "wqT": wqT, "wkT": wkT, "wvT": wvT, "woT": woT, "WbT": WbT,
            "wq_bP": wq_bP, "wo_bP": wo_bP, "bb_P": bb_P,
            "wk_bP": wk_bP, "wv_bP": wv_bP,
            "AT": AT, "Bt": Bt, "rT": rT, "rb_P": rb_P, "prior_P": prior_P,
            "act_i": act_i, "iota16": iota16, "iota16b": iota16.astype(bf16), "iotaR4": iotaR4,
            "identB": identB, "identF4": identF4, "R2": R2,
        })

    nc = _get_nc()
    res = run_bass_kernel_spmd(nc, in_maps, core_ids=list(range(8)))
    out = np.empty((2, T, D), dtype=np.float32)
    for c in range(8):
        b = c // 4
        outT = np.asarray(res.results[c]["outT"])
        out[b, qrows_all[c], :] = outT.T
    return out
